# version 18
# baseline (speedup 1.0000x reference)
import numpy as np
import ml_dtypes
import concourse.bass as bass
import concourse.mybir as mybir
from concourse.bass_utils import run_bass_kernel_spmd
from contextlib import ExitStack

F32 = mybir.dt.float32
BF16 = mybir.dt.bfloat16
I16 = mybir.dt.int16
FP8 = mybir.dt.float8e4
AF = mybir.ActivationFunctionType
ALU = mybir.AluOpType
AX = mybir.AxisListType

C = 1024
NH = 16
BD = 16
HS = 64
WIN = 64
INNER = 4096
PAD = 256
EPS = 1e-5


class Planner:
    def __init__(self):
        self.steps = []
        self.counts = {}
        self.engs = ("sp", "pe", "dv", "ac", "pl")
        self.chain = ("dv", "ac")
        self.last_chain = {}

    def reg(self, s):
        self.counts[id(s)] = 0
        return s

    def n(self, s):
        return self.counts[id(s)]

    def wait(self, eng, s, val=None):
        v = self.counts[id(s)] if val is None else val
        if v > 0:
            self.steps.append((eng, "w", (s, v)))

    def op(self, eng, fn, s=None, inc=1):
        if eng in self.chain and s is not None:
            cur = self.counts[id(s)]
            if cur > 0 and self.last_chain.get(eng, -1) < cur:
                self.steps.append((eng, "w", (s, cur)))
                self.last_chain[eng] = cur
        self.steps.append((eng, "o", (fn, s, inc)))
        if s is not None:
            self.counts[id(s)] += inc
            return self.counts[id(s)]
        return None

    def dma(self, eng, fn, s):
        return self.op(eng, fn, s, 16)

    def replay(self, key, eng):
        for k, kind, pl in self.steps:
            if k != key:
                continue
            if kind == "w":
                eng.wait_ge(pl[0], pl[1])
            else:
                fn, s, inc = pl
                r = fn(eng)
                if s is not None:
                    r.then_inc(s, inc)


def build(T=4096, dbg=False, stop_after=None, p1_lvl=8, p2_lvl=8, p3_lvl=4):
    NT = T // 128
    P2S = min(2048, T)
    NSUP2 = T // P2S
    NSUB = P2S // 128
    NQC = NSUB // 4
    P3S = min(1024, T)
    NSUP3 = T // P3S
    NTT = P3S // 128
    DF1 = T + 128

    nc = bass.Bass("TRN2", target_bir_lowering=False)

    x_in = nc.dram_tensor("x", [T, C], F32, kind="ExternalInput").ap()
    wdisp = nc.dram_tensor("wdisp", [NH * BD, C], F32, kind="ExternalInput").ap()
    wval = nc.dram_tensor("wval", [C, C], F32, kind="ExternalInput").ap()
    relpos = nc.dram_tensor("relpos", [WIN, BD], F32, kind="ExternalInput").ap()
    pposw = nc.dram_tensor("pposw", [BD, BD], F32, kind="ExternalInput").ap()
    cprojw = nc.dram_tensor("cprojw", [C, C], F32, kind="ExternalInput").ap()
    wgate = nc.dram_tensor("wgate", [INNER, C], F32, kind="ExternalInput").ap()
    wup = nc.dram_tensor("wup", [INNER, C], F32, kind="ExternalInput").ap()
    wdown = nc.dram_tensor("wdown", [C, INNER], F32, kind="ExternalInput").ap()
    identb_in = nc.dram_tensor("identb", [128, 128], BF16, kind="ExternalInput").ap()
    maskneg_in = nc.dram_tensor("maskneg", [128, WIN], BF16, kind="ExternalInput").ap()
    logitw_in = nc.dram_tensor("logitw32", [128, 8], F32, kind="ExternalInput").ap()
    dfw_in = nc.dram_tensor("dfw32", [128, 256], F32, kind="ExternalInput").ap()
    csfb_in = nc.dram_tensor("csfb", [128, WIN], F32, kind="ExternalInput").ap()
    dbrep_in = nc.dram_tensor("dbrep", [128, 1], F32, kind="ExternalInput").ap()

    wp = nc.dram_tensor("wp", [NH * T * PAD], BF16, kind="Internal")
    x2s = nc.dram_tensor("x2s", [T * C], BF16, kind="Internal")
    vals = nc.dram_tensor("vals", [T * C], BF16, kind="Internal")
    wg8 = nc.dram_tensor("wg8", [8 * 128 * INNER], FP8, kind="Internal")
    wu8 = nc.dram_tensor("wu8", [8 * 128 * INNER], FP8, kind="Internal")
    wd8 = nc.dram_tensor("wd8", [32 * 128 * C], FP8, kind="Internal")

    out_t = nc.dram_tensor("out", [T, C], F32, kind="ExternalOutput")
    out = out_t.ap()
    dbg_t = {}
    if dbg:
        for nm, shp in [("d_df", [128, 4 * DF1]), ("d_val", [T, C]),
                        ("d_ltm", [128, 16 * 512]), ("d_x2", [T, C]),
                        ("d_g", [128, min(8, NTT) * INNER]), ("d_ptm", [128, 16 * 256])]:
            dbg_t[nm] = nc.dram_tensor(nm, shp, F32, kind="ExternalOutput").ap()

    P = Planner()

    with ExitStack() as es:
        def sb(name, shape, dtype=F32):
            return es.enter_context(nc.sbuf_tensor(name, shape, dtype))

        def psum(name, shape, dtype=F32):
            return es.enter_context(nc.psum_tensor(name, shape, dtype))

        def sem(name):
            return P.reg(es.enter_context(nc.semaphore(name)))

        df = sb("df", [128, max(4 * DF1, 8 * P3S)], BF16)
        wvT = sb("wvT", [128, 8 * C], BF16)
        cpT = sb("cpT", [128, 8 * C], BF16)
        wdT = sb("wdT", [128, 2048], BF16)
        big = sb("big", [128, 32768], BF16)
        f32a = sb("f32a", [128, 3088], F32)
        i16a = sb("i16a", [128, 3072], I16)
        fp8a = sb("fp8a", [128, 8 * INNER + 16384], FP8)
        junk = sb("junk", [128, 1024], BF16)
        identb = sb("identb_t", [128, 128], BF16)
        dfwb = sb("dfwb", [128, 256], BF16)
        logit_w = sb("logit_w", [128, 8], BF16)
        c_all = sb("c_all", [128, WIN])
        maskneg = sb("maskneg_t", [128, WIN], BF16)
        dbcol = sb("dbcol", [128, 1])
        stats = sb("stats", [128, 128])
        scal = sb("scal", [128, 16])
        onesb = sb("onesb", [1, 128], BF16)
        ones32 = sb("ones32", [1, 128], F32)
        onescol = sb("onescol", [128, 1], F32)
        epscol = sb("epscol", [128, 1], F32)
        absac = sb("absac", [128, 32])
        ternb = sb("ternb", [128, 2048], BF16)

        w32 = f32a

        psA = [psum(f"psA{i}", [128, 512], F32) for i in range(4)]
        psB = [psum(f"psB{i}", [128, 1024], BF16) for i in range(2)]
        psC = [psum(f"psC{i}", [128, 512], F32) for i in range(2)]

        ev = sem("ev")
        ea = sem("ea")
        ep = sem("ep")
        eq = sem("eq")
        ALLS = [ev, ea, ep]
        _streams = {}

        def dsem(name):
            if name not in _streams:
                s = sem("q_" + name)
                _streams[name] = s
                ALLS.append(s)
            return _streams[name]

        def fence():
            for e in P.engs:
                for s in ALLS:
                    P.wait(e, s)

        def ap_rows(a, r0, r1):
            return bass.AP(tensor=a.tensor, offset=a.offset + a.ap[0][0] * r0,
                           ap=[[a.ap[0][0], r1 - r0]] + [list(d) for d in a.ap[1:]])

        def splits(n):
            return [(128 * i // n, 128 * (i + 1) // n) for i in range(n)]

        B_h1buf, B_h1fm, B_disp, B_vstg = 0, 1024, 2048, 3072
        B_args, B_gargs, B_ptm = 0, 8192, 14336
        B_ebuf, B_btb, B_btb2 = 14848, 15360, 19456
        B_bb, B_attn, B_x2b, B_vring = 23552, 25600, 27648, 29696
        B_blob = B_x2b
        F_xb, F_sc = 0, 2048

        P.dma("sp", lambda e: e.dma_start(out=identb[:], in_=identb_in), dsem("c"))
        P.dma("sp", lambda e: e.dma_start(out=maskneg[:], in_=maskneg_in), dsem("c"))
        P.dma("sp", lambda e: e.dma_start(out=c_all[:], in_=csfb_in), dsem("c"))
        P.dma("sp", lambda e: e.dma_start(out=stats[:, 0:1], in_=dbrep_in), dsem("c"))
        P.dma("sp", lambda e: e.dma_start(out=f32a[:, 0:8], in_=logitw_in), dsem("c"))
        P.dma("sp", lambda e: e.dma_start(out=f32a[:, 8:264], in_=dfw_in), dsem("c"))
        P.dma("sp", lambda e: e.dma_start(out=f32a[0:WIN, 264:280], in_=relpos), dsem("c"))
        P.dma("sp", lambda e: e.dma_start(out=f32a[0:BD, 280:296], in_=pposw), dsem("c"))
        t_cl = P.n(dsem("c"))

        P.wait("dv", dsem("c"), t_cl)
        P.op("dv", lambda e: e.tensor_copy(out=logit_w[:], in_=f32a[:, 0:8]), ev)
        P.op("dv", lambda e: e.tensor_copy(out=dfwb[:], in_=f32a[:, 8:264]), ev)
        P.op("dv", lambda e: e.tensor_copy(out=ternb[0:WIN, 0:BD], in_=f32a[0:WIN, 264:280]), ev)
        P.op("dv", lambda e: e.tensor_copy(out=ternb[0:BD, 16:32], in_=f32a[0:BD, 280:296]), ev)
        P.op("dv", lambda e: e.memset(onesb[:], 1.0), ev)
        P.op("dv", lambda e: e.memset(ones32[:], 1.0), ev)
        P.op("dv", lambda e: e.memset(onescol[:], 1.0), ev)
        P.op("dv", lambda e: e.memset(epscol[:], float(EPS)), ev)
        P.op("dv", lambda e: e.memset(df[:], 0.0), ev)
        P.op("dv", lambda e: e.tensor_scalar(out=dbcol[:], in0=stats[:, 0:1],
                                             scalar1=0.5, scalar2=None, op0=ALU.mult), ev)
        t_cv = P.n(ev)

        for _pi in range(4):
            P.op("dv", lambda e, _pi=_pi: e.memset(psA[_pi][:], 0.0), ev)
        P.op("dv", lambda e: e.memset(psC[0][:], 0.0), ev)
        P.op("dv", lambda e: e.memset(psC[1][:], 0.0), ev)
        P.wait("pe", ev, t_cv)
        P.op("pe", lambda e: e.transpose(out=psB[0][0:BD, 0:WIN], in_=ternb[0:WIN, 0:BD],
                                         identity=identb[0:WIN, 0:WIN]), ep)
        P.op("pe", lambda e: e.transpose(out=psB[0][0:BD, 64:80], in_=ternb[0:BD, 16:32],
                                         identity=identb[0:BD, 0:BD]), ep)
        t_ct = P.n(ep)
        P.wait("dv", ep, t_ct)
        P.op("dv", lambda e: e.tensor_copy(out=ternb[0:BD, 32:96], in_=psB[0][0:BD, 0:WIN]), ev)
        P.op("dv", lambda e: e.tensor_copy(out=ternb[0:BD, 96:112], in_=psB[0][0:BD, 64:80]), ev)
        t_ce = P.n(ev)
        P.wait("pe", ev, t_ce)
        P.op("pe", lambda e: e.matmul(out=psA[0][0:BD, 0:WIN], lhsT=ternb[0:BD, 96:112],
                                      rhs=ternb[0:BD, 32:96], start=True, stop=True), ep)
        t_cp = P.n(ep)
        P.wait("dv", ep, t_cp)
        for hl in range(4):
            P.op("dv", lambda e, hl=hl: e.tensor_tensor(
                out=c_all[32 * hl : 32 * hl + BD, :], in0=c_all[32 * hl : 32 * hl + BD, :],
                in1=psA[0][0:BD, 0:WIN], op=ALU.add), ev)
        P.op("dv", lambda e: e.memset(big[:, B_x2b : B_x2b + 2048], 0.0), ev)
        P.op("dv", lambda e: e.memset(ternb[:], 0.0), ev)
        fence()

        tok_w32 = [0, 0]
        tok_tern = [0, 0]
        tok_psB = [0, 0]

        def prep_attn_w(src_fn, n_tiles, dest_tensor, dest_stride):
            for r in range(n_tiles):
                sl = r % 2
                P.wait("sp", ev, tok_w32[sl])
                for r0, r1 in splits(4):
                    t_ld = P.dma("sp", lambda e, r=r, sl=sl, src_fn=src_fn, r0=r0, r1=r1: e.dma_start(
                        out=w32[r0:r1, 1024 * sl : 1024 * sl + 1024],
                        in_=ap_rows(src_fn(r), r0, r1)), dsem(f"w{sl}"))
                P.wait("dv", dsem(f"w{sl}"), t_ld)
                P.wait("dv", ep, tok_tern[sl])
                t_cast = P.op("dv", lambda e, sl=sl: e.tensor_copy(
                    out=ternb[:, 1024 * sl : 1024 * sl + 1024],
                    in_=w32[:, 1024 * sl : 1024 * sl + 1024]), ev)
                tok_w32[sl] = t_cast
                P.wait("pe", ev, t_cast)
                P.wait("pe", ev, tok_psB[sl])
                t_tr = None
                for fc in range(8):
                    t_tr = P.op("pe", lambda e, fc=fc, sl=sl: e.transpose(
                        out=psB[sl][:, 128 * fc : 128 * fc + 128],
                        in_=ternb[:, 1024 * sl + 128 * fc : 1024 * sl + 128 * fc + 128],
                        identity=identb[:]), ep)
                tok_tern[sl] = t_tr
                P.wait("dv", ep, t_tr)
                t_ev = P.op("dv", lambda e, r=r, sl=sl, d=dest_tensor, ds=dest_stride: e.tensor_copy(
                    out=bass.AP(tensor=d[:].tensor, offset=128 * r,
                                ap=[list(d[:].ap[0]), [ds, 8], [1, 128]]),
                    in_=psB[sl][:].rearrange("p (a b) -> p a b", a=8)), ev)
                tok_psB[sl] = t_ev

        if stop_after != "consts":
            prep_attn_w(lambda r: wval[128 * r : 128 * r + 128, :], 8, wvT, C)
            prep_attn_w(lambda r: cprojw[128 * r : 128 * r + 128, :], 8, cpT, C)
            prep_attn_w(lambda r: wdisp[128 * r : 128 * r + 128, :], 2, wdT, 256)
        fence()
        tok_w32 = [0, 0]
        tok_psB = [0, 0]

        nzc = (NH * T * PAD) // (128 * 2048)
        t_zwp = 0
        for z in range(nzc):
            t_zwp = P.dma("sp", lambda e, z=z: e.dma_start(
                out=bass.AP(tensor=wp, offset=z * 128 * 2048, ap=[[2048, 128], [1, 2048]]),
                in_=df[:, 0:2048]), dsem("z"))

        def w_tile_ap(mat, ti):
            if mat == 0:
                return wgate[128 * ti : 128 * ti + 128, :]
            if mat == 1:
                return wup[128 * ti : 128 * ti + 128, :]
            r, q = divmod(ti, 4)
            return wdown[128 * r : 128 * r + 128, 1024 * q : 1024 * q + 1024]

        def w8_dst(mat, ti):
            if mat < 2:
                t8 = wg8 if mat == 0 else wu8
                return bass.AP(tensor=t8, offset=128 * ti,
                               ap=[[INNER, 128], [128 * INNER, 8], [1, 128]])
            r, q = divmod(ti, 4)
            return bass.AP(tensor=wd8, offset=(8 * q) * (128 * C) + 128 * r,
                           ap=[[C, 128], [128 * C, 8], [1, 128]])

        wb_on = stop_after != "consts" and stop_after != "wa"
        n_wbc = 192 if wb_on else 0
        t_red = 0
        t_scal = 0

        def wbc_chunk(k):
            nonlocal t_red, t_scal
            mat, ti, half = wcc_idx(k)
            csl = k % 2
            t_my = tld_wcc[csl]
            P.wait("ac", dsem(f"w{csl}"), t_my)
            if 2 * ti + half < 4:
                P.wait("ac", ev, t_red)
            t_abs = P.op("ac", lambda e, ti=ti, half=half, csl=csl: e.activation(
                out=junk[:, 0:512],
                in_=f32a[:, F_sc + 512 * csl : F_sc + 512 * csl + 512],
                func=AF.Abs, accum_out=stats[:, 64 + 2 * ti + half : 65 + 2 * ti + half]), ea)
            tok_w32c[csl] = t_abs
            if k + 2 < n_wbc:
                wcc_load(k + 2)
            elif k + 2 < n_wbc + 2 and n_wcc:
                wcc_load(k + 2 - n_wbc)
            if ti == 31 and half == 1:
                P.wait("dv", ea, t_abs)
                t_red = P.op("dv", lambda e: e.tensor_reduce(
                    out=stats[:, 30:31], in_=stats[:, 64:128], axis=AX.X, op=ALU.add), ev)
                P.wait("pe", ev, t_red)
                t_mm = P.op("pe", lambda e, mat=mat: e.matmul(
                    out=psC[0][0:1, mat : mat + 1], lhsT=onescol[:],
                    rhs=stats[:, 30:31], start=True, stop=True), ep)
                P.wait("dv", ep, t_mm)
                P.op("dv", lambda e, mat=mat: e.tensor_scalar(
                    out=stats[0:1, 60 + mat : 61 + mat], in0=psC[0][0:1, mat : mat + 1],
                    scalar1=1.0 / (INNER * C), scalar2=1e-5, op0=ALU.mult, op1=ALU.max), ev)
                if mat == 2:
                    t_mus = P.n(ev)
                    P.wait("pe", ev, t_mus)
                    t_mm2 = P.op("pe", lambda e: e.matmul(out=psC[1][0:128, 0:3], lhsT=ones32[0:1, :],
                                                          rhs=stats[0:1, 60:63],
                                                          start=True, stop=True), ep)
                    P.wait("dv", ep, t_mm2)
                    P.op("dv", lambda e: e.tensor_copy(out=scal[:, 6:9], in_=psC[1][0:128, 0:3]), ev)
                    P.op("dv", lambda e: e.tensor_scalar(out=scal[:, 0:3], in0=psC[1][0:128, 0:3],
                                                         scalar1=0.5, scalar2=None, op0=ALU.mult), ev)
                    P.op("dv", lambda e: e.tensor_scalar(out=scal[:, 3:6], in0=psC[1][0:128, 0:3],
                                                         scalar1=-0.5, scalar2=None, op0=ALU.mult), ev)
                    t_scal = P.n(ev)

        wc_on = stop_after not in ("consts", "wa", "wb")
        n_wcc = 192 if wc_on else 0
        tok_w32c = [0, 0]
        tok_terc = [0, 0]
        tok_psB1 = [0, 0]
        tok_stgc = [0, 0]
        tld_wcc = [0, 0]

        def ap_mid(a, i0, n):
            return bass.AP(tensor=a.tensor, offset=a.offset + a.ap[1][0] * i0,
                           ap=[list(a.ap[0]), [a.ap[1][0], n]] + [list(d) for d in a.ap[2:]])

        def wcc_idx(k):
            mat = k // 64
            ti = (k % 64) // 2
            half = k % 2
            return mat, ti, half

        def wcc_load(k):
            mat, ti, half = wcc_idx(k)
            csl = k % 2
            P.wait("sp", ea, tok_w32c[csl])
            for r0, r1 in splits(2):
                tld_wcc[csl] = P.dma("sp", lambda e, mat=mat, ti=ti, half=half, csl=csl, r0=r0, r1=r1: e.dma_start(
                    out=f32a[r0:r1, F_sc + 512 * csl : F_sc + 512 * csl + 512],
                    in_=ap_cols(ap_rows(w_tile_ap(mat, ti), r0, r1), half)), dsem(f"w{csl}"))

        def ap_cols(a, half):
            return bass.AP(tensor=a.tensor, offset=a.offset + 512 * half,
                           ap=[list(a.ap[0]), [1, 512]])

        def wcc_chunk(k):
            mat, ti, half = wcc_idx(k)
            csl = k % 2
            t_my_ld = tld_wcc[csl]
            P.wait("ac", dsem(f"w{csl}"), t_my_ld)
            if k < 2 and n_wbc:
                P.wait("ac", ev, t_scal)
            P.wait("ac", ep, tok_terc[csl])
            P.op("ac", lambda e, csl=csl, mat=mat: e.activation(
                out=ternb[:, 512 * csl : 512 * csl + 512],
                in_=f32a[:, F_sc + 512 * csl : F_sc + 512 * csl + 512],
                func=AF.Sign, bias=scal[:, 3 + mat : 4 + mat]), ea)
            t_b = P.op("ac", lambda e, csl=csl, mat=mat: e.activation(
                out=big[:, 5120 + 512 * csl : 5120 + 512 * csl + 512],
                in_=f32a[:, F_sc + 512 * csl : F_sc + 512 * csl + 512],
                func=AF.Sign, bias=scal[:, mat : mat + 1]), ea)
            tok_w32c[csl] = t_b
            if k + 2 < n_wcc:
                wcc_load(k + 2)
            P.wait("dv", ea, t_b)
            t_t = P.op("dv", lambda e, csl=csl: e.tensor_tensor(
                out=ternb[:, 512 * csl : 512 * csl + 512],
                in0=ternb[:, 512 * csl : 512 * csl + 512],
                in1=big[:, 5120 + 512 * csl : 5120 + 512 * csl + 512], op=ALU.add), ev)
            P.wait("pe", ev, t_t)
            P.wait("pe", ev, tok_psB1[csl])
            t_tr = None
            for fc2 in range(4):
                t_tr = P.op("pe", lambda e, csl=csl, fc2=fc2: e.transpose(
                    out=psB[1][:, 512 * csl + 128 * fc2 : 512 * csl + 128 * fc2 + 128],
                    in_=ternb[:, 512 * csl + 128 * fc2 : 512 * csl + 128 * fc2 + 128],
                    identity=identb[:]), ep)
            tok_terc[csl] = t_tr
            P.wait("dv", ep, t_tr)
            P.wait("dv", dsem(f"st{csl}"), tok_stgc[csl])
            t_ev = P.op("dv", lambda e, csl=csl: e.tensor_copy(
                out=fp8a[:, 32768 + 512 * csl : 32768 + 512 * csl + 512],
                in_=psB[1][:, 512 * csl : 512 * csl + 512]), ev)
            tok_psB1[csl] = t_ev
            P.wait("sp", ev, t_ev)
            for r0, r1 in splits(2):
                tok_stgc[csl] = P.dma("sp", lambda e, mat=mat, ti=ti, half=half, csl=csl, r0=r0, r1=r1: e.dma_start(
                    out=ap_rows(ap_mid(w8_dst(mat, ti), 4 * half, 4), r0, r1),
                    in_=fp8a[r0:r1, 32768 + 512 * csl : 32768 + 512 * csl + 512]), dsem(f"st{csl}"))

        fence()
        tok_w32 = [0, 0]
        tok_psB = [0, 0]

        tok_xb = [0, 0]
        tok_h1b = 0
        tok_h1fm = 0
        tok_psA01 = 0
        tok_psA2 = 0
        tok_psA3 = [0, 0]
        tok_dispb = 0
        tok_vstg = [0, 0]
        n_p1 = NT if stop_after not in ("consts", "wa", "wb", "wc") else 0
        tld_x = [0, 0]

        def p1_load(g):
            sl = g % 2
            P.wait("sp", ev, tok_xb[sl])
            for r0, r1 in splits(4):
                tld_x[sl] = P.dma("sp", lambda e, g=g, sl=sl, r0=r0, r1=r1: e.dma_start(
                    out=f32a[r0:r1, F_xb + 1024 * sl : F_xb + 1024 * sl + 1024],
                    in_=x_in[128 * g + r0 : 128 * g + r1, :]), dsem(f"xb{sl}"))

        t_rs_g = [0, 0]

        def p1_stats(g):
            sl = g % 2
            sp2 = 2 + 8 * (g % 2)
            t_x = tld_x[sl]

            def xt_ap(sl=sl):
                return f32a[:, F_xb + 1024 * sl : F_xb + 1024 * sl + 1024]

            P.wait("dv", dsem(f"xb{sl}"), t_x)
            P.op("dv", lambda e, sl=sl, sp2=sp2: e.tensor_reduce(
                out=stats[:, sp2 : sp2 + 1], in_=xt_ap(sl), axis=AX.X, op=ALU.add), ev)
            P.wait("ac", dsem(f"xb{sl}"), t_x)
            t_sq = P.op("ac", lambda e, sl=sl, sp2=sp2: e.activation(
                out=junk[:], in_=xt_ap(sl), func=AF.Square, accum_out=stats[:, sp2 + 1 : sp2 + 2]), ea)
            P.op("dv", lambda e, sp2=sp2: e.tensor_scalar(out=stats[:, sp2 + 2 : sp2 + 3], in0=stats[:, sp2 : sp2 + 1],
                                                 scalar1=-1.0 / C, scalar2=None, op0=ALU.mult), ev)
            P.wait("dv", ea, t_sq)
            P.op("dv", lambda e, sp2=sp2: e.tensor_scalar(out=stats[:, sp2 + 3 : sp2 + 4], in0=stats[:, sp2 + 1 : sp2 + 2],
                                                 scalar1=1.0 / C, scalar2=None, op0=ALU.mult), ev)
            P.op("dv", lambda e, sp2=sp2: e.tensor_tensor(out=stats[:, sp2 + 4 : sp2 + 5], in0=stats[:, sp2 + 2 : sp2 + 3],
                                                 in1=stats[:, sp2 + 2 : sp2 + 3], op=ALU.mult), ev)
            t_var = P.op("dv", lambda e, sp2=sp2: e.tensor_tensor(out=stats[:, sp2 + 5 : sp2 + 6], in0=stats[:, sp2 + 3 : sp2 + 4],
                                                         in1=stats[:, sp2 + 4 : sp2 + 5], op=ALU.subtract), ev)
            P.wait("ac", ev, t_var)
            P.op("ac", lambda e, sp2=sp2: e.activation(out=stats[:, sp2 + 6 : sp2 + 7], in_=stats[:, sp2 + 5 : sp2 + 6],
                                              func=AF.Ln, bias=epscol[:, 0:1]), ea)
            t_rs_g[g % 2] = P.op("ac", lambda e, sp2=sp2: e.activation(
                out=stats[:, sp2 + 7 : sp2 + 8], in_=stats[:, sp2 + 6 : sp2 + 7],
                func=AF.Exp, scale=-0.5), ea)

        n_units = n_wbc + n_wcc

        def emit_unit(u):
            if u < n_wbc:
                wbc_chunk(u)
            else:
                wcc_chunk(u - n_wbc)

        if n_p1:
            p1_load(0)
        if n_wbc:
            wcc_load(0)
            wcc_load(1)
        elif n_wcc:
            wcc_load(0)
            wcc_load(1)
        if n_p1:
            p1_stats(0)
        for g in range(n_p1):
            sl = g % 2
            sp2 = 2 + 8 * (g % 2)
            if g + 1 < n_p1:
                p1_load(g + 1)
                p1_stats(g + 1)
            for _j in range(12):
                if 12 * g + _j < n_units:
                    emit_unit(12 * g + _j)

            def xt_ap(sl=sl):
                return f32a[:, F_xb + 1024 * sl : F_xb + 1024 * sl + 1024]

            t_rs = t_rs_g[g % 2]
            if p1_lvl < 2:
                continue
            P.wait("dv", ea, t_rs)
            P.wait("dv", ep, tok_h1b)
            t_h1 = P.op("dv", lambda e, sl=sl, sp2=sp2: e.tensor_scalar(
                out=big[:, B_h1buf : B_h1buf + 1024], in0=xt_ap(sl),
                scalar1=stats[:, sp2 + 2 : sp2 + 3], scalar2=stats[:, sp2 + 7 : sp2 + 8],
                op0=ALU.add, op1=ALU.mult), ev)
            tok_xb[sl] = t_h1
            P.wait("pe", ev, t_h1)
            P.wait("pe", ev, tok_psB[0])
            t_tr = None
            for fc in range(8):
                t_tr = P.op("pe", lambda e, fc=fc: e.transpose(
                    out=psB[0][:, 128 * fc : 128 * fc + 128],
                    in_=big[:, B_h1buf + 128 * fc : B_h1buf + 128 * fc + 128],
                    identity=identb[:]), ep)
            tok_h1b = t_tr
            P.wait("dv", ep, t_tr)
            P.wait("dv", ep, tok_h1fm)
            t_fm = P.op("dv", lambda e: e.tensor_copy(
                out=big[:, B_h1fm : B_h1fm + 1024], in_=psB[0][:]), ev)
            tok_psB[0] = t_fm
            if p1_lvl < 3:
                continue
            P.wait("pe", ev, t_fm)
            P.wait("pe", ev, tok_psA01)
            for nck in range(2):
                for fc in range(8):
                    P.op("pe", lambda e, fc=fc, nck=nck: e.matmul(
                        out=psA[nck][:, :],
                        lhsT=big[:, B_h1fm + 128 * fc : B_h1fm + 128 * fc + 128],
                        rhs=wvT[:, C * fc + 512 * nck : C * fc + 512 * nck + 512],
                        start=(fc == 0), stop=(fc == 7)), ep)
            t_vmm = P.n(ep)
            if p1_lvl < 4:
                continue
            P.wait("pe", ev, tok_psA2)
            for oc in range(2):
                for fc in range(8):
                    P.op("pe", lambda e, oc=oc, fc=fc: e.matmul(
                        out=psA[2][:, 128 * oc : 128 * oc + 128],
                        lhsT=wdT[:, 256 * fc + 128 * oc : 256 * fc + 128 * oc + 128],
                        rhs=big[:, B_h1fm + 128 * fc : B_h1fm + 128 * fc + 128],
                        start=(fc == 0), stop=(fc == 7)), ep)
            t_dmm = P.n(ep)
            tok_h1fm = t_dmm
            P.wait("dv", ep, t_vmm)
            P.wait("dv", dsem(f"vs{sl}"), tok_vstg[sl])
            P.op("dv", lambda e, sl=sl: e.tensor_copy(
                out=big[:, B_vstg + 1024 * sl : B_vstg + 1024 * sl + 512], in_=psA[0][:]), ev)
            t_vst = P.op("dv", lambda e, sl=sl: e.tensor_copy(
                out=big[:, B_vstg + 1024 * sl + 512 : B_vstg + 1024 * sl + 1024],
                in_=psA[1][:]), ev)
            tok_psA01 = t_vst
            P.wait("sp", ev, t_vst)
            t_vd = P.dma("sp", lambda e, g=g, sl=sl: e.dma_start(
                out=bass.AP(tensor=vals, offset=g * 128 * C, ap=[[C, 128], [1, C]]),
                in_=big[:, B_vstg + 1024 * sl : B_vstg + 1024 * sl + 1024]), dsem(f"vs{sl}"))
            if dbg:
                P.wait("pl", ev, t_vst)
                t_dbg = P.dma("pl", lambda e, g=g, sl=sl: e.dma_start(
                    out=dbg_t["d_val"][128 * g : 128 * g + 128, :],
                    in_=big[:, B_vstg + 1024 * sl : B_vstg + 1024 * sl + 1024]), dsem("dbg"))
                P.wait("dv", dsem("dbg"), t_dbg)
            tok_vstg[sl] = t_vd
            if p1_lvl < 5:
                continue
            P.wait("dv", ep, t_dmm)
            P.wait("dv", ep, tok_dispb)
            t_de = P.op("dv", lambda e: e.tensor_copy(
                out=big[:, B_disp : B_disp + 256], in_=psA[2][:, 0:256]), ev)
            tok_psA2 = t_de
            t_mmdf_last = 0
            P.wait("pe", ev, t_de)
            P.wait("pe", ev, tok_psA3[0])
            for hg in range(4 if p1_lvl >= 6 else 0):
                oc, hh = hg // 2, hg % 2
                t_mmdf_last = P.op("pe", lambda e, oc=oc, hh=hh, hg=hg: e.matmul(
                    out=psA[3][:, 128 * hg : 128 * hg + 128],
                    lhsT=dfwb[:, 128 * hh : 128 * hh + 128],
                    rhs=big[:, B_disp + 128 * oc : B_disp + 128 * oc + 128],
                    start=True, stop=True), ep)
            if p1_lvl >= 7:
                P.wait("dv", ep, t_mmdf_last)
                if g == 0:
                    P.wait("dv", dsem("z"), t_zwp)
                t_dfe = P.op("dv", lambda e, g=g: e.tensor_copy(
                    out=bass.AP(tensor=df[:].tensor, offset=64 + 128 * g,
                                ap=[list(df[:].ap[0]), [DF1, 4], [1, 128]]),
                    in_=psA[3][:]), ev)
                tok_psA3[0] = t_dfe
            if p1_lvl >= 6:
                tok_dispb = t_mmdf_last
        for _k in range(12 * n_p1, n_units):
            emit_unit(_k)
        if dbg:
            P.wait("pl", ev)
            P.dma("pl", lambda e: e.dma_start(out=dbg_t["d_df"], in_=df[:, 0 : 4 * DF1]), dsem("dbg"))
        fence()

        tok_args = [0, 0, 0, 0]
        tok_gargs = [0, 0, 0]
        tok_psA_q = [0, 0, 0, 0]
        tok_psA_j0 = [0, 0, 0, 0]
        tok_blob = [0, 0]
        tok_ltm_w = 0
        tok_ltm_r = 0
        tok_ebuf = [0, 0]
        tok_ptm = [0, 0]
        tok_btb = [0, 0]
        tok_bb = [0, 0]
        tok_attn = [0, 0]
        tok_psC_ev = 0
        tok_psC_ac = 0
        tok_xres = [0, 0]
        tok_x2b = [0, 0]
        tok_vring = [0, 0, 0]

        if stop_after == "p2w":
            NSUBX = 0
        _stopped2 = stop_after in ("consts", "wa", "wb", "wc", "p1")
        for S in range(NSUP2 if not _stopped2 else 0):
            T0 = S * P2S
            for hg in range(4):
                dfo = DF1 * hg
                t_diff = [0, 0, 0, 0]
                t_eb_prev = 0

                def blob_ap(q2, r0, r1, c0, c1):
                    if q2 % 2 == 0:
                        return big[r0:r1, B_blob + c0 : B_blob + c1]
                    return ternb[r0:r1, c0:c1]

                def wl_tail(q2, t_eb2):
                    nonlocal tok_ltm_r
                    P.wait("pe", ev, t_eb2)
                    P.wait("pe", ev, tok_psB[0])
                    P.wait("pe", ev, tok_psB[1])
                    t_tr = None
                    for tc in range(NQC):
                        for i in range(4):
                            t_tr = P.op("pe", lambda e, tc=tc, i=i, q2=q2: e.transpose(
                                out=psB[tc % 2][:, 512 * (tc // 2) + 128 * i : 512 * (tc // 2) + 128 * i + 128],
                                in_=blob_ap(q2, 0, 128, 512 * tc + 128 * i, 512 * tc + 128 * i + 128),
                                identity=identb[:]), ep)
                    tok_blob[q2 % 2] = t_tr
                    P.wait("dv", ep, t_tr)
                    if q2 == 0:
                        P.wait("dv", ea, tok_ltm_r)
                    t_cp2 = None
                    for tc in range(NQC):
                        t_cp2 = P.op("dv", lambda e, tc=tc, q2=q2: e.tensor_copy(
                            out=bass.AP(tensor=wvT[:].tensor,
                                        offset=512 * (tc * 4) + 32 * q2,
                                        ap=[list(wvT[:].ap[0]), [512, 4], [8, 4], [1, 8]]),
                            in_=bass.AP(tensor=psB[tc % 2][:].tensor,
                                        offset=512 * (tc // 2),
                                        ap=[list(psB[tc % 2][:].ap[0]), [128, 4], [32, 4], [1, 8]])), ev)
                    tok_psB[0] = t_cp2
                    tok_psB[1] = t_cp2

                def w_diff2(q2, h2, dfo=dfo, T0=T0):
                    for j2 in (0, 1) if h2 == 0 else (2, 3):
                        w2 = 4 * q2 + j2
                        if j2 == 0:
                            P.wait("pl", ea, tok_args[0])
                            P.wait("pl", eq, t_diff[0])
                            t_diff[0] = P.op("pl", lambda e, w2=w2, dfo=dfo, T0=T0: e.tensor_tensor(
                                out=big[:, B_args : B_args + P2S],
                                in0=df[:, dfo + 1 + T0 + w2 : dfo + 1 + T0 + w2 + P2S],
                                in1=df[:, dfo + 64 + T0 : dfo + 64 + T0 + P2S],
                                op=ALU.subtract), eq)
                            continue
                        P.wait("dv", ea, tok_args[j2])
                        t_diff[j2] = P.op("dv", lambda e, w2=w2, j2=j2, dfo=dfo, T0=T0: e.tensor_tensor(
                            out=big[:, B_args + 2048 * j2 : B_args + 2048 * j2 + P2S],
                            in0=df[:, dfo + 1 + T0 + w2 : dfo + 1 + T0 + w2 + P2S],
                            in1=df[:, dfo + 64 + T0 : dfo + 64 + T0 + P2S],
                            op=ALU.subtract), ev)

                def w_diff(q2, dfo=dfo, T0=T0):
                    for j2 in range(4):
                        w2 = 4 * q2 + j2
                        P.wait("dv", ea, tok_args[j2])
                        t_diff[j2] = P.op("dv", lambda e, w2=w2, j2=j2, dfo=dfo, T0=T0: e.tensor_tensor(
                            out=big[:, B_args + 2048 * j2 : B_args + 2048 * j2 + P2S],
                            in0=df[:, dfo + 1 + T0 + w2 : dfo + 1 + T0 + w2 + P2S],
                            in1=df[:, dfo + 64 + T0 : dfo + 64 + T0 + P2S],
                            op=ALU.subtract), ev)

                if p2_lvl >= 0:
                    w_diff2(0, 0)
                    w_diff2(0, 1)
                for q in range(16 if p2_lvl >= 1 else 0):
                    t_lm = None
                    for j in range(4):
                        w = 4 * q + j
                        gsl = w % 3
                        P.wait("ac", eq if j == 0 else ev, t_diff[j])
                        P.wait("ac", ep, tok_gargs[gsl])
                        t_gel = P.op("ac", lambda e, w=w, j=j, gsl=gsl: e.activation(
                            out=big[:, B_gargs + 2048 * gsl : B_gargs + 2048 * gsl + P2S],
                            in_=big[:, B_args + 2048 * j : B_args + 2048 * j + P2S],
                            func=AF.Gelu, bias=c_all[:, w : w + 1]), ea)
                        tok_args[j] = t_gel
                        if p2_lvl < 2:
                            tok_gargs[gsl] = 0
                            continue
                        P.wait("pe", ea, t_gel)
                        for tc in range(NQC):
                            if j == 0:
                                P.wait("pe", ev, tok_psA_j0[tc])
                            elif j == 1:
                                P.wait("pe", ev, tok_psA_q[tc])
                            t_lm = P.op("pe", lambda e, gsl=gsl, tc=tc, j=j: e.matmul(
                                out=psA[tc][32 * j : 32 * j + 8, :],
                                lhsT=logit_w[:, :],
                                rhs=big[:, B_gargs + 2048 * gsl + 512 * tc : B_gargs + 2048 * gsl + 512 * tc + 512],
                                start=True, stop=True, tile_position=(0, 32 * j)), ep)
                        tok_gargs[gsl] = t_lm
                    if q + 1 < 16:
                        w_diff2(q + 1, 0)
                    if p2_lvl >= 3:
                        P.wait("dv", ep, t_lm)
                        P.wait("dv", ep, tok_blob[q % 2])
                        if hg == 0 and q == 0:
                            P.wait("dv", dsem("x20"), tok_x2b[0])
                            P.wait("dv", dsem("x21"), tok_x2b[1])
                        t_eb = None
                        for tc in range(NQC):
                            t_eb = P.op("dv", lambda e, tc=tc, q=q: e.tensor_copy(
                                out=blob_ap(q, 0, 128, 512 * tc, 512 * tc + 512),
                                in_=psA[tc][:]), ev)
                        for tc in range(NQC):
                            tok_psA_j0[tc] = t_eb
                            tok_psA_q[tc] = t_eb
                        if q + 1 < 16:
                            w_diff2(q + 1, 1)
                        if q > 0:
                            wl_tail(q - 1, t_eb_prev)
                        t_eb_prev = t_eb
                    elif q + 1 < 16:
                        w_diff2(q + 1, 1)
                tok_ltm_w_pending = True
                if p2_lvl >= 3:
                    wl_tail(15, t_eb_prev)
                tok_ltm_w = P.n(ev)
                if dbg and S == 0 and hg == 0:
                    P.wait("pl", ev, tok_ltm_w)
                    t_dl = P.dma("pl", lambda e: e.dma_start(out=dbg_t["d_ltm"], in_=wvT[:, 0 : 16 * 512]), dsem("dbg"))
                    P.wait("dv", dsem("dbg"), t_dl)
                for s_sub in range(NSUB if stop_after != "p2w" else 0):
                    sl = s_sub % 2
                    lt_off = 512 * s_sub
                    P.wait("ac", ev, tok_ltm_w)
                    t_sg = P.op("ac", lambda e, lt_off=lt_off: e.activation(
                        out=f32a[:, F_sc + 4 : F_sc + 260],
                        in_=bass.AP(tensor=wvT[:].tensor, offset=lt_off + 4,
                                    ap=[list(wvT[:].ap[0]), [8, WIN], [1, 4]]),
                        func=AF.Tanh, scale=0.5, bias=dbcol[:]), ea)
                    tok_ltm_r = t_sg
                    P.wait("dv", ea, t_sg)
                    t_lg = P.op("dv", lambda e, lt_off=lt_off: e.scalar_tensor_tensor(
                        out=f32a[:, F_sc + 260 : F_sc + 516],
                        in0=f32a[:, F_sc + 4 : F_sc + 260],
                        scalar=-5.0, op0=ALU.mult,
                        in1=bass.AP(tensor=wvT[:].tensor, offset=lt_off,
                                    ap=[list(wvT[:].ap[0]), [8, WIN], [1, 4]]),
                        op1=ALU.add), ev)
                    if S == 0 and s_sub == 0:
                        t_lg = P.op("dv", lambda e: e.tensor_tensor(
                            out=f32a[:, F_sc + 260 : F_sc + 516],
                            in0=f32a[:, F_sc + 260 : F_sc + 516],
                            in1=bass.AP(tensor=maskneg[:].tensor, offset=0,
                                        ap=[list(maskneg[:].ap[0]), [1, WIN], [0, 4]]),
                            op=ALU.add), ev)
                    P.op("dv", lambda e: e.tensor_reduce(
                        out=stats[:, 16:20],
                        in_=bass.AP(tensor=f32a[:].tensor, offset=F_sc + 260,
                                    ap=[list(f32a[:].ap[0]), [1, 4], [4, WIN]]),
                        axis=AX.X, op=ALU.max, negate=True), ev)
                    t_ei = P.op("dv", lambda e: e.tensor_tensor(
                        out=f32a[:, F_sc + 260 : F_sc + 516],
                        in0=f32a[:, F_sc + 260 : F_sc + 516],
                        in1=bass.AP(tensor=stats[:].tensor, offset=16,
                                    ap=[list(stats[:].ap[0]), [0, WIN], [1, 4]]),
                        op=ALU.add), ev)
                    P.wait("ac", ev, t_ei)
                    P.wait("ac", ev, tok_ebuf[sl])
                    t_ex = P.op("ac", lambda e, sl=sl: e.activation(
                        out=big[:, B_ebuf + 256 * sl : B_ebuf + 256 * sl + 256],
                        in_=f32a[:, F_sc + 260 : F_sc + 516], func=AF.Exp), ea)
                    P.wait("dv", ea, t_ex)
                    P.op("dv", lambda e, sl=sl: e.tensor_reduce(
                        out=stats[:, 20:24],
                        in_=bass.AP(tensor=big[:].tensor, offset=B_ebuf + 256 * sl,
                                    ap=[list(big[:].ap[0]), [1, 4], [4, WIN]]),
                        axis=AX.X, op=ALU.add), ev)
                    P.op("dv", lambda e: e.reciprocal(out=stats[:, 24:28], in_=stats[:, 20:24]), ev)
                    P.wait("dv", dsem(f"sk{sl}"), tok_ptm[sl])
                    t_p = P.op("dv", lambda e, sl=sl: e.tensor_tensor(
                        out=bass.AP(tensor=big[:].tensor, offset=B_ptm + 256 * sl,
                                    ap=[list(big[:].ap[0]), [1, WIN], [WIN, 4]]),
                        in0=big[:, B_ebuf + 256 * sl : B_ebuf + 256 * sl + 256],
                        in1=bass.AP(tensor=stats[:].tensor, offset=24,
                                    ap=[list(stats[:].ap[0]), [0, WIN], [1, 4]]),
                        op=ALU.mult), ev)
                    tok_ebuf[sl] = t_p
                    if dbg and S == 0 and hg == 0:
                        P.wait("pl", ev, t_p)
                        t_dbg = P.dma("pl", lambda e, s_sub=s_sub, sl=sl: e.dma_start(
                            out=dbg_t["d_ptm"][:, 256 * s_sub : 256 * s_sub + 256],
                            in_=big[:, B_ptm + 256 * sl : B_ptm + 256 * sl + 256]), dsem("dbg"))
                        P.wait("dv", dsem("dbg"), t_dbg)
                    P.wait("sp", ev, t_p)
                    if S == 0 and hg == 0 and s_sub == sl:
                        P.wait("sp", dsem("z"), t_zwp)
                    for r0, r1 in splits(4):
                        tok_ptm[sl] = P.dma("sp", lambda e, hg=hg, s_sub=s_sub, sl=sl, T0=T0, r0=r0, r1=r1: e.dma_start(
                            out=bass.AP(tensor=wp,
                                        offset=(T0 + 128 * s_sub + r0) * NH * PAD + (hg * 4) * PAD + 65 + r0,
                                        ap=[[NH * PAD + 1, r1 - r0], [PAD, 4], [1, WIN]]),
                            in_=big[r0:r1, B_ptm + 256 * sl : B_ptm + 256 * sl + 256].rearrange(
                                "p (a b) -> p a b", a=4)), dsem(f"sk{sl}"))
            t_skew_all = (P.n(dsem("sk0")), P.n(dsem("sk1")))
            nsub_b = NSUB if stop_after not in ("p2a", "p2w") else 0
            tbt = [0, 0]
            tvl = [0, 0, 0]
            txr = [0, 0]

            def bl_load(bs2, T0=T0):
                gb = (T0 // 128) + bs2
                t0 = 128 * gb
                bb_ = gb % 2
                btbase = B_btb if bb_ == 0 else B_btb2
                P.wait("sp", dsem("sk0"), t_skew_all[0])
                P.wait("sp", dsem("sk1"), t_skew_all[1])
                P.wait("sp", ep, tok_btb[bb_])
                for r0, r1 in splits(2):
                    tbt[bb_] = P.dma("sp", lambda e, t0=t0, r0=r0, r1=r1, btbase=btbase: e.dma_start(
                        out=big[r0:r1, btbase : btbase + 4096],
                        in_=bass.AP(tensor=wp, offset=(t0 + r0) * NH * PAD,
                                    ap=[[NH * PAD, r1 - r0], [1, NH * PAD]])), dsem(f"bt{bb_}"))
                vsl = gb % 3
                P.wait("sp", ep, tok_vring[vsl])
                for r0, r1 in splits(2):
                    tvl[vsl] = P.dma("sp", lambda e, gb=gb, vsl=vsl, r0=r0, r1=r1: e.dma_start(
                        out=big[r0:r1, B_vring + 1024 * vsl : B_vring + 1024 * vsl + 1024],
                        in_=bass.AP(tensor=vals, offset=(gb * 128 + r0) * C,
                                    ap=[[C, r1 - r0], [1, C]])), dsem(f"vr{vsl}"))
                xsl = gb % 2
                P.wait("sp", ev, tok_xres[xsl])
                for r0, r1 in splits(4):
                    txr[xsl] = P.dma("sp", lambda e, t0=t0, xsl=xsl, r0=r0, r1=r1: e.dma_start(
                        out=f32a[r0:r1, F_xb + 1024 * xsl : F_xb + 1024 * xsl + 1024],
                        in_=x_in[t0 + r0 : t0 + r1, :]), dsem(f"xr{xsl}"))

            if nsub_b:
                bl_load(0)
            for bs_ in range(nsub_b):
                gb = (T0 // 128) + bs_
                t0 = 128 * gb
                bb_ = gb % 2
                btbase = B_btb if bb_ == 0 else B_btb2
                if bs_ + 1 < nsub_b:
                    bl_load(bs_ + 1)
                vsl = gb % 3
                xsl = gb % 2
                t_bt = tbt[bb_]
                t_vl = tvl[vsl]
                t_xr = txr[xsl]
                t_band = None
                for g2 in range(2):
                    P.wait("pe", dsem(f"bt{bb_}"), t_bt)
                    P.wait("pe", ev, tok_psB[0])
                    P.wait("pe", ev, tok_psB[1])
                    t_tr = None
                    for k in range(8):
                        h = 8 * g2 + k
                        for half in range(2):
                            t_tr = P.op("pe", lambda e, h=h, k=k, half=half, btbase=btbase: e.transpose(
                                out=psB[k % 2][:, 256 * (k // 2) + 128 * half : 256 * (k // 2) + 128 * half + 128],
                                in_=big[:, btbase + 256 * h + 128 * half : btbase + 256 * h + 128 * half + 128],
                                identity=identb[:]), ep)
                    if g2 == 1:
                        tok_btb[bb_] = t_tr
                    P.wait("dv", ep, t_tr)
                    P.wait("dv", ep, tok_bb[0])
                    P.op("dv", lambda e: e.tensor_copy(
                        out=big[:, B_bb : B_bb + 1024], in_=psB[0][:]), ev)
                    t_bbe = P.op("dv", lambda e: e.tensor_copy(
                        out=big[:, B_bb + 1024 : B_bb + 2048], in_=psB[1][:]), ev)
                    tok_psB[0] = t_bbe
                    tok_psB[1] = t_bbe
                    if p2_lvl < 5:
                        continue
                    P.wait("pe", ev, t_bbe)
                    P.wait("pe", dsem(f"vr{vsl}"), t_vl)
                    if g2 == 0:
                        P.wait("pe", ev, tok_psC_ev)
                        P.wait("pe", ea, tok_psC_ac)
                    for k in range(8):
                        h = 8 * g2 + k
                        bank = psC[g2]
                        r0 = 64 * (h % 2)
                        c0 = 128 * ((h // 2) % 4)
                        boff = B_bb + 1024 * (k % 2) + 256 * (k // 2)
                        if gb == 0:
                            t_band = P.op("pe", lambda e, h=h, bank=bank, r0=r0, c0=c0, vsl=vsl, boff=boff: e.matmul(
                                out=bank[r0 : r0 + 64, c0 : c0 + 128],
                                lhsT=big[:, B_vring + 1024 * vsl + 64 * h : B_vring + 1024 * vsl + 64 * h + 64],
                                rhs=big[:, boff + 128 : boff + 256],
                                start=True, stop=True), ep)
                        else:
                            pv = (gb + 2) % 3
                            P.op("pe", lambda e, h=h, bank=bank, r0=r0, c0=c0, pv=pv, boff=boff: e.matmul(
                                out=bank[r0 : r0 + 64, c0 : c0 + 128],
                                lhsT=big[:, B_vring + 1024 * pv + 64 * h : B_vring + 1024 * pv + 64 * h + 64],
                                rhs=big[:, boff : boff + 128],
                                start=True, stop=False), ep)
                            t_band = P.op("pe", lambda e, h=h, bank=bank, r0=r0, c0=c0, vsl=vsl, boff=boff: e.matmul(
                                out=bank[r0 : r0 + 64, c0 : c0 + 128],
                                lhsT=big[:, B_vring + 1024 * vsl + 64 * h : B_vring + 1024 * vsl + 64 * h + 64],
                                rhs=big[:, boff + 128 : boff + 256],
                                start=False, stop=True), ep)
                    tok_bb[0] = t_band
                tok_vring[vsl] = t_band
                if gb > 0:
                    tok_vring[(gb + 2) % 3] = t_band
                if p2_lvl < 6:
                    continue
                asl = gb % 2
                P.wait("dv", ep, t_band)
                P.wait("dv", ep, tok_attn[asl])
                P.op("dv", lambda e, asl=asl: e.tensor_copy(
                    out=big[:, B_attn + 1024 * asl : B_attn + 1024 * asl + 512], in_=psC[0][:]), ev)
                t_ae2 = P.op("dv", lambda e, asl=asl: e.tensor_copy(
                    out=big[:, B_attn + 1024 * asl + 512 : B_attn + 1024 * asl + 1024], in_=psC[1][:]), ev)
                P.wait("pe", ev, t_ae2)
                for ncx in range(2):
                    for ch in range(8):
                        P.op("pe", lambda e, ncx=ncx, ch=ch, asl=asl: e.matmul(
                            out=psC[ncx][:, :],
                            lhsT=big[:, B_attn + 1024 * asl + 128 * ch : B_attn + 1024 * asl + 128 * ch + 128],
                            rhs=cpT[:, C * ch + 512 * ncx : C * ch + 512 * ncx + 512],
                            start=(ch == 0), stop=(ch == 7)), ep)
                t_cpj = P.n(ep)
                tok_attn[asl] = t_cpj
                if p2_lvl < 7:
                    continue
                P.wait("dv", ep, t_cpj)
                P.op("dv", lambda e: e.tensor_copy(out=f32a[:, F_sc : F_sc + 512], in_=psC[0][:]), ev)
                t_yev = P.op("dv", lambda e: e.tensor_copy(out=f32a[:, F_sc + 512 : F_sc + 1024], in_=psC[1][:]), ev)
                tok_psC_ev = t_yev
                tok_psC_ac = 0
                P.op("dv", lambda e: e.tensor_reduce(out=stats[:, 36:37],
                                                     in_=f32a[:, F_sc : F_sc + 1024],
                                                     axis=AX.X, op=ALU.add), ev)
                P.wait("ac", ev, t_yev)
                t_sq2 = P.op("ac", lambda e: e.activation(out=junk[:], in_=f32a[:, F_sc : F_sc + 1024],
                                                          func=AF.Square, accum_out=stats[:, 37:38]), ea)
                P.wait("dv", ea, t_sq2)
                P.op("dv", lambda e: e.tensor_scalar(out=stats[:, 38:39], in0=stats[:, 36:37],
                                                     scalar1=-1.0 / C, scalar2=None, op0=ALU.mult), ev)
                P.op("dv", lambda e: e.tensor_scalar(out=stats[:, 39:40], in0=stats[:, 37:38],
                                                     scalar1=1.0 / C, scalar2=None, op0=ALU.mult), ev)
                P.op("dv", lambda e: e.tensor_tensor(out=stats[:, 40:41], in0=stats[:, 38:39],
                                                     in1=stats[:, 38:39], op=ALU.mult), ev)
                t_v2 = P.op("dv", lambda e: e.tensor_tensor(out=stats[:, 41:42], in0=stats[:, 39:40],
                                                            in1=stats[:, 40:41], op=ALU.subtract), ev)
                P.wait("ac", ev, t_v2)
                P.op("ac", lambda e: e.activation(out=stats[:, 42:43], in_=stats[:, 41:42],
                                                  func=AF.Ln, bias=epscol[:, 0:1]), ea)
                t_r2 = P.op("ac", lambda e: e.activation(out=stats[:, 43:44], in_=stats[:, 42:43],
                                                         func=AF.Exp, scale=-0.5), ea)
                if p2_lvl < 8:
                    continue
                P.wait("dv", ea, t_r2)
                P.wait("dv", dsem(f"xr{xsl}"), t_xr)
                P.wait("dv", dsem(f"x2{asl}"), tok_x2b[asl])
                t_y1 = P.op("dv", lambda e: e.tensor_scalar(
                    out=f32a[:, F_sc : F_sc + 1024], in0=f32a[:, F_sc : F_sc + 1024],
                    scalar1=stats[:, 38:39], scalar2=stats[:, 43:44],
                    op0=ALU.add, op1=ALU.mult), ev)
                P.wait("dv", ep, tok_blob[0])
                t_x2 = P.op("dv", lambda e, xsl=xsl, asl=asl: e.tensor_tensor(
                    out=big[:, B_x2b + 1024 * asl : B_x2b + 1024 * asl + 1024],
                    in0=f32a[:, F_sc : F_sc + 1024],
                    in1=f32a[:, F_xb + 1024 * xsl : F_xb + 1024 * xsl + 1024],
                    op=ALU.add), ev)
                tok_xres[xsl] = t_x2
                P.wait("sp", ev, t_x2)
                t_x2d = P.dma("sp", lambda e, t0=t0, asl=asl: e.dma_start(
                    out=bass.AP(tensor=x2s, offset=t0 * C, ap=[[C, 128], [1, C]]),
                    in_=big[:, B_x2b + 1024 * asl : B_x2b + 1024 * asl + 1024]), dsem(f"x2{asl}"))
                if dbg:
                    P.wait("pl", ev, t_x2)
                    t_dbg = P.dma("pl", lambda e, t0=t0, asl=asl: e.dma_start(
                        out=dbg_t["d_x2"][t0 : t0 + 128, :],
                        in_=big[:, B_x2b + 1024 * asl : B_x2b + 1024 * asl + 1024]), dsem("dbg"))
                    P.wait("dv", dsem("dbg"), t_dbg)
                tok_x2b[asl] = t_x2d
        fence()

        for r0, r1 in splits(4):
            t_wd = P.dma("sp", lambda e, r0=r0, r1=r1: e.dma_start(
                out=fp8a[r0:r1, 0:32768],
                in_=bass.AP(tensor=wd8, offset=C * r0,
                            ap=[[C, r1 - r0], [128 * C, 32], [1, C]])), dsem("wd"))

        tok_x2t = [0, 0]
        tok_a1i = 0
        tok_a1b = 0
        tok_a1fm = 0
        tok_gw = [0, 0]
        tok_uw = [0, 0]
        tok_gA = [0, 0]
        tok_uA = [0, 0]
        tok_sil = 0
        tok_a2tm = 0
        tok_a2i = 0
        tok_a2fm = 0
        tok_ybuf = 0
        tok_scr_ac = 0
        tok_ob = [0, 0]
        tok_obx = 0
        tok_xtb = 0
        tok_g = 0

        _stopped3 = stop_after in ("consts", "wa", "wb", "wc", "p1", "p2", "p2w", "p2a")
        _NO_ILV = False
        t_ic_done_prev = 0
        t_sg3 = [0] * 8
        t_fm3 = [0] * 8
        t_dn3 = [0] * 8

        def ln4_tt(tt, sb):
            nonlocal tok_psC_ac, tok_ybuf
            P.wait("dv", ep, t_dn3[tt])
            P.op("dv", lambda e, tt=tt: e.tensor_copy(
                out=cpT[:, 1024 * tt : 1024 * tt + 512], in_=psC[0][:]), ev)
            t_ye2 = P.op("dv", lambda e, tt=tt: e.tensor_copy(
                out=cpT[:, 1024 * tt + 512 : 1024 * tt + 1024], in_=psC[1][:]), ev)
            P.op("dv", lambda e, tt=tt, sb=sb: e.tensor_reduce(
                out=stats[:, sb + 56 : sb + 57], in_=cpT[:, 1024 * tt : 1024 * tt + 1024],
                axis=AX.X, op=ALU.add), ev)
            P.wait("ac", ev, t_ye2)
            t_sq4 = P.op("ac", lambda e, tt=tt, sb=sb: e.activation(
                out=junk[:], in_=cpT[:, 1024 * tt : 1024 * tt + 1024],
                func=AF.Square, accum_out=stats[:, sb + 58 : sb + 59]), ea)
            tok_psC_ac = t_sq4
            P.op("dv", lambda e, tt=tt, sb=sb: e.tensor_scalar(out=stats[:, sb + 40 + tt : sb + 41 + tt],
                                                        in0=stats[:, sb + 56 : sb + 57],
                                                        scalar1=-1.0 / C, scalar2=None, op0=ALU.mult), ev)
            P.wait("dv", ea, t_sq4)
            P.op("dv", lambda e, tt=tt, sb=sb: e.tensor_scalar(out=stats[:, sb + 48 + tt : sb + 49 + tt],
                                                        in0=stats[:, sb + 58 : sb + 59],
                                                        scalar1=1.0 / C, scalar2=None, op0=ALU.mult), ev)
            P.op("dv", lambda e, tt=tt, sb=sb: e.tensor_tensor(out=stats[:, sb + 56 : sb + 57],
                                                        in0=stats[:, sb + 40 + tt : sb + 41 + tt],
                                                        in1=stats[:, sb + 40 + tt : sb + 41 + tt], op=ALU.mult), ev)
            P.op("dv", lambda e, tt=tt, sb=sb: e.tensor_tensor(out=stats[:, sb + 48 + tt : sb + 49 + tt],
                                                        in0=stats[:, sb + 48 + tt : sb + 49 + tt],
                                                        in1=stats[:, sb + 56 : sb + 57], op=ALU.subtract), ev)
            tok_ybuf = t_ye2

        def ln3_tt(F2, tt, t_icd):
            nonlocal tok_scr_ac, tok_a1i, tok_a1b
            sb = 64 * (F2 % 2)
            TF2 = F2 * P3S
            sl = tt % 2
            P.wait("sp", ea, tok_x2t[sl])
            for r0, r1 in splits(2):
                t_xl = P.dma("sp", lambda e, TF2=TF2, tt=tt, sl=sl, r0=r0, r1=r1: e.dma_start(
                    out=ternb[r0:r1, 1024 * sl : 1024 * sl + 1024],
                    in_=bass.AP(tensor=x2s, offset=(TF2 + 128 * tt + r0) * C,
                                ap=[[C, r1 - r0], [1, C]])), dsem(f"xt{sl}"))

            def x2t_ap(sl=sl):
                return ternb[:, 1024 * sl : 1024 * sl + 1024]

            P.wait("dv", dsem(f"xt{sl}"), t_xl)
            P.wait("dv", ea, tok_scr_ac)
            P.op("dv", lambda e, sl=sl: e.tensor_reduce(
                out=stats[:, sb + 56 : sb + 57], in_=x2t_ap(sl), axis=AX.X, op=ALU.add), ev)
            P.op("dv", lambda e, sl=sl: e.tensor_reduce(
                out=stats[:, sb + 58 : sb + 59], in_=x2t_ap(sl), axis=AX.X, op=ALU.max), ev)
            P.op("dv", lambda e, sl=sl: e.tensor_reduce(
                out=stats[:, sb + 59 : sb + 60], in_=x2t_ap(sl), axis=AX.X, op=ALU.min, negate=True), ev)
            P.wait("ac", dsem(f"xt{sl}"), t_xl)
            t_sq = P.op("ac", lambda e, sl=sl: e.activation(
                out=junk[:], in_=x2t_ap(sl), func=AF.Square, accum_out=stats[:, sb + 57 : sb + 58]), ea)
            P.op("dv", lambda e, tt=tt: e.tensor_scalar(
                out=stats[:, sb + 0 + tt : sb + 1 + tt], in0=stats[:, sb + 56 : sb + 57],
                scalar1=-1.0 / C, scalar2=None, op0=ALU.mult), ev)
            P.wait("dv", ea, t_sq)
            P.op("dv", lambda e: e.tensor_scalar(out=stats[:, sb + 60 : sb + 61], in0=stats[:, sb + 57 : sb + 58],
                                                 scalar1=1.0 / C, scalar2=None, op0=ALU.mult), ev)
            P.op("dv", lambda e, tt=tt: e.tensor_tensor(out=stats[:, sb + 61 : sb + 62],
                                                        in0=stats[:, sb + 0 + tt : sb + 1 + tt],
                                                        in1=stats[:, sb + 0 + tt : sb + 1 + tt], op=ALU.mult), ev)
            t_v3 = P.op("dv", lambda e: e.tensor_tensor(out=stats[:, sb + 62 : sb + 63], in0=stats[:, sb + 60 : sb + 61],
                                                        in1=stats[:, sb + 61 : sb + 62], op=ALU.subtract), ev)
            P.wait("ac", ev, t_v3)
            P.op("ac", lambda e: e.activation(out=stats[:, sb + 63 : sb + 64], in_=stats[:, sb + 62 : sb + 63],
                                              func=AF.Ln, bias=epscol[:, 0:1]), ea)
            t_r3 = P.op("ac", lambda e, tt=tt: e.activation(
                out=stats[:, sb + 8 + tt : sb + 9 + tt], in_=stats[:, sb + 63 : sb + 64],
                func=AF.Exp, scale=-0.5), ea)
            P.wait("dv", ea, t_r3)
            P.op("dv", lambda e, tt=tt: e.tensor_scalar(out=stats[:, sb + 58 : sb + 59], in0=stats[:, sb + 58 : sb + 59],
                                                        scalar1=stats[:, sb + 0 + tt : sb + 1 + tt],
                                                        scalar2=None, op0=ALU.add), ev)
            P.op("dv", lambda e, tt=tt: e.tensor_scalar(out=stats[:, sb + 59 : sb + 60], in0=stats[:, sb + 59 : sb + 60],
                                                        scalar1=stats[:, sb + 0 + tt : sb + 1 + tt],
                                                        scalar2=None, op0=ALU.add), ev)
            P.op("dv", lambda e: e.tensor_tensor(out=stats[:, sb + 58 : sb + 59], in0=stats[:, sb + 58 : sb + 59],
                                                 in1=stats[:, sb + 59 : sb + 60], op=ALU.max), ev)
            P.op("dv", lambda e, tt=tt: e.tensor_tensor(out=stats[:, sb + 16 + tt : sb + 17 + tt],
                                                        in0=stats[:, sb + 58 : sb + 59],
                                                        in1=stats[:, sb + 8 + tt : sb + 9 + tt], op=ALU.mult), ev)
            P.op("dv", lambda e, tt=tt: e.tensor_scalar(out=stats[:, sb + 60 : sb + 61],
                                                        in0=stats[:, sb + 16 + tt : sb + 17 + tt],
                                                        scalar1=1e-5, scalar2=None, op0=ALU.max), ev)
            P.op("dv", lambda e: e.reciprocal(out=stats[:, sb + 61 : sb + 62], in_=stats[:, sb + 60 : sb + 61]), ev)
            P.op("dv", lambda e: e.tensor_scalar(out=stats[:, sb + 62 : sb + 63], in0=stats[:, sb + 61 : sb + 62],
                                                 scalar1=127.0, scalar2=None, op0=ALU.mult), ev)
            P.op("dv", lambda e, tt=tt: e.tensor_tensor(out=stats[:, sb + 61 : sb + 62], in0=stats[:, sb + 62 : sb + 63],
                                                        in1=stats[:, sb + 8 + tt : sb + 9 + tt], op=ALU.mult), ev)
            t_ab = P.op("dv", lambda e, tt=tt: e.tensor_tensor(out=stats[:, sb + 63 : sb + 64],
                                                               in0=stats[:, sb + 61 : sb + 62],
                                                               in1=stats[:, sb + 0 + tt : sb + 1 + tt],
                                                               op=ALU.mult), ev)
            P.op("dv", lambda e, tt=tt: e.tensor_tensor(out=stats[:, sb + 24 + tt : sb + 25 + tt],
                                                        in0=stats[:, sb + 16 + tt : sb + 17 + tt],
                                                        in1=scal[:, 6:7], op=ALU.mult), ev)
            P.op("dv", lambda e, tt=tt: e.tensor_scalar(out=stats[:, sb + 24 + tt : sb + 25 + tt],
                                                        in0=stats[:, sb + 24 + tt : sb + 25 + tt],
                                                        scalar1=1.0 / 254.0, scalar2=None, op0=ALU.mult), ev)
            t_sg3[tt] = P.n(ev)
            P.op("dv", lambda e, tt=tt: e.tensor_tensor(out=stats[:, sb + 32 + tt : sb + 33 + tt],
                                                        in0=stats[:, sb + 16 + tt : sb + 17 + tt],
                                                        in1=scal[:, 7:8], op=ALU.mult), ev)
            P.op("dv", lambda e, tt=tt: e.tensor_tensor(out=stats[:, sb + 32 + tt : sb + 33 + tt],
                                                        in0=stats[:, sb + 32 + tt : sb + 33 + tt],
                                                        in1=scal[:, 8:9], op=ALU.mult), ev)
            P.op("dv", lambda e, tt=tt: e.tensor_scalar(out=stats[:, sb + 32 + tt : sb + 33 + tt],
                                                        in0=stats[:, sb + 32 + tt : sb + 33 + tt],
                                                        scalar1=1.0 / 64516.0, scalar2=None, op0=ALU.mult), ev)
            P.wait("ac", ev, t_ab)
            P.wait("ac", ev, tok_a1i)
            t_ai = P.op("ac", lambda e, sl=sl: e.activation(
                out=i16a[:, 2048:3072], in_=x2t_ap(sl), func=AF.Identity,
                scale=stats[:, sb + 61 : sb + 62], bias=stats[:, sb + 63 : sb + 64]), ea)
            tok_x2t[sl] = t_ai
            tok_scr_ac = t_ai
            P.wait("dv", ea, t_ai)
            P.wait("dv", ep, tok_a1b)
            t_a1b = P.op("dv", lambda e: e.tensor_copy(out=wdT[:, 0:1024], in_=i16a[:, 2048:3072]), ev)
            tok_a1i = t_a1b
            P.wait("pe", ev, t_a1b)
            P.wait("pe", ev, tok_psB[0])
            t_tr = None
            for fc in range(8):
                t_tr = P.op("pe", lambda e, fc=fc: e.transpose(
                    out=psB[0][:, 128 * fc : 128 * fc + 128],
                    in_=wdT[:, 128 * fc : 128 * fc + 128], identity=identb[:]), ep)
            tok_a1b = t_tr
            P.wait("dv", ep, t_tr)
            if tt == 0:
                P.wait("dv", ep, t_icd)
            t_fm = P.op("dv", lambda e, tt=tt: e.tensor_copy(
                out=bass.AP(tensor=df[:].tensor, offset=128 * tt,
                            ap=[list(df[:].ap[0]), [P3S, 8], [1, 128]]),
                in_=psB[0][:].rearrange("p (a b) -> p a b", a=8)), ev)
            tok_psB[0] = t_fm
            t_fm3[tt] = t_fm


        for F in range(NSUP3 if not _stopped3 else 0):
            TF = F * P3S
            sb = 64 * (F % 2)
            if F == 0 or _NO_ILV:
                for tt in range(NTT):
                    ln3_tt(F, tt, t_ic_done_prev)
            t_a1_all = P.n(ev)
            for ic in range(8 if p3_lvl >= 2 else 0):
                wsl = ic % 2
                P.wait("sp", ep, tok_gw[wsl])
                for r0, r1 in splits(2):
                    t_gl = P.dma("sp", lambda e, ic=ic, wsl=wsl, r0=r0, r1=r1: e.dma_start(
                        out=fp8a[r0:r1, 32768 + 4096 * wsl : 32768 + 4096 * wsl + 4096],
                        in_=bass.AP(tensor=wg8, offset=512 * ic + INNER * r0,
                                    ap=[[INNER, r1 - r0], [128 * INNER, 8], [1, 512]])), dsem(f"wg{wsl}"))
                P.wait("sp", ep, tok_uw[wsl])
                for r0, r1 in splits(2):
                    t_ul = P.dma("sp", lambda e, ic=ic, wsl=wsl, r0=r0, r1=r1: e.dma_start(
                        out=fp8a[r0:r1, 40960 + 4096 * wsl : 40960 + 4096 * wsl + 4096],
                        in_=bass.AP(tensor=wu8, offset=512 * ic + INNER * r0,
                                    ap=[[INNER, r1 - r0], [128 * INNER, 8], [1, 512]])), dsem(f"wu{wsl}"))
                t_gmm = t_umm = None
                for tt in range(NTT):
                    ba = tt % 2
                    P.wait("pe", ev, t_fm3[tt] if ic == 0 else t_a1_all)
                    P.wait("pe", dsem(f"wg{wsl}"), t_gl)
                    P.wait("pe", ea, tok_gA[ba])
                    for fc in range(8):
                        P.op("pe", lambda e, tt=tt, fc=fc, ba=ba, wsl=wsl: e.matmul(
                            out=psA[ba][:, :],
                            lhsT=bass.AP(tensor=df[:].tensor, offset=128 * tt + P3S * fc,
                                         ap=[list(df[:].ap[0]), [1, 128]]),
                            rhs=fp8a[:, 32768 + 4096 * wsl + 512 * fc : 32768 + 4096 * wsl + 512 * fc + 512],
                            start=(fc == 0), stop=(fc == 7)), ep)
                    t_gmm = P.n(ep)
                    P.wait("pe", dsem(f"wu{wsl}"), t_ul)
                    P.wait("pe", ev, tok_uA[ba])
                    for fc in range(8):
                        P.op("pe", lambda e, tt=tt, fc=fc, ba=ba, wsl=wsl: e.matmul(
                            out=psA[2 + ba][:, :],
                            lhsT=bass.AP(tensor=df[:].tensor, offset=128 * tt + P3S * fc,
                                         ap=[list(df[:].ap[0]), [1, 128]]),
                            rhs=fp8a[:, 40960 + 4096 * wsl + 512 * fc : 40960 + 4096 * wsl + 512 * fc + 512],
                            start=(fc == 0), stop=(fc == 7)), ep)
                    t_umm = P.n(ep)
                    P.wait("ac", ep, t_gmm)
                    P.wait("ac", ev, tok_sil)
                    if ic == 0:
                        P.wait("ac", ev, t_sg3[tt])
                    t_si = P.op("ac", lambda e, tt=tt, sb=sb: e.activation(
                        out=wdT[:, 1024:1536], in_=psA[tt % 2][:],
                        func=AF.Silu, scale=stats[:, sb + 24 + tt : sb + 25 + tt]), ea)
                    tok_gA[tt % 2] = t_si
                    P.wait("dv", ep, t_umm)
                    P.wait("dv", ea, t_si)
                    if ic == 0 and tt in (2, 3):
                        P.wait("dv", dsem("obx"), tok_obx)
                    if ic == 0 and tt in (4, 5):
                        P.wait("dv", dsem("xtb"), tok_xtb)
                    t_g = P.op("dv", lambda e, tt=tt, ic=ic: e.tensor_tensor(
                        out=big[:, 4096 * tt + 512 * ic : 4096 * tt + 512 * ic + 512],
                        in0=wdT[:, 1024:1536], in1=psA[2 + tt % 2][:], op=ALU.mult), ev)
                    tok_sil = t_g
                    tok_uA[tt % 2] = t_g
                tok_gw[wsl] = t_gmm
                tok_uw[wsl] = t_umm
            t_ic_done = P.n(ep)
            t_ic_done_prev = t_ic_done
            t_g_all = P.n(ev)
            if dbg and F == 0:
                P.wait("pl", ev, t_g_all)
                t_dg = P.dma("pl", lambda e: e.dma_start(out=dbg_t["d_g"], in_=big[:, 0 : NTT * INNER]), dsem("dbg"))
                P.wait("dv", dsem("dbg"), t_dg)
            for tt in range(NTT if p3_lvl >= 3 else 0):
                P.wait("dv", ea, tok_scr_ac)
                t_mg = P.op("dv", lambda e, tt=tt, sb=sb: e.tensor_reduce(
                    out=stats[:, sb + 56 : sb + 57], in_=big[:, 4096 * tt : 4096 * tt + 4096],
                    axis=AX.X, op=ALU.max, apply_absolute_value=True), ev)
                P.op("dv", lambda e, sb=sb: e.tensor_scalar(out=stats[:, sb + 57 : sb + 58], in0=stats[:, sb + 56 : sb + 57],
                                                     scalar1=1e-30, scalar2=None, op0=ALU.max), ev)
                P.op("dv", lambda e, sb=sb: e.reciprocal(out=stats[:, sb + 58 : sb + 59], in_=stats[:, sb + 57 : sb + 58]), ev)
                t_s2 = P.op("dv", lambda e, sb=sb: e.tensor_scalar(out=stats[:, sb + 59 : sb + 60], in0=stats[:, sb + 58 : sb + 59],
                                                            scalar1=127.0, scalar2=None, op0=ALU.mult), ev)
                P.op("dv", lambda e, tt=tt, sb=sb: e.tensor_tensor(out=stats[:, sb + 32 + tt : sb + 33 + tt],
                                                            in0=stats[:, sb + 32 + tt : sb + 33 + tt],
                                                            in1=stats[:, sb + 56 : sb + 57], op=ALU.mult), ev)
                t_c = None
                for pair in range(2):
                    for hf2 in range(2):
                        hf = 2 * pair + hf2
                        P.wait("ac", ev, t_s2)
                        P.wait("ac", ev, tok_a2i)
                        t_c = P.op("ac", lambda e, tt=tt, hf=hf, hf2=hf2, sb=sb: e.activation(
                            out=i16a[:, 1024 * hf2 : 1024 * hf2 + 1024],
                            in_=big[:, 4096 * tt + 1024 * hf : 4096 * tt + 1024 * hf + 1024],
                            func=AF.Identity, scale=stats[:, sb + 59 : sb + 60]), ea)
                    P.wait("dv", ea, t_c)
                    P.wait("dv", ep, tok_a2tm)
                    t_cp3 = None
                    for hf2 in range(2):
                        hf = 2 * pair + hf2
                        t_cp3 = P.op("dv", lambda e, hf=hf, hf2=hf2: e.tensor_copy(
                            out=wvT[:, 1024 * hf : 1024 * hf + 1024],
                            in_=i16a[:, 1024 * hf2 : 1024 * hf2 + 1024]), ev)
                    tok_a2i = t_cp3
                tok_scr_ac = t_c
                t_a2tm = tok_a2i
                P.wait("pe", ev, t_a2tm)
                P.wait("pe", ep, tok_a2fm)
                t_fm2 = None
                for w4 in range(2):
                    P.wait("pe", ev, tok_psB[0])
                    P.wait("pe", ev, tok_psB[1])
                    t_tr = None
                    for kk in range(16):
                        icc = 16 * w4 + kk
                        t_tr = P.op("pe", lambda e, icc=icc, kk=kk: e.transpose(
                            out=psB[kk % 2][:, 128 * (kk // 2) : 128 * (kk // 2) + 128],
                            in_=wvT[:, 128 * icc : 128 * icc + 128], identity=identb[:]), ep)
                    if w4 == 1:
                        tok_a2tm = t_tr
                    P.wait("dv", ep, t_tr)
                    for b in range(2):
                        t_fm2 = P.op("dv", lambda e, w4=w4, b=b: e.tensor_copy(
                            out=bass.AP(tensor=wvT[:].tensor,
                                        offset=4096 + 2048 * w4 + 128 * b,
                                        ap=[list(wvT[:].ap[0]), [256, 8], [1, 128]]),
                            in_=psB[b][:]), ev)
                    tok_psB[0] = t_fm2
                    tok_psB[1] = t_fm2
                t_a2f = t_fm2
                if F + 1 < NSUP3 and p3_lvl >= 4 and not _NO_ILV:
                    ln3_tt(F + 1, tt, t_ic_done)
                if tt > 0:
                    ln4_tt(tt - 1, sb)
                P.wait("pe", ev, t_a2f)
                P.wait("pe", dsem("wd"), t_wd)
                P.wait("pe", ev, tok_ybuf)
                P.wait("pe", ea, tok_psC_ac)
                for ncx in range(2):
                    for icc in range(32):
                        P.op("pe", lambda e, icc=icc, ncx=ncx: e.matmul(
                            out=psC[ncx][:, :],
                            lhsT=wvT[:, 4096 + 128 * icc : 4096 + 128 * icc + 128],
                            rhs=fp8a[:, 1024 * icc + 512 * ncx : 1024 * icc + 512 * ncx + 512],
                            start=(icc == 0), stop=(icc == 31)), ep)
                t_dn = P.n(ep)
                tok_a2fm = t_dn
                t_dn3[tt] = t_dn
            if NTT and p3_lvl >= 3:
                ln4_tt(NTT - 1, sb)
            t_q_ev = P.n(ev)
            t_q_ea = P.n(ea)
            if p3_lvl < 4:
                continue
            P.op("dv", lambda e, sb=sb: e.tensor_tensor(out=stats[:, sb + 56 : sb + 56 + NTT],
                                                 in0=stats[:, sb + 32 : sb + 32 + NTT],
                                                 in1=stats[:, sb + 32 : sb + 32 + NTT], op=ALU.mult), ev)
            t_vt = P.op("dv", lambda e, sb=sb: e.tensor_tensor(out=stats[:, sb + 48 : sb + 48 + NTT],
                                                        in0=stats[:, sb + 48 : sb + 48 + NTT],
                                                        in1=stats[:, sb + 56 : sb + 56 + NTT], op=ALU.mult), ev)
            P.wait("ac", ev, t_vt)
            P.op("ac", lambda e, sb=sb: e.activation(out=stats[:, sb + 56 : sb + 56 + NTT], in_=stats[:, sb + 48 : sb + 48 + NTT],
                                              func=AF.Ln, bias=epscol[:, 0:1]), ea)
            t_r4 = P.op("ac", lambda e, sb=sb: e.activation(out=stats[:, sb + 48 : sb + 48 + NTT], in_=stats[:, sb + 56 : sb + 56 + NTT],
                                                     func=AF.Exp, scale=-0.5), ea)
            P.wait("dv", ea, t_r4)
            t_cr = P.op("dv", lambda e, sb=sb: e.tensor_tensor(out=stats[:, sb + 48 : sb + 48 + NTT],
                                                        in0=stats[:, sb + 48 : sb + 48 + NTT],
                                                        in1=stats[:, sb + 32 : sb + 32 + NTT], op=ALU.mult), ev)
            P.wait("sp", ev, t_q_ev)
            P.wait("sp", ea, t_q_ea)
            P.wait("sp", dsem("obx"), tok_obx)
            for r0, r1 in splits(4):
                t_xb = P.dma("sp", lambda e, TF=TF, r0=r0, r1=r1: e.dma_start(
                    out=big[r0:r1, 16384 : 16384 + NTT * 1024],
                    in_=bass.AP(tensor=x2s, offset=(TF + r0) * C,
                                ap=[[C, r1 - r0], [128 * C, NTT], [1, C]])), dsem("xtb"))
            tok_xtb = t_xb
            P.wait("dv", dsem("xtb"), t_xb)
            t_ou = None
            for tt in range(NTT):
                P.op("dv", lambda e, tt=tt, sb=sb: e.tensor_scalar(
                    out=f32a[:, F_sc : F_sc + 1024], in0=cpT[:, 1024 * tt : 1024 * tt + 1024],
                    scalar1=stats[:, sb + 40 + tt : sb + 41 + tt], scalar2=stats[:, sb + 48 + tt : sb + 49 + tt],
                    op0=ALU.add, op1=ALU.mult), ev)
                t_ou = P.op("dv", lambda e, tt=tt: e.tensor_tensor(
                    out=big[:, 8192 + 1024 * tt : 8192 + 1024 * tt + 1024],
                    in0=f32a[:, F_sc : F_sc + 1024],
                    in1=big[:, 16384 + 1024 * tt : 16384 + 1024 * tt + 1024], op=ALU.add), ev)
            P.wait("pl", ev, t_ou)
            for r0, r1 in splits(4):
                tok_obx = P.dma("pl", lambda e, TF=TF, r0=r0, r1=r1: e.dma_start(
                    out=bass.AP(tensor=out_t, offset=(TF + r0) * C,
                                ap=[[C, r1 - r0], [128 * C, NTT], [1, C]]),
                    in_=big[r0:r1, 8192 : 8192 + NTT * 1024]), dsem("obx"))

            tok_ybuf = 0
            tok_a1fm = tok_gw[0] if tok_gw[0] else 0
        fence()

        with nc.Block() as block:
            @block.sync
            def _(e):
                P.replay("sp", e)

            @block.tensor
            def _(e):
                P.replay("pe", e)

            @block.vector
            def _(e):
                P.replay("dv", e)

            @block.scalar
            def _(e):
                P.replay("ac", e)

            @block.gpsimd
            def _(e):
                P.replay("pl", e)

    return nc



def _host_consts(inputs):
    sfw = np.asarray(inputs["L0_attn_strain_fused_w"], np.float32)
    sfb = np.asarray(inputs["L0_attn_strain_fused_b"], np.float32)
    bondw = np.asarray(inputs["L0_attn_bond_out_w"], np.float32)
    dmgw = np.asarray(inputs["L0_attn_damage_out_w"], np.float32)
    dmgb = np.asarray(inputs["L0_attn_damage_out_b"], np.float32)

    identb = np.eye(128, dtype=ml_dtypes.bfloat16)
    maskneg = np.zeros((128, WIN), np.float32)
    r = np.arange(128)[:, None]
    wv = np.arange(WIN)[None, :]
    maskneg[(r + wv) < (WIN - 1)] = -1e9

    logitw = np.zeros((128, 8), np.float32)
    for h in range(4):
        logitw[32 * h : 32 * h + BD, h] = bondw[0]
        logitw[32 * h + BD : 32 * h + 2 * BD, 4 + h] = dmgw[0]

    dfw32 = np.zeros((128, 256), np.float32)
    for h in range(8):
        c0 = (h % 4) * 32 + (h // 4) * 128
        dfw32[16 * h : 16 * h + BD, c0 : c0 + 2 * BD] = sfw.T
    csfb = np.repeat(np.tile(sfb, 4)[:, None], WIN, axis=1).astype(np.float32)
    dbrep = np.full((128, 1), float(dmgb[0]), np.float32)
    return {
        "identb": identb, "maskneg": maskneg.astype(ml_dtypes.bfloat16), "logitw32": logitw,
        "dfw32": dfw32, "csfb": csfb, "dbrep": dbrep,
    }


def kernel(**inputs):
    T_full = int(np.asarray(inputs["L0_attn_x"]).shape[1])
    nc = build(T=T_full, dbg=False)

    x = np.ascontiguousarray(np.asarray(inputs["L0_attn_x"], np.float32))
    consts = _host_consts(inputs)
    shared = {
        "wdisp": np.ascontiguousarray(np.asarray(inputs["L0_attn_W_disp"], np.float32)),
        "wval": np.ascontiguousarray(np.asarray(inputs["L0_attn_W_val"], np.float32)),
        "relpos": np.ascontiguousarray(np.asarray(inputs["L0_attn_rel_pos_emb"], np.float32)[:WIN]),
        "pposw": np.ascontiguousarray(np.asarray(inputs["L0_attn_pos_proj_w"], np.float32)),
        "cprojw": np.ascontiguousarray(np.asarray(inputs["L0_attn_c_proj_w"], np.float32)),
        "wgate": np.ascontiguousarray(np.asarray(inputs["L0_ffn_w_gate"], np.float32)),
        "wup": np.ascontiguousarray(np.asarray(inputs["L0_ffn_w_up"], np.float32)),
        "wdown": np.ascontiguousarray(np.asarray(inputs["L0_ffn_w_down"], np.float32)),
        **consts,
    }
    ncores = x.shape[0]
    in_maps = [{"x": np.ascontiguousarray(x[c]), **shared} for c in range(ncores)]
    res = run_bass_kernel_spmd(nc, in_maps, core_ids=list(range(ncores)))
    outs = [np.asarray(r["out"], np.float32) for r in res.results]
    return np.stack(outs, axis=0)


# version 19
# speedup vs baseline: 1.0229x; 1.0229x over previous
import numpy as np
import ml_dtypes
import concourse.bass as bass
import concourse.mybir as mybir
from concourse.bass_utils import run_bass_kernel_spmd
from contextlib import ExitStack

F32 = mybir.dt.float32
BF16 = mybir.dt.bfloat16
I16 = mybir.dt.int16
FP8 = mybir.dt.float8e4
AF = mybir.ActivationFunctionType
ALU = mybir.AluOpType
AX = mybir.AxisListType

C = 1024
NH = 16
BD = 16
HS = 64
WIN = 64
INNER = 4096
PAD = 256
EPS = 1e-5


class Planner:
    def __init__(self):
        self.steps = []
        self.counts = {}
        self.engs = ("sp", "pe", "dv", "ac", "pl")
        self.chain = ("dv", "ac")
        self.last_chain = {}

    def reg(self, s):
        self.counts[id(s)] = 0
        return s

    def n(self, s):
        return self.counts[id(s)]

    def wait(self, eng, s, val=None):
        v = self.counts[id(s)] if val is None else val
        if v > 0:
            self.steps.append((eng, "w", (s, v)))

    def op(self, eng, fn, s=None, inc=1):
        if eng in self.chain and s is not None:
            cur = self.counts[id(s)]
            if cur > 0 and self.last_chain.get(eng, -1) < cur:
                self.steps.append((eng, "w", (s, cur)))
                self.last_chain[eng] = cur
        self.steps.append((eng, "o", (fn, s, inc)))
        if s is not None:
            self.counts[id(s)] += inc
            return self.counts[id(s)]
        return None

    def dma(self, eng, fn, s):
        return self.op(eng, fn, s, 16)

    def replay(self, key, eng):
        for k, kind, pl in self.steps:
            if k != key:
                continue
            if kind == "w":
                eng.wait_ge(pl[0], pl[1])
            else:
                fn, s, inc = pl
                r = fn(eng)
                if s is not None:
                    r.then_inc(s, inc)


def build(T=4096, dbg=False, stop_after=None, p1_lvl=8, p2_lvl=8, p3_lvl=4):
    NT = T // 128
    P2S = min(2048, T)
    NSUP2 = T // P2S
    NSUB = P2S // 128
    NQC = NSUB // 4
    P3S = min(1024, T)
    NSUP3 = T // P3S
    NTT = P3S // 128
    DF1 = T + 128

    nc = bass.Bass("TRN2", target_bir_lowering=False)

    x_in = nc.dram_tensor("x", [T, C], F32, kind="ExternalInput").ap()
    wdisp = nc.dram_tensor("wdisp", [NH * BD, C], F32, kind="ExternalInput").ap()
    wval = nc.dram_tensor("wval", [C, C], F32, kind="ExternalInput").ap()
    relpos = nc.dram_tensor("relpos", [WIN, BD], F32, kind="ExternalInput").ap()
    pposw = nc.dram_tensor("pposw", [BD, BD], F32, kind="ExternalInput").ap()
    cprojw = nc.dram_tensor("cprojw", [C, C], F32, kind="ExternalInput").ap()
    wgate = nc.dram_tensor("wgate", [INNER, C], F32, kind="ExternalInput").ap()
    wup = nc.dram_tensor("wup", [INNER, C], F32, kind="ExternalInput").ap()
    wdown = nc.dram_tensor("wdown", [C, INNER], F32, kind="ExternalInput").ap()
    identb_in = nc.dram_tensor("identb", [128, 128], BF16, kind="ExternalInput").ap()
    maskneg_in = nc.dram_tensor("maskneg", [128, WIN], BF16, kind="ExternalInput").ap()
    logitw_in = nc.dram_tensor("logitw32", [128, 8], F32, kind="ExternalInput").ap()
    dfw_in = nc.dram_tensor("dfw32", [128, 256], F32, kind="ExternalInput").ap()
    csfb_in = nc.dram_tensor("csfb", [128, WIN], F32, kind="ExternalInput").ap()
    dbrep_in = nc.dram_tensor("dbrep", [128, 1], F32, kind="ExternalInput").ap()

    wp = nc.dram_tensor("wp", [NH * T * PAD], BF16, kind="Internal")
    x2s = nc.dram_tensor("x2s", [T * C], BF16, kind="Internal")
    vals = nc.dram_tensor("vals", [T * C], BF16, kind="Internal")
    wg8 = nc.dram_tensor("wg8", [8 * 128 * INNER], FP8, kind="Internal")
    wu8 = nc.dram_tensor("wu8", [8 * 128 * INNER], FP8, kind="Internal")
    wd8 = nc.dram_tensor("wd8", [32 * 128 * C], FP8, kind="Internal")

    out_t = nc.dram_tensor("out", [T, C], F32, kind="ExternalOutput")
    out = out_t.ap()
    dbg_t = {}
    if dbg:
        for nm, shp in [("d_df", [128, 4 * DF1]), ("d_val", [T, C]),
                        ("d_ltm", [128, 16 * 512]), ("d_x2", [T, C]),
                        ("d_g", [128, min(8, NTT) * INNER]), ("d_ptm", [128, 16 * 256])]:
            dbg_t[nm] = nc.dram_tensor(nm, shp, F32, kind="ExternalOutput").ap()

    P = Planner()

    with ExitStack() as es:
        def sb(name, shape, dtype=F32):
            return es.enter_context(nc.sbuf_tensor(name, shape, dtype))

        def psum(name, shape, dtype=F32):
            return es.enter_context(nc.psum_tensor(name, shape, dtype))

        def sem(name):
            return P.reg(es.enter_context(nc.semaphore(name)))

        df = sb("df", [128, max(4 * DF1, 8 * P3S)], BF16)
        wvT = sb("wvT", [128, 8 * C], BF16)
        cpT = sb("cpT", [128, 8 * C], BF16)
        wdT = sb("wdT", [128, 2048], BF16)
        big = sb("big", [128, 32768], BF16)
        f32a = sb("f32a", [128, 3088], F32)
        i16a = sb("i16a", [128, 3072], I16)
        fp8a = sb("fp8a", [128, 8 * INNER + 16384], FP8)
        junk = sb("junk", [128, 1024], BF16)
        identb = sb("identb_t", [128, 128], BF16)
        dfwb = sb("dfwb", [128, 256], BF16)
        logit_w = sb("logit_w", [128, 8], BF16)
        c_all = sb("c_all", [128, WIN])
        maskneg = sb("maskneg_t", [128, WIN], BF16)
        dbcol = sb("dbcol", [128, 1])
        stats = sb("stats", [128, 128])
        scal = sb("scal", [128, 16])
        onesb = sb("onesb", [1, 128], BF16)
        ones32 = sb("ones32", [1, 128], F32)
        onescol = sb("onescol", [128, 1], F32)
        epscol = sb("epscol", [128, 1], F32)
        absac = sb("absac", [128, 32])
        ternb = sb("ternb", [128, 2048], BF16)

        w32 = f32a

        psA = [psum(f"psA{i}", [128, 512], F32) for i in range(4)]
        psB = [psum(f"psB{i}", [128, 1024], BF16) for i in range(2)]
        psC = [psum(f"psC{i}", [128, 512], F32) for i in range(2)]

        ev = sem("ev")
        ea = sem("ea")
        ep = sem("ep")
        eq = sem("eq")
        ALLS = [ev, ea, ep]
        _streams = {}

        def dsem(name):
            if name not in _streams:
                s = sem("q_" + name)
                _streams[name] = s
                ALLS.append(s)
            return _streams[name]

        def fence():
            for e in P.engs:
                for s in ALLS:
                    P.wait(e, s)

        def ap_rows(a, r0, r1):
            return bass.AP(tensor=a.tensor, offset=a.offset + a.ap[0][0] * r0,
                           ap=[[a.ap[0][0], r1 - r0]] + [list(d) for d in a.ap[1:]])

        def splits(n):
            return [(128 * i // n, 128 * (i + 1) // n) for i in range(n)]

        B_h1buf, B_h1fm, B_disp, B_vstg = 0, 1024, 2048, 3072
        B_args, B_gargs, B_ptm = 0, 8192, 14336
        B_ebuf, B_btb, B_btb2 = 14848, 15360, 19456
        B_bb, B_attn, B_x2b, B_vring = 23552, 25600, 27648, 29696
        B_blob = B_x2b
        F_xb, F_sc = 0, 2048

        P.dma("sp", lambda e: e.dma_start(out=identb[:], in_=identb_in), dsem("c"))
        P.dma("sp", lambda e: e.dma_start(out=maskneg[:], in_=maskneg_in), dsem("c"))
        P.dma("sp", lambda e: e.dma_start(out=c_all[:], in_=csfb_in), dsem("c"))
        P.dma("sp", lambda e: e.dma_start(out=stats[:, 0:1], in_=dbrep_in), dsem("c"))
        P.dma("sp", lambda e: e.dma_start(out=f32a[:, 0:8], in_=logitw_in), dsem("c"))
        P.dma("sp", lambda e: e.dma_start(out=f32a[:, 8:264], in_=dfw_in), dsem("c"))
        P.dma("sp", lambda e: e.dma_start(out=f32a[0:WIN, 264:280], in_=relpos), dsem("c"))
        P.dma("sp", lambda e: e.dma_start(out=f32a[0:BD, 280:296], in_=pposw), dsem("c"))
        t_cl = P.n(dsem("c"))

        P.wait("dv", dsem("c"), t_cl)
        P.op("dv", lambda e: e.tensor_copy(out=logit_w[:], in_=f32a[:, 0:8]), ev)
        P.op("dv", lambda e: e.tensor_copy(out=dfwb[:], in_=f32a[:, 8:264]), ev)
        P.op("dv", lambda e: e.tensor_copy(out=ternb[0:WIN, 0:BD], in_=f32a[0:WIN, 264:280]), ev)
        P.op("dv", lambda e: e.tensor_copy(out=ternb[0:BD, 16:32], in_=f32a[0:BD, 280:296]), ev)
        P.op("dv", lambda e: e.memset(onesb[:], 1.0), ev)
        P.op("dv", lambda e: e.memset(ones32[:], 1.0), ev)
        P.op("dv", lambda e: e.memset(onescol[:], 1.0), ev)
        P.op("dv", lambda e: e.memset(epscol[:], float(EPS)), ev)
        P.op("dv", lambda e: e.memset(df[:], 0.0), ev)
        P.op("dv", lambda e: e.tensor_scalar(out=dbcol[:], in0=stats[:, 0:1],
                                             scalar1=0.5, scalar2=None, op0=ALU.mult), ev)
        t_cv = P.n(ev)

        for _pi in range(4):
            P.op("dv", lambda e, _pi=_pi: e.memset(psA[_pi][:], 0.0), ev)
        P.op("dv", lambda e: e.memset(psC[0][:], 0.0), ev)
        P.op("dv", lambda e: e.memset(psC[1][:], 0.0), ev)
        P.wait("pe", ev, t_cv)
        P.op("pe", lambda e: e.transpose(out=psB[0][0:BD, 0:WIN], in_=ternb[0:WIN, 0:BD],
                                         identity=identb[0:WIN, 0:WIN]), ep)
        P.op("pe", lambda e: e.transpose(out=psB[0][0:BD, 64:80], in_=ternb[0:BD, 16:32],
                                         identity=identb[0:BD, 0:BD]), ep)
        t_ct = P.n(ep)
        P.wait("dv", ep, t_ct)
        P.op("dv", lambda e: e.tensor_copy(out=ternb[0:BD, 32:96], in_=psB[0][0:BD, 0:WIN]), ev)
        P.op("dv", lambda e: e.tensor_copy(out=ternb[0:BD, 96:112], in_=psB[0][0:BD, 64:80]), ev)
        t_ce = P.n(ev)
        P.wait("pe", ev, t_ce)
        P.op("pe", lambda e: e.matmul(out=psA[0][0:BD, 0:WIN], lhsT=ternb[0:BD, 96:112],
                                      rhs=ternb[0:BD, 32:96], start=True, stop=True), ep)
        t_cp = P.n(ep)
        P.wait("dv", ep, t_cp)
        for hl in range(4):
            P.op("dv", lambda e, hl=hl: e.tensor_tensor(
                out=c_all[32 * hl : 32 * hl + BD, :], in0=c_all[32 * hl : 32 * hl + BD, :],
                in1=psA[0][0:BD, 0:WIN], op=ALU.add), ev)
        P.op("dv", lambda e: e.memset(big[:, B_x2b : B_x2b + 2048], 0.0), ev)
        P.op("dv", lambda e: e.memset(ternb[:], 0.0), ev)
        fence()

        tok_w32 = [0, 0]
        tok_tern = [0, 0]
        tok_psB = [0, 0]

        def prep_attn_w(src_fn, n_tiles, dest_tensor, dest_stride):
            for r in range(n_tiles):
                sl = r % 2
                P.wait("sp", ev, tok_w32[sl])
                for r0, r1 in splits(4):
                    t_ld = P.dma("sp", lambda e, r=r, sl=sl, src_fn=src_fn, r0=r0, r1=r1: e.dma_start(
                        out=w32[r0:r1, 1024 * sl : 1024 * sl + 1024],
                        in_=ap_rows(src_fn(r), r0, r1)), dsem(f"w{sl}"))
                P.wait("dv", dsem(f"w{sl}"), t_ld)
                P.wait("dv", ep, tok_tern[sl])
                t_cast = P.op("dv", lambda e, sl=sl: e.tensor_copy(
                    out=ternb[:, 1024 * sl : 1024 * sl + 1024],
                    in_=w32[:, 1024 * sl : 1024 * sl + 1024]), ev)
                tok_w32[sl] = t_cast
                P.wait("pe", ev, t_cast)
                P.wait("pe", ev, tok_psB[sl])
                t_tr = None
                for fc in range(8):
                    t_tr = P.op("pe", lambda e, fc=fc, sl=sl: e.transpose(
                        out=psB[sl][:, 128 * fc : 128 * fc + 128],
                        in_=ternb[:, 1024 * sl + 128 * fc : 1024 * sl + 128 * fc + 128],
                        identity=identb[:]), ep)
                tok_tern[sl] = t_tr
                P.wait("dv", ep, t_tr)
                t_ev = P.op("dv", lambda e, r=r, sl=sl, d=dest_tensor, ds=dest_stride: e.tensor_copy(
                    out=bass.AP(tensor=d[:].tensor, offset=128 * r,
                                ap=[list(d[:].ap[0]), [ds, 8], [1, 128]]),
                    in_=psB[sl][:].rearrange("p (a b) -> p a b", a=8)), ev)
                tok_psB[sl] = t_ev

        if stop_after != "consts":
            prep_attn_w(lambda r: wval[128 * r : 128 * r + 128, :], 8, wvT, C)
            prep_attn_w(lambda r: cprojw[128 * r : 128 * r + 128, :], 8, cpT, C)
            prep_attn_w(lambda r: wdisp[128 * r : 128 * r + 128, :], 2, wdT, 256)
        fence()
        tok_w32 = [0, 0]
        tok_psB = [0, 0]

        nzc = (NH * T * PAD) // (128 * 2048)
        t_zwp = 0
        for z in range(nzc):
            t_zwp = P.dma("sp", lambda e, z=z: e.dma_start(
                out=bass.AP(tensor=wp, offset=z * 128 * 2048, ap=[[2048, 128], [1, 2048]]),
                in_=df[:, 0:2048]), dsem("z"))

        def w_tile_ap(mat, ti):
            if mat == 0:
                return wgate[128 * ti : 128 * ti + 128, :]
            if mat == 1:
                return wup[128 * ti : 128 * ti + 128, :]
            r, q = divmod(ti, 4)
            return wdown[128 * r : 128 * r + 128, 1024 * q : 1024 * q + 1024]

        def w8_dst(mat, ti):
            if mat < 2:
                t8 = wg8 if mat == 0 else wu8
                return bass.AP(tensor=t8, offset=128 * ti,
                               ap=[[INNER, 128], [128 * INNER, 8], [1, 128]])
            r, q = divmod(ti, 4)
            return bass.AP(tensor=wd8, offset=(8 * q) * (128 * C) + 128 * r,
                           ap=[[C, 128], [128 * C, 8], [1, 128]])

        wb_on = stop_after != "consts" and stop_after != "wa"
        n_wbc = 192 if wb_on else 0
        t_red = 0
        t_scal = 0

        def wbc_chunk(k):
            nonlocal t_red, t_scal
            mat, ti, half = wcc_idx(k)
            csl = k % 2
            t_my = tld_wcc[csl]
            P.wait("ac", dsem(f"w{csl}"), t_my)
            if 2 * ti + half < 4:
                P.wait("ac", ev, t_red)
            t_abs = P.op("ac", lambda e, ti=ti, half=half, csl=csl: e.activation(
                out=junk[:, 0:512],
                in_=f32a[:, F_sc + 512 * csl : F_sc + 512 * csl + 512],
                func=AF.Abs, accum_out=stats[:, 64 + 2 * ti + half : 65 + 2 * ti + half]), ea)
            tok_w32c[csl] = t_abs
            if k + 2 < n_wbc:
                wcc_load(k + 2)
            elif k + 2 < n_wbc + 2 and n_wcc:
                wcc_load(k + 2 - n_wbc)
            if ti == 31 and half == 1:
                P.wait("dv", ea, t_abs)
                t_red = P.op("dv", lambda e: e.tensor_reduce(
                    out=stats[:, 30:31], in_=stats[:, 64:128], axis=AX.X, op=ALU.add), ev)
                P.wait("pe", ev, t_red)
                t_mm = P.op("pe", lambda e, mat=mat: e.matmul(
                    out=psC[0][0:1, mat : mat + 1], lhsT=onescol[:],
                    rhs=stats[:, 30:31], start=True, stop=True), ep)
                P.wait("dv", ep, t_mm)
                P.op("dv", lambda e, mat=mat: e.tensor_scalar(
                    out=stats[0:1, 60 + mat : 61 + mat], in0=psC[0][0:1, mat : mat + 1],
                    scalar1=1.0 / (INNER * C), scalar2=1e-5, op0=ALU.mult, op1=ALU.max), ev)
                if mat == 2:
                    t_mus = P.n(ev)
                    P.wait("pe", ev, t_mus)
                    t_mm2 = P.op("pe", lambda e: e.matmul(out=psC[1][0:128, 0:3], lhsT=ones32[0:1, :],
                                                          rhs=stats[0:1, 60:63],
                                                          start=True, stop=True), ep)
                    P.wait("dv", ep, t_mm2)
                    P.op("dv", lambda e: e.tensor_copy(out=scal[:, 6:9], in_=psC[1][0:128, 0:3]), ev)
                    P.op("dv", lambda e: e.tensor_scalar(out=scal[:, 0:3], in0=psC[1][0:128, 0:3],
                                                         scalar1=0.5, scalar2=None, op0=ALU.mult), ev)
                    P.op("dv", lambda e: e.tensor_scalar(out=scal[:, 3:6], in0=psC[1][0:128, 0:3],
                                                         scalar1=-0.5, scalar2=None, op0=ALU.mult), ev)
                    t_scal = P.n(ev)

        wc_on = stop_after not in ("consts", "wa", "wb")
        n_wcc = 192 if wc_on else 0
        tok_w32c = [0, 0]
        tok_terc = [0, 0]
        tok_psB1 = [0, 0]
        tok_stgc = [0, 0]
        tld_wcc = [0, 0]

        def ap_mid(a, i0, n):
            return bass.AP(tensor=a.tensor, offset=a.offset + a.ap[1][0] * i0,
                           ap=[list(a.ap[0]), [a.ap[1][0], n]] + [list(d) for d in a.ap[2:]])

        def wcc_idx(k):
            mat = k // 64
            ti = (k % 64) // 2
            half = k % 2
            return mat, ti, half

        def wcc_load(k):
            mat, ti, half = wcc_idx(k)
            csl = k % 2
            P.wait("sp", ea, tok_w32c[csl])
            for r0, r1 in splits(2):
                tld_wcc[csl] = P.dma("sp", lambda e, mat=mat, ti=ti, half=half, csl=csl, r0=r0, r1=r1: e.dma_start(
                    out=f32a[r0:r1, F_sc + 512 * csl : F_sc + 512 * csl + 512],
                    in_=ap_cols(ap_rows(w_tile_ap(mat, ti), r0, r1), half)), dsem(f"w{csl}"))

        def ap_cols(a, half):
            return bass.AP(tensor=a.tensor, offset=a.offset + 512 * half,
                           ap=[list(a.ap[0]), [1, 512]])

        def wcc_chunk(k):
            mat, ti, half = wcc_idx(k)
            csl = k % 2
            t_my_ld = tld_wcc[csl]
            P.wait("ac", dsem(f"w{csl}"), t_my_ld)
            if k < 2 and n_wbc:
                P.wait("ac", ev, t_scal)
            P.wait("ac", ep, tok_terc[csl])
            P.op("ac", lambda e, csl=csl, mat=mat: e.activation(
                out=ternb[:, 512 * csl : 512 * csl + 512],
                in_=f32a[:, F_sc + 512 * csl : F_sc + 512 * csl + 512],
                func=AF.Sign, bias=scal[:, 3 + mat : 4 + mat]), ea)
            t_b = P.op("ac", lambda e, csl=csl, mat=mat: e.activation(
                out=big[:, 5120 + 512 * csl : 5120 + 512 * csl + 512],
                in_=f32a[:, F_sc + 512 * csl : F_sc + 512 * csl + 512],
                func=AF.Sign, bias=scal[:, mat : mat + 1]), ea)
            tok_w32c[csl] = t_b
            if k + 2 < n_wcc:
                wcc_load(k + 2)
            P.wait("dv", ea, t_b)
            t_t = P.op("dv", lambda e, csl=csl: e.tensor_tensor(
                out=ternb[:, 512 * csl : 512 * csl + 512],
                in0=ternb[:, 512 * csl : 512 * csl + 512],
                in1=big[:, 5120 + 512 * csl : 5120 + 512 * csl + 512], op=ALU.add), ev)
            P.wait("pe", ev, t_t)
            P.wait("pe", ev, tok_psB1[csl])
            t_tr = None
            for fc2 in range(4):
                t_tr = P.op("pe", lambda e, csl=csl, fc2=fc2: e.transpose(
                    out=psB[1][:, 512 * csl + 128 * fc2 : 512 * csl + 128 * fc2 + 128],
                    in_=ternb[:, 512 * csl + 128 * fc2 : 512 * csl + 128 * fc2 + 128],
                    identity=identb[:]), ep)
            tok_terc[csl] = t_tr
            P.wait("dv", ep, t_tr)
            P.wait("dv", dsem(f"st{csl}"), tok_stgc[csl])
            t_ev = P.op("dv", lambda e, csl=csl: e.tensor_copy(
                out=fp8a[:, 32768 + 512 * csl : 32768 + 512 * csl + 512],
                in_=psB[1][:, 512 * csl : 512 * csl + 512]), ev)
            tok_psB1[csl] = t_ev
            P.wait("sp", ev, t_ev)
            for r0, r1 in splits(2):
                tok_stgc[csl] = P.dma("sp", lambda e, mat=mat, ti=ti, half=half, csl=csl, r0=r0, r1=r1: e.dma_start(
                    out=ap_rows(ap_mid(w8_dst(mat, ti), 4 * half, 4), r0, r1),
                    in_=fp8a[r0:r1, 32768 + 512 * csl : 32768 + 512 * csl + 512]), dsem(f"st{csl}"))

        fence()
        tok_w32 = [0, 0]
        tok_psB = [0, 0]

        tok_xb = [0, 0]
        tok_h1b = 0
        tok_h1fm = 0
        tok_psA01 = 0
        tok_psA2 = 0
        tok_psA3 = [0, 0]
        tok_dispb = 0
        tok_vstg = [0, 0]
        n_p1 = NT if stop_after not in ("consts", "wa", "wb", "wc") else 0
        tld_x = [0, 0]

        def p1_load(g):
            sl = g % 2
            P.wait("sp", ev, tok_xb[sl])
            for r0, r1 in splits(4):
                tld_x[sl] = P.dma("sp", lambda e, g=g, sl=sl, r0=r0, r1=r1: e.dma_start(
                    out=f32a[r0:r1, F_xb + 1024 * sl : F_xb + 1024 * sl + 1024],
                    in_=x_in[128 * g + r0 : 128 * g + r1, :]), dsem(f"xb{sl}"))

        t_rs_g = [0, 0]

        def p1_stats(g):
            sl = g % 2
            sp2 = 2 + 8 * (g % 2)
            t_x = tld_x[sl]

            def xt_ap(sl=sl):
                return f32a[:, F_xb + 1024 * sl : F_xb + 1024 * sl + 1024]

            P.wait("dv", dsem(f"xb{sl}"), t_x)
            P.op("dv", lambda e, sl=sl, sp2=sp2: e.tensor_reduce(
                out=stats[:, sp2 : sp2 + 1], in_=xt_ap(sl), axis=AX.X, op=ALU.add), ev)
            P.wait("ac", dsem(f"xb{sl}"), t_x)
            t_sq = P.op("ac", lambda e, sl=sl, sp2=sp2: e.activation(
                out=junk[:], in_=xt_ap(sl), func=AF.Square, accum_out=stats[:, sp2 + 1 : sp2 + 2]), ea)
            P.op("dv", lambda e, sp2=sp2: e.tensor_scalar(out=stats[:, sp2 + 2 : sp2 + 3], in0=stats[:, sp2 : sp2 + 1],
                                                 scalar1=-1.0 / C, scalar2=None, op0=ALU.mult), ev)
            P.wait("dv", ea, t_sq)
            P.op("dv", lambda e, sp2=sp2: e.tensor_scalar(out=stats[:, sp2 + 3 : sp2 + 4], in0=stats[:, sp2 + 1 : sp2 + 2],
                                                 scalar1=1.0 / C, scalar2=None, op0=ALU.mult), ev)
            P.op("dv", lambda e, sp2=sp2: e.tensor_tensor(out=stats[:, sp2 + 4 : sp2 + 5], in0=stats[:, sp2 + 2 : sp2 + 3],
                                                 in1=stats[:, sp2 + 2 : sp2 + 3], op=ALU.mult), ev)
            t_var = P.op("dv", lambda e, sp2=sp2: e.tensor_tensor(out=stats[:, sp2 + 5 : sp2 + 6], in0=stats[:, sp2 + 3 : sp2 + 4],
                                                         in1=stats[:, sp2 + 4 : sp2 + 5], op=ALU.subtract), ev)
            P.wait("ac", ev, t_var)
            P.op("ac", lambda e, sp2=sp2: e.activation(out=stats[:, sp2 + 6 : sp2 + 7], in_=stats[:, sp2 + 5 : sp2 + 6],
                                              func=AF.Ln, bias=epscol[:, 0:1]), ea)
            t_rs_g[g % 2] = P.op("ac", lambda e, sp2=sp2: e.activation(
                out=stats[:, sp2 + 7 : sp2 + 8], in_=stats[:, sp2 + 6 : sp2 + 7],
                func=AF.Exp, scale=-0.5), ea)

        n_units = n_wbc + n_wcc

        def emit_unit(u):
            if u < n_wbc:
                wbc_chunk(u)
            else:
                wcc_chunk(u - n_wbc)

        if n_p1:
            p1_load(0)
        if n_wbc:
            wcc_load(0)
            wcc_load(1)
        elif n_wcc:
            wcc_load(0)
            wcc_load(1)
        if n_p1:
            p1_stats(0)
        for g in range(n_p1):
            sl = g % 2
            sp2 = 2 + 8 * (g % 2)
            if g + 1 < n_p1:
                p1_load(g + 1)
                p1_stats(g + 1)
            for _j in range(12):
                if 12 * g + _j < n_units:
                    emit_unit(12 * g + _j)

            def xt_ap(sl=sl):
                return f32a[:, F_xb + 1024 * sl : F_xb + 1024 * sl + 1024]

            t_rs = t_rs_g[g % 2]
            if p1_lvl < 2:
                continue
            P.wait("dv", ea, t_rs)
            P.wait("dv", ep, tok_h1b)
            t_h1 = P.op("dv", lambda e, sl=sl, sp2=sp2: e.tensor_scalar(
                out=big[:, B_h1buf : B_h1buf + 1024], in0=xt_ap(sl),
                scalar1=stats[:, sp2 + 2 : sp2 + 3], scalar2=stats[:, sp2 + 7 : sp2 + 8],
                op0=ALU.add, op1=ALU.mult), ev)
            tok_xb[sl] = t_h1
            P.wait("pe", ev, t_h1)
            P.wait("pe", ev, tok_psB[0])
            t_tr = None
            for fc in range(8):
                t_tr = P.op("pe", lambda e, fc=fc: e.transpose(
                    out=psB[0][:, 128 * fc : 128 * fc + 128],
                    in_=big[:, B_h1buf + 128 * fc : B_h1buf + 128 * fc + 128],
                    identity=identb[:]), ep)
            tok_h1b = t_tr
            P.wait("dv", ep, t_tr)
            P.wait("dv", ep, tok_h1fm)
            t_fm = P.op("dv", lambda e: e.tensor_copy(
                out=big[:, B_h1fm : B_h1fm + 1024], in_=psB[0][:]), ev)
            tok_psB[0] = t_fm
            if p1_lvl < 3:
                continue
            P.wait("pe", ev, t_fm)
            P.wait("pe", ev, tok_psA01)
            for nck in range(2):
                for fc in range(8):
                    P.op("pe", lambda e, fc=fc, nck=nck: e.matmul(
                        out=psA[nck][:, :],
                        lhsT=big[:, B_h1fm + 128 * fc : B_h1fm + 128 * fc + 128],
                        rhs=wvT[:, C * fc + 512 * nck : C * fc + 512 * nck + 512],
                        start=(fc == 0), stop=(fc == 7)), ep)
            t_vmm = P.n(ep)
            if p1_lvl < 4:
                continue
            P.wait("pe", ev, tok_psA2)
            for oc in range(2):
                for fc in range(8):
                    P.op("pe", lambda e, oc=oc, fc=fc: e.matmul(
                        out=psA[2][:, 128 * oc : 128 * oc + 128],
                        lhsT=wdT[:, 256 * fc + 128 * oc : 256 * fc + 128 * oc + 128],
                        rhs=big[:, B_h1fm + 128 * fc : B_h1fm + 128 * fc + 128],
                        start=(fc == 0), stop=(fc == 7)), ep)
            t_dmm = P.n(ep)
            tok_h1fm = t_dmm
            P.wait("dv", ep, t_vmm)
            P.wait("dv", dsem(f"vs{sl}"), tok_vstg[sl])
            P.op("dv", lambda e, sl=sl: e.tensor_copy(
                out=big[:, B_vstg + 1024 * sl : B_vstg + 1024 * sl + 512], in_=psA[0][:]), ev)
            t_vst = P.op("dv", lambda e, sl=sl: e.tensor_copy(
                out=big[:, B_vstg + 1024 * sl + 512 : B_vstg + 1024 * sl + 1024],
                in_=psA[1][:]), ev)
            tok_psA01 = t_vst
            P.wait("sp", ev, t_vst)
            t_vd = P.dma("sp", lambda e, g=g, sl=sl: e.dma_start(
                out=bass.AP(tensor=vals, offset=g * 128 * C, ap=[[C, 128], [1, C]]),
                in_=big[:, B_vstg + 1024 * sl : B_vstg + 1024 * sl + 1024]), dsem(f"vs{sl}"))
            if dbg:
                P.wait("pl", ev, t_vst)
                t_dbg = P.dma("pl", lambda e, g=g, sl=sl: e.dma_start(
                    out=dbg_t["d_val"][128 * g : 128 * g + 128, :],
                    in_=big[:, B_vstg + 1024 * sl : B_vstg + 1024 * sl + 1024]), dsem("dbg"))
                P.wait("dv", dsem("dbg"), t_dbg)
            tok_vstg[sl] = t_vd
            if p1_lvl < 5:
                continue
            P.wait("dv", ep, t_dmm)
            P.wait("dv", ep, tok_dispb)
            t_de = P.op("dv", lambda e: e.tensor_copy(
                out=big[:, B_disp : B_disp + 256], in_=psA[2][:, 0:256]), ev)
            tok_psA2 = t_de
            t_mmdf_last = 0
            P.wait("pe", ev, t_de)
            P.wait("pe", ev, tok_psA3[0])
            for hg in range(4 if p1_lvl >= 6 else 0):
                oc, hh = hg // 2, hg % 2
                t_mmdf_last = P.op("pe", lambda e, oc=oc, hh=hh, hg=hg: e.matmul(
                    out=psA[3][:, 128 * hg : 128 * hg + 128],
                    lhsT=dfwb[:, 128 * hh : 128 * hh + 128],
                    rhs=big[:, B_disp + 128 * oc : B_disp + 128 * oc + 128],
                    start=True, stop=True), ep)
            if p1_lvl >= 7:
                P.wait("dv", ep, t_mmdf_last)
                if g == 0:
                    P.wait("dv", dsem("z"), t_zwp)
                t_dfe = P.op("dv", lambda e, g=g: e.tensor_copy(
                    out=bass.AP(tensor=df[:].tensor, offset=64 + 128 * g,
                                ap=[list(df[:].ap[0]), [DF1, 4], [1, 128]]),
                    in_=psA[3][:]), ev)
                tok_psA3[0] = t_dfe
            if p1_lvl >= 6:
                tok_dispb = t_mmdf_last
        for _k in range(12 * n_p1, n_units):
            emit_unit(_k)
        if dbg:
            P.wait("pl", ev)
            P.dma("pl", lambda e: e.dma_start(out=dbg_t["d_df"], in_=df[:, 0 : 4 * DF1]), dsem("dbg"))
        fence()

        tok_args = [0, 0, 0, 0]
        tok_gargs = [0, 0, 0]
        tok_psA_q = [0, 0, 0, 0]
        tok_psA_j0 = [0, 0, 0, 0]
        tok_blob = [0, 0]
        tok_ltm_w = 0
        tok_ltm_r = 0
        tok_ebuf = [0, 0]
        tok_ptm = [0, 0]
        tok_btb = [0, 0]
        tok_bb = [0, 0]
        tok_attn = [0, 0]
        tok_psC_ev = 0
        tok_psC_ac = 0
        tok_xres = [0, 0]
        tok_x2b = [0, 0]
        tok_vring = [0, 0, 0]

        if stop_after == "p2w":
            NSUBX = 0
        _stopped2 = stop_after in ("consts", "wa", "wb", "wc", "p1")
        for S in range(NSUP2 if not _stopped2 else 0):
            T0 = S * P2S
            for hg in range(4):
                dfo = DF1 * hg
                t_diff = [0, 0, 0, 0]
                t_eb_prev = 0

                def blob_ap(q2, r0, r1, c0, c1):
                    if q2 % 2 == 0:
                        return big[r0:r1, B_blob + c0 : B_blob + c1]
                    return ternb[r0:r1, c0:c1]

                def wl_tail(q2, t_eb2):
                    nonlocal tok_ltm_r
                    P.wait("pe", ev, t_eb2)
                    P.wait("pe", ev, tok_psB[0])
                    P.wait("pe", ev, tok_psB[1])
                    t_tr = None
                    for tc in range(NQC):
                        for i in range(4):
                            t_tr = P.op("pe", lambda e, tc=tc, i=i, q2=q2: e.transpose(
                                out=psB[tc % 2][:, 512 * (tc // 2) + 128 * i : 512 * (tc // 2) + 128 * i + 128],
                                in_=blob_ap(q2, 0, 128, 512 * tc + 128 * i, 512 * tc + 128 * i + 128),
                                identity=identb[:]), ep)
                    tok_blob[q2 % 2] = t_tr
                    P.wait("dv", ep, t_tr)
                    if q2 == 0:
                        P.wait("dv", ea, tok_ltm_r)
                    t_cp2 = None
                    for tc in range(NQC):
                        t_cp2 = P.op("dv", lambda e, tc=tc, q2=q2: e.tensor_copy(
                            out=bass.AP(tensor=wvT[:].tensor,
                                        offset=512 * (tc * 4) + 32 * q2,
                                        ap=[list(wvT[:].ap[0]), [512, 4], [8, 4], [1, 8]]),
                            in_=bass.AP(tensor=psB[tc % 2][:].tensor,
                                        offset=512 * (tc // 2),
                                        ap=[list(psB[tc % 2][:].ap[0]), [128, 4], [32, 4], [1, 8]])), ev)
                    tok_psB[0] = t_cp2
                    tok_psB[1] = t_cp2

                def w_diff2(q2, h2, dfo=dfo, T0=T0):
                    for j2 in (0, 1) if h2 == 0 else (2, 3):
                        w2 = 4 * q2 + j2
                        if j2 == 3:
                            P.wait("pl", ea, tok_args[3])
                            P.wait("pl", eq, t_diff[3])
                            t_diff[3] = P.op("pl", lambda e, w2=w2, dfo=dfo, T0=T0: e.tensor_tensor(
                                out=big[:, B_args + 2048 * 3 : B_args + 2048 * 3 + P2S],
                                in0=df[:, dfo + 1 + T0 + w2 : dfo + 1 + T0 + w2 + P2S],
                                in1=df[:, dfo + 64 + T0 : dfo + 64 + T0 + P2S],
                                op=ALU.subtract), eq)
                            continue
                        P.wait("dv", ea, tok_args[j2])
                        t_diff[j2] = P.op("dv", lambda e, w2=w2, j2=j2, dfo=dfo, T0=T0: e.tensor_tensor(
                            out=big[:, B_args + 2048 * j2 : B_args + 2048 * j2 + P2S],
                            in0=df[:, dfo + 1 + T0 + w2 : dfo + 1 + T0 + w2 + P2S],
                            in1=df[:, dfo + 64 + T0 : dfo + 64 + T0 + P2S],
                            op=ALU.subtract), ev)

                def w_diff(q2, dfo=dfo, T0=T0):
                    for j2 in range(4):
                        w2 = 4 * q2 + j2
                        P.wait("dv", ea, tok_args[j2])
                        t_diff[j2] = P.op("dv", lambda e, w2=w2, j2=j2, dfo=dfo, T0=T0: e.tensor_tensor(
                            out=big[:, B_args + 2048 * j2 : B_args + 2048 * j2 + P2S],
                            in0=df[:, dfo + 1 + T0 + w2 : dfo + 1 + T0 + w2 + P2S],
                            in1=df[:, dfo + 64 + T0 : dfo + 64 + T0 + P2S],
                            op=ALU.subtract), ev)

                if p2_lvl >= 0:
                    w_diff2(0, 0)
                    w_diff2(0, 1)
                for q in range(16 if p2_lvl >= 1 else 0):
                    t_lm = None
                    for j in range(4):
                        w = 4 * q + j
                        gsl = w % 3
                        P.wait("ac", eq if j == 3 else ev, t_diff[j])
                        P.wait("ac", ep, tok_gargs[gsl])
                        t_gel = P.op("ac", lambda e, w=w, j=j, gsl=gsl: e.activation(
                            out=big[:, B_gargs + 2048 * gsl : B_gargs + 2048 * gsl + P2S],
                            in_=big[:, B_args + 2048 * j : B_args + 2048 * j + P2S],
                            func=AF.Gelu, bias=c_all[:, w : w + 1]), ea)
                        tok_args[j] = t_gel
                        if p2_lvl < 2:
                            tok_gargs[gsl] = 0
                            continue
                        P.wait("pe", ea, t_gel)
                        for tc in range(NQC):
                            if j == 0:
                                P.wait("pe", ev, tok_psA_j0[tc])
                            elif j == 1:
                                P.wait("pe", ev, tok_psA_q[tc])
                            t_lm = P.op("pe", lambda e, gsl=gsl, tc=tc, j=j: e.matmul(
                                out=psA[tc][32 * j : 32 * j + 8, :],
                                lhsT=logit_w[:, :],
                                rhs=big[:, B_gargs + 2048 * gsl + 512 * tc : B_gargs + 2048 * gsl + 512 * tc + 512],
                                start=True, stop=True, tile_position=(0, 32 * j)), ep)
                        tok_gargs[gsl] = t_lm
                    if q + 1 < 16:
                        w_diff2(q + 1, 0)
                    if p2_lvl >= 3:
                        P.wait("dv", ep, t_lm)
                        P.wait("dv", ep, tok_blob[q % 2])
                        if hg == 0 and q == 0:
                            P.wait("dv", dsem("x20"), tok_x2b[0])
                            P.wait("dv", dsem("x21"), tok_x2b[1])
                        t_eb = None
                        for tc in range(NQC):
                            t_eb = P.op("dv", lambda e, tc=tc, q=q: e.tensor_copy(
                                out=blob_ap(q, 0, 128, 512 * tc, 512 * tc + 512),
                                in_=psA[tc][:]), ev)
                        for tc in range(NQC):
                            tok_psA_j0[tc] = t_eb
                            tok_psA_q[tc] = t_eb
                        if q + 1 < 16:
                            w_diff2(q + 1, 1)
                        if q > 0:
                            wl_tail(q - 1, t_eb_prev)
                        t_eb_prev = t_eb
                    elif q + 1 < 16:
                        w_diff2(q + 1, 1)
                tok_ltm_w_pending = True
                if p2_lvl >= 3:
                    wl_tail(15, t_eb_prev)
                tok_ltm_w = P.n(ev)
                if dbg and S == 0 and hg == 0:
                    P.wait("pl", ev, tok_ltm_w)
                    t_dl = P.dma("pl", lambda e: e.dma_start(out=dbg_t["d_ltm"], in_=wvT[:, 0 : 16 * 512]), dsem("dbg"))
                    P.wait("dv", dsem("dbg"), t_dl)
                for s_sub in range(NSUB if stop_after != "p2w" else 0):
                    sl = s_sub % 2
                    lt_off = 512 * s_sub
                    P.wait("ac", ev, tok_ltm_w)
                    t_sg = P.op("ac", lambda e, lt_off=lt_off: e.activation(
                        out=f32a[:, F_sc + 4 : F_sc + 260],
                        in_=bass.AP(tensor=wvT[:].tensor, offset=lt_off + 4,
                                    ap=[list(wvT[:].ap[0]), [8, WIN], [1, 4]]),
                        func=AF.Tanh, scale=0.5, bias=dbcol[:]), ea)
                    tok_ltm_r = t_sg
                    P.wait("dv", ea, t_sg)
                    t_lg = P.op("dv", lambda e, lt_off=lt_off: e.scalar_tensor_tensor(
                        out=f32a[:, F_sc + 260 : F_sc + 516],
                        in0=f32a[:, F_sc + 4 : F_sc + 260],
                        scalar=-5.0, op0=ALU.mult,
                        in1=bass.AP(tensor=wvT[:].tensor, offset=lt_off,
                                    ap=[list(wvT[:].ap[0]), [8, WIN], [1, 4]]),
                        op1=ALU.add), ev)
                    if S == 0 and s_sub == 0:
                        t_lg = P.op("dv", lambda e: e.tensor_tensor(
                            out=f32a[:, F_sc + 260 : F_sc + 516],
                            in0=f32a[:, F_sc + 260 : F_sc + 516],
                            in1=bass.AP(tensor=maskneg[:].tensor, offset=0,
                                        ap=[list(maskneg[:].ap[0]), [1, WIN], [0, 4]]),
                            op=ALU.add), ev)
                    P.op("dv", lambda e: e.tensor_reduce(
                        out=stats[:, 16:20],
                        in_=bass.AP(tensor=f32a[:].tensor, offset=F_sc + 260,
                                    ap=[list(f32a[:].ap[0]), [1, 4], [4, WIN]]),
                        axis=AX.X, op=ALU.max, negate=True), ev)
                    t_ei = P.op("dv", lambda e: e.tensor_tensor(
                        out=f32a[:, F_sc + 260 : F_sc + 516],
                        in0=f32a[:, F_sc + 260 : F_sc + 516],
                        in1=bass.AP(tensor=stats[:].tensor, offset=16,
                                    ap=[list(stats[:].ap[0]), [0, WIN], [1, 4]]),
                        op=ALU.add), ev)
                    P.wait("ac", ev, t_ei)
                    P.wait("ac", ev, tok_ebuf[sl])
                    t_ex = P.op("ac", lambda e, sl=sl: e.activation(
                        out=big[:, B_ebuf + 256 * sl : B_ebuf + 256 * sl + 256],
                        in_=f32a[:, F_sc + 260 : F_sc + 516], func=AF.Exp), ea)
                    P.wait("dv", ea, t_ex)
                    P.op("dv", lambda e, sl=sl: e.tensor_reduce(
                        out=stats[:, 20:24],
                        in_=bass.AP(tensor=big[:].tensor, offset=B_ebuf + 256 * sl,
                                    ap=[list(big[:].ap[0]), [1, 4], [4, WIN]]),
                        axis=AX.X, op=ALU.add), ev)
                    P.op("dv", lambda e: e.reciprocal(out=stats[:, 24:28], in_=stats[:, 20:24]), ev)
                    P.wait("dv", dsem(f"sk{sl}"), tok_ptm[sl])
                    t_p = P.op("dv", lambda e, sl=sl: e.tensor_tensor(
                        out=bass.AP(tensor=big[:].tensor, offset=B_ptm + 256 * sl,
                                    ap=[list(big[:].ap[0]), [1, WIN], [WIN, 4]]),
                        in0=big[:, B_ebuf + 256 * sl : B_ebuf + 256 * sl + 256],
                        in1=bass.AP(tensor=stats[:].tensor, offset=24,
                                    ap=[list(stats[:].ap[0]), [0, WIN], [1, 4]]),
                        op=ALU.mult), ev)
                    tok_ebuf[sl] = t_p
                    if dbg and S == 0 and hg == 0:
                        P.wait("pl", ev, t_p)
                        t_dbg = P.dma("pl", lambda e, s_sub=s_sub, sl=sl: e.dma_start(
                            out=dbg_t["d_ptm"][:, 256 * s_sub : 256 * s_sub + 256],
                            in_=big[:, B_ptm + 256 * sl : B_ptm + 256 * sl + 256]), dsem("dbg"))
                        P.wait("dv", dsem("dbg"), t_dbg)
                    P.wait("sp", ev, t_p)
                    if S == 0 and hg == 0 and s_sub == sl:
                        P.wait("sp", dsem("z"), t_zwp)
                    for r0, r1 in splits(4):
                        tok_ptm[sl] = P.dma("sp", lambda e, hg=hg, s_sub=s_sub, sl=sl, T0=T0, r0=r0, r1=r1: e.dma_start(
                            out=bass.AP(tensor=wp,
                                        offset=(T0 + 128 * s_sub + r0) * NH * PAD + (hg * 4) * PAD + 65 + r0,
                                        ap=[[NH * PAD + 1, r1 - r0], [PAD, 4], [1, WIN]]),
                            in_=big[r0:r1, B_ptm + 256 * sl : B_ptm + 256 * sl + 256].rearrange(
                                "p (a b) -> p a b", a=4)), dsem(f"sk{sl}"))
            t_skew_all = (P.n(dsem("sk0")), P.n(dsem("sk1")))
            nsub_b = NSUB if stop_after not in ("p2a", "p2w") else 0
            tbt = [0, 0]
            tvl = [0, 0, 0]
            txr = [0, 0]

            def bl_load(bs2, T0=T0):
                gb = (T0 // 128) + bs2
                t0 = 128 * gb
                bb_ = gb % 2
                btbase = B_btb if bb_ == 0 else B_btb2
                P.wait("sp", dsem("sk0"), t_skew_all[0])
                P.wait("sp", dsem("sk1"), t_skew_all[1])
                P.wait("sp", ep, tok_btb[bb_])
                for r0, r1 in splits(2):
                    tbt[bb_] = P.dma("sp", lambda e, t0=t0, r0=r0, r1=r1, btbase=btbase: e.dma_start(
                        out=big[r0:r1, btbase : btbase + 4096],
                        in_=bass.AP(tensor=wp, offset=(t0 + r0) * NH * PAD,
                                    ap=[[NH * PAD, r1 - r0], [1, NH * PAD]])), dsem(f"bt{bb_}"))
                vsl = gb % 3
                P.wait("sp", ep, tok_vring[vsl])
                for r0, r1 in splits(2):
                    tvl[vsl] = P.dma("sp", lambda e, gb=gb, vsl=vsl, r0=r0, r1=r1: e.dma_start(
                        out=big[r0:r1, B_vring + 1024 * vsl : B_vring + 1024 * vsl + 1024],
                        in_=bass.AP(tensor=vals, offset=(gb * 128 + r0) * C,
                                    ap=[[C, r1 - r0], [1, C]])), dsem(f"vr{vsl}"))
                xsl = gb % 2
                P.wait("sp", ev, tok_xres[xsl])
                for r0, r1 in splits(4):
                    txr[xsl] = P.dma("sp", lambda e, t0=t0, xsl=xsl, r0=r0, r1=r1: e.dma_start(
                        out=f32a[r0:r1, F_xb + 1024 * xsl : F_xb + 1024 * xsl + 1024],
                        in_=x_in[t0 + r0 : t0 + r1, :]), dsem(f"xr{xsl}"))

            if nsub_b:
                bl_load(0)
            for bs_ in range(nsub_b):
                gb = (T0 // 128) + bs_
                t0 = 128 * gb
                bb_ = gb % 2
                btbase = B_btb if bb_ == 0 else B_btb2
                if bs_ + 1 < nsub_b:
                    bl_load(bs_ + 1)
                vsl = gb % 3
                xsl = gb % 2
                t_bt = tbt[bb_]
                t_vl = tvl[vsl]
                t_xr = txr[xsl]
                t_band = None
                for g2 in range(2):
                    P.wait("pe", dsem(f"bt{bb_}"), t_bt)
                    P.wait("pe", ev, tok_psB[0])
                    P.wait("pe", ev, tok_psB[1])
                    t_tr = None
                    for k in range(8):
                        h = 8 * g2 + k
                        for half in range(2):
                            t_tr = P.op("pe", lambda e, h=h, k=k, half=half, btbase=btbase: e.transpose(
                                out=psB[k % 2][:, 256 * (k // 2) + 128 * half : 256 * (k // 2) + 128 * half + 128],
                                in_=big[:, btbase + 256 * h + 128 * half : btbase + 256 * h + 128 * half + 128],
                                identity=identb[:]), ep)
                    if g2 == 1:
                        tok_btb[bb_] = t_tr
                    P.wait("dv", ep, t_tr)
                    P.wait("dv", ep, tok_bb[0])
                    P.op("dv", lambda e: e.tensor_copy(
                        out=big[:, B_bb : B_bb + 1024], in_=psB[0][:]), ev)
                    t_bbe = P.op("dv", lambda e: e.tensor_copy(
                        out=big[:, B_bb + 1024 : B_bb + 2048], in_=psB[1][:]), ev)
                    tok_psB[0] = t_bbe
                    tok_psB[1] = t_bbe
                    if p2_lvl < 5:
                        continue
                    P.wait("pe", ev, t_bbe)
                    P.wait("pe", dsem(f"vr{vsl}"), t_vl)
                    if g2 == 0:
                        P.wait("pe", ev, tok_psC_ev)
                        P.wait("pe", ea, tok_psC_ac)
                    for k in range(8):
                        h = 8 * g2 + k
                        bank = psC[g2]
                        r0 = 64 * (h % 2)
                        c0 = 128 * ((h // 2) % 4)
                        boff = B_bb + 1024 * (k % 2) + 256 * (k // 2)
                        if gb == 0:
                            t_band = P.op("pe", lambda e, h=h, bank=bank, r0=r0, c0=c0, vsl=vsl, boff=boff: e.matmul(
                                out=bank[r0 : r0 + 64, c0 : c0 + 128],
                                lhsT=big[:, B_vring + 1024 * vsl + 64 * h : B_vring + 1024 * vsl + 64 * h + 64],
                                rhs=big[:, boff + 128 : boff + 256],
                                start=True, stop=True), ep)
                        else:
                            pv = (gb + 2) % 3
                            P.op("pe", lambda e, h=h, bank=bank, r0=r0, c0=c0, pv=pv, boff=boff: e.matmul(
                                out=bank[r0 : r0 + 64, c0 : c0 + 128],
                                lhsT=big[:, B_vring + 1024 * pv + 64 * h : B_vring + 1024 * pv + 64 * h + 64],
                                rhs=big[:, boff : boff + 128],
                                start=True, stop=False), ep)
                            t_band = P.op("pe", lambda e, h=h, bank=bank, r0=r0, c0=c0, vsl=vsl, boff=boff: e.matmul(
                                out=bank[r0 : r0 + 64, c0 : c0 + 128],
                                lhsT=big[:, B_vring + 1024 * vsl + 64 * h : B_vring + 1024 * vsl + 64 * h + 64],
                                rhs=big[:, boff + 128 : boff + 256],
                                start=False, stop=True), ep)
                    tok_bb[0] = t_band
                tok_vring[vsl] = t_band
                if gb > 0:
                    tok_vring[(gb + 2) % 3] = t_band
                if p2_lvl < 6:
                    continue
                asl = gb % 2
                P.wait("dv", ep, t_band)
                P.wait("dv", ep, tok_attn[asl])
                P.op("dv", lambda e, asl=asl: e.tensor_copy(
                    out=big[:, B_attn + 1024 * asl : B_attn + 1024 * asl + 512], in_=psC[0][:]), ev)
                t_ae2 = P.op("dv", lambda e, asl=asl: e.tensor_copy(
                    out=big[:, B_attn + 1024 * asl + 512 : B_attn + 1024 * asl + 1024], in_=psC[1][:]), ev)
                P.wait("pe", ev, t_ae2)
                for ncx in range(2):
                    for ch in range(8):
                        P.op("pe", lambda e, ncx=ncx, ch=ch, asl=asl: e.matmul(
                            out=psC[ncx][:, :],
                            lhsT=big[:, B_attn + 1024 * asl + 128 * ch : B_attn + 1024 * asl + 128 * ch + 128],
                            rhs=cpT[:, C * ch + 512 * ncx : C * ch + 512 * ncx + 512],
                            start=(ch == 0), stop=(ch == 7)), ep)
                t_cpj = P.n(ep)
                tok_attn[asl] = t_cpj
                if p2_lvl < 7:
                    continue
                P.wait("dv", ep, t_cpj)
                P.op("dv", lambda e: e.tensor_copy(out=f32a[:, F_sc : F_sc + 512], in_=psC[0][:]), ev)
                t_yev = P.op("dv", lambda e: e.tensor_copy(out=f32a[:, F_sc + 512 : F_sc + 1024], in_=psC[1][:]), ev)
                tok_psC_ev = t_yev
                tok_psC_ac = 0
                P.op("dv", lambda e: e.tensor_reduce(out=stats[:, 36:37],
                                                     in_=f32a[:, F_sc : F_sc + 1024],
                                                     axis=AX.X, op=ALU.add), ev)
                P.wait("ac", ev, t_yev)
                t_sq2 = P.op("ac", lambda e: e.activation(out=junk[:], in_=f32a[:, F_sc : F_sc + 1024],
                                                          func=AF.Square, accum_out=stats[:, 37:38]), ea)
                P.wait("dv", ea, t_sq2)
                P.op("dv", lambda e: e.tensor_scalar(out=stats[:, 38:39], in0=stats[:, 36:37],
                                                     scalar1=-1.0 / C, scalar2=None, op0=ALU.mult), ev)
                P.op("dv", lambda e: e.tensor_scalar(out=stats[:, 39:40], in0=stats[:, 37:38],
                                                     scalar1=1.0 / C, scalar2=None, op0=ALU.mult), ev)
                P.op("dv", lambda e: e.tensor_tensor(out=stats[:, 40:41], in0=stats[:, 38:39],
                                                     in1=stats[:, 38:39], op=ALU.mult), ev)
                t_v2 = P.op("dv", lambda e: e.tensor_tensor(out=stats[:, 41:42], in0=stats[:, 39:40],
                                                            in1=stats[:, 40:41], op=ALU.subtract), ev)
                P.wait("ac", ev, t_v2)
                P.op("ac", lambda e: e.activation(out=stats[:, 42:43], in_=stats[:, 41:42],
                                                  func=AF.Ln, bias=epscol[:, 0:1]), ea)
                t_r2 = P.op("ac", lambda e: e.activation(out=stats[:, 43:44], in_=stats[:, 42:43],
                                                         func=AF.Exp, scale=-0.5), ea)
                if p2_lvl < 8:
                    continue
                P.wait("dv", ea, t_r2)
                P.wait("dv", dsem(f"xr{xsl}"), t_xr)
                P.wait("dv", dsem(f"x2{asl}"), tok_x2b[asl])
                t_y1 = P.op("dv", lambda e: e.tensor_scalar(
                    out=f32a[:, F_sc : F_sc + 1024], in0=f32a[:, F_sc : F_sc + 1024],
                    scalar1=stats[:, 38:39], scalar2=stats[:, 43:44],
                    op0=ALU.add, op1=ALU.mult), ev)
                P.wait("dv", ep, tok_blob[0])
                t_x2 = P.op("dv", lambda e, xsl=xsl, asl=asl: e.tensor_tensor(
                    out=big[:, B_x2b + 1024 * asl : B_x2b + 1024 * asl + 1024],
                    in0=f32a[:, F_sc : F_sc + 1024],
                    in1=f32a[:, F_xb + 1024 * xsl : F_xb + 1024 * xsl + 1024],
                    op=ALU.add), ev)
                tok_xres[xsl] = t_x2
                P.wait("sp", ev, t_x2)
                t_x2d = P.dma("sp", lambda e, t0=t0, asl=asl: e.dma_start(
                    out=bass.AP(tensor=x2s, offset=t0 * C, ap=[[C, 128], [1, C]]),
                    in_=big[:, B_x2b + 1024 * asl : B_x2b + 1024 * asl + 1024]), dsem(f"x2{asl}"))
                if dbg:
                    P.wait("pl", ev, t_x2)
                    t_dbg = P.dma("pl", lambda e, t0=t0, asl=asl: e.dma_start(
                        out=dbg_t["d_x2"][t0 : t0 + 128, :],
                        in_=big[:, B_x2b + 1024 * asl : B_x2b + 1024 * asl + 1024]), dsem("dbg"))
                    P.wait("dv", dsem("dbg"), t_dbg)
                tok_x2b[asl] = t_x2d
        fence()

        for r0, r1 in splits(4):
            t_wd = P.dma("sp", lambda e, r0=r0, r1=r1: e.dma_start(
                out=fp8a[r0:r1, 0:32768],
                in_=bass.AP(tensor=wd8, offset=C * r0,
                            ap=[[C, r1 - r0], [128 * C, 32], [1, C]])), dsem("wd"))

        tok_x2t = [0, 0]
        tok_a1i = 0
        tok_a1b = 0
        tok_a1fm = 0
        tok_gw = [0, 0]
        tok_uw = [0, 0]
        tok_gA = [0, 0]
        tok_uA = [0, 0]
        tok_sil = 0
        tok_a2tm = 0
        tok_a2i = 0
        tok_a2fm = 0
        tok_ybuf = 0
        tok_scr_ac = 0
        tok_ob = [0, 0]
        tok_obx = 0
        tok_xtb = 0
        tok_g = 0

        _stopped3 = stop_after in ("consts", "wa", "wb", "wc", "p1", "p2", "p2w", "p2a")
        _NO_ILV = False
        t_ic_done_prev = 0
        t_sg3 = [0] * 8
        t_fm3 = [0] * 8
        t_dn3 = [0] * 8

        def ln4_tt(tt, sb):
            nonlocal tok_psC_ac, tok_ybuf
            P.wait("dv", ep, t_dn3[tt])
            P.op("dv", lambda e, tt=tt: e.tensor_copy(
                out=cpT[:, 1024 * tt : 1024 * tt + 512], in_=psC[0][:]), ev)
            t_ye2 = P.op("dv", lambda e, tt=tt: e.tensor_copy(
                out=cpT[:, 1024 * tt + 512 : 1024 * tt + 1024], in_=psC[1][:]), ev)
            P.op("dv", lambda e, tt=tt, sb=sb: e.tensor_reduce(
                out=stats[:, sb + 56 : sb + 57], in_=cpT[:, 1024 * tt : 1024 * tt + 1024],
                axis=AX.X, op=ALU.add), ev)
            P.wait("ac", ev, t_ye2)
            t_sq4 = P.op("ac", lambda e, tt=tt, sb=sb: e.activation(
                out=junk[:], in_=cpT[:, 1024 * tt : 1024 * tt + 1024],
                func=AF.Square, accum_out=stats[:, sb + 58 : sb + 59]), ea)
            tok_psC_ac = t_sq4
            P.op("dv", lambda e, tt=tt, sb=sb: e.tensor_scalar(out=stats[:, sb + 40 + tt : sb + 41 + tt],
                                                        in0=stats[:, sb + 56 : sb + 57],
                                                        scalar1=-1.0 / C, scalar2=None, op0=ALU.mult), ev)
            P.wait("dv", ea, t_sq4)
            P.op("dv", lambda e, tt=tt, sb=sb: e.tensor_scalar(out=stats[:, sb + 48 + tt : sb + 49 + tt],
                                                        in0=stats[:, sb + 58 : sb + 59],
                                                        scalar1=1.0 / C, scalar2=None, op0=ALU.mult), ev)
            P.op("dv", lambda e, tt=tt, sb=sb: e.tensor_tensor(out=stats[:, sb + 56 : sb + 57],
                                                        in0=stats[:, sb + 40 + tt : sb + 41 + tt],
                                                        in1=stats[:, sb + 40 + tt : sb + 41 + tt], op=ALU.mult), ev)
            P.op("dv", lambda e, tt=tt, sb=sb: e.tensor_tensor(out=stats[:, sb + 48 + tt : sb + 49 + tt],
                                                        in0=stats[:, sb + 48 + tt : sb + 49 + tt],
                                                        in1=stats[:, sb + 56 : sb + 57], op=ALU.subtract), ev)
            tok_ybuf = t_ye2

        def ln3_tt(F2, tt, t_icd):
            nonlocal tok_scr_ac, tok_a1i, tok_a1b
            sb = 64 * (F2 % 2)
            TF2 = F2 * P3S
            sl = tt % 2
            P.wait("sp", ea, tok_x2t[sl])
            for r0, r1 in splits(2):
                t_xl = P.dma("sp", lambda e, TF2=TF2, tt=tt, sl=sl, r0=r0, r1=r1: e.dma_start(
                    out=ternb[r0:r1, 1024 * sl : 1024 * sl + 1024],
                    in_=bass.AP(tensor=x2s, offset=(TF2 + 128 * tt + r0) * C,
                                ap=[[C, r1 - r0], [1, C]])), dsem(f"xt{sl}"))

            def x2t_ap(sl=sl):
                return ternb[:, 1024 * sl : 1024 * sl + 1024]

            P.wait("dv", dsem(f"xt{sl}"), t_xl)
            P.wait("dv", ea, tok_scr_ac)
            P.op("dv", lambda e, sl=sl: e.tensor_reduce(
                out=stats[:, sb + 56 : sb + 57], in_=x2t_ap(sl), axis=AX.X, op=ALU.add), ev)
            P.op("dv", lambda e, sl=sl: e.tensor_reduce(
                out=stats[:, sb + 58 : sb + 59], in_=x2t_ap(sl), axis=AX.X, op=ALU.max), ev)
            P.op("dv", lambda e, sl=sl: e.tensor_reduce(
                out=stats[:, sb + 59 : sb + 60], in_=x2t_ap(sl), axis=AX.X, op=ALU.min, negate=True), ev)
            P.wait("ac", dsem(f"xt{sl}"), t_xl)
            t_sq = P.op("ac", lambda e, sl=sl: e.activation(
                out=junk[:], in_=x2t_ap(sl), func=AF.Square, accum_out=stats[:, sb + 57 : sb + 58]), ea)
            P.op("dv", lambda e, tt=tt: e.tensor_scalar(
                out=stats[:, sb + 0 + tt : sb + 1 + tt], in0=stats[:, sb + 56 : sb + 57],
                scalar1=-1.0 / C, scalar2=None, op0=ALU.mult), ev)
            P.wait("dv", ea, t_sq)
            P.op("dv", lambda e: e.tensor_scalar(out=stats[:, sb + 60 : sb + 61], in0=stats[:, sb + 57 : sb + 58],
                                                 scalar1=1.0 / C, scalar2=None, op0=ALU.mult), ev)
            P.op("dv", lambda e, tt=tt: e.tensor_tensor(out=stats[:, sb + 61 : sb + 62],
                                                        in0=stats[:, sb + 0 + tt : sb + 1 + tt],
                                                        in1=stats[:, sb + 0 + tt : sb + 1 + tt], op=ALU.mult), ev)
            t_v3 = P.op("dv", lambda e: e.tensor_tensor(out=stats[:, sb + 62 : sb + 63], in0=stats[:, sb + 60 : sb + 61],
                                                        in1=stats[:, sb + 61 : sb + 62], op=ALU.subtract), ev)
            P.wait("ac", ev, t_v3)
            P.op("ac", lambda e: e.activation(out=stats[:, sb + 63 : sb + 64], in_=stats[:, sb + 62 : sb + 63],
                                              func=AF.Ln, bias=epscol[:, 0:1]), ea)
            t_r3 = P.op("ac", lambda e, tt=tt: e.activation(
                out=stats[:, sb + 8 + tt : sb + 9 + tt], in_=stats[:, sb + 63 : sb + 64],
                func=AF.Exp, scale=-0.5), ea)
            P.wait("dv", ea, t_r3)
            P.op("dv", lambda e, tt=tt: e.tensor_scalar(out=stats[:, sb + 58 : sb + 59], in0=stats[:, sb + 58 : sb + 59],
                                                        scalar1=stats[:, sb + 0 + tt : sb + 1 + tt],
                                                        scalar2=None, op0=ALU.add), ev)
            P.op("dv", lambda e, tt=tt: e.tensor_scalar(out=stats[:, sb + 59 : sb + 60], in0=stats[:, sb + 59 : sb + 60],
                                                        scalar1=stats[:, sb + 0 + tt : sb + 1 + tt],
                                                        scalar2=None, op0=ALU.add), ev)
            P.op("dv", lambda e: e.tensor_tensor(out=stats[:, sb + 58 : sb + 59], in0=stats[:, sb + 58 : sb + 59],
                                                 in1=stats[:, sb + 59 : sb + 60], op=ALU.max), ev)
            P.op("dv", lambda e, tt=tt: e.tensor_tensor(out=stats[:, sb + 16 + tt : sb + 17 + tt],
                                                        in0=stats[:, sb + 58 : sb + 59],
                                                        in1=stats[:, sb + 8 + tt : sb + 9 + tt], op=ALU.mult), ev)
            P.op("dv", lambda e, tt=tt: e.tensor_scalar(out=stats[:, sb + 60 : sb + 61],
                                                        in0=stats[:, sb + 16 + tt : sb + 17 + tt],
                                                        scalar1=1e-5, scalar2=None, op0=ALU.max), ev)
            P.op("dv", lambda e: e.reciprocal(out=stats[:, sb + 61 : sb + 62], in_=stats[:, sb + 60 : sb + 61]), ev)
            P.op("dv", lambda e: e.tensor_scalar(out=stats[:, sb + 62 : sb + 63], in0=stats[:, sb + 61 : sb + 62],
                                                 scalar1=127.0, scalar2=None, op0=ALU.mult), ev)
            P.op("dv", lambda e, tt=tt: e.tensor_tensor(out=stats[:, sb + 61 : sb + 62], in0=stats[:, sb + 62 : sb + 63],
                                                        in1=stats[:, sb + 8 + tt : sb + 9 + tt], op=ALU.mult), ev)
            t_ab = P.op("dv", lambda e, tt=tt: e.tensor_tensor(out=stats[:, sb + 63 : sb + 64],
                                                               in0=stats[:, sb + 61 : sb + 62],
                                                               in1=stats[:, sb + 0 + tt : sb + 1 + tt],
                                                               op=ALU.mult), ev)
            P.op("dv", lambda e, tt=tt: e.tensor_tensor(out=stats[:, sb + 24 + tt : sb + 25 + tt],
                                                        in0=stats[:, sb + 16 + tt : sb + 17 + tt],
                                                        in1=scal[:, 6:7], op=ALU.mult), ev)
            P.op("dv", lambda e, tt=tt: e.tensor_scalar(out=stats[:, sb + 24 + tt : sb + 25 + tt],
                                                        in0=stats[:, sb + 24 + tt : sb + 25 + tt],
                                                        scalar1=1.0 / 254.0, scalar2=None, op0=ALU.mult), ev)
            t_sg3[tt] = P.n(ev)
            P.op("dv", lambda e, tt=tt: e.tensor_tensor(out=stats[:, sb + 32 + tt : sb + 33 + tt],
                                                        in0=stats[:, sb + 16 + tt : sb + 17 + tt],
                                                        in1=scal[:, 7:8], op=ALU.mult), ev)
            P.op("dv", lambda e, tt=tt: e.tensor_tensor(out=stats[:, sb + 32 + tt : sb + 33 + tt],
                                                        in0=stats[:, sb + 32 + tt : sb + 33 + tt],
                                                        in1=scal[:, 8:9], op=ALU.mult), ev)
            P.op("dv", lambda e, tt=tt: e.tensor_scalar(out=stats[:, sb + 32 + tt : sb + 33 + tt],
                                                        in0=stats[:, sb + 32 + tt : sb + 33 + tt],
                                                        scalar1=1.0 / 64516.0, scalar2=None, op0=ALU.mult), ev)
            P.wait("ac", ev, t_ab)
            P.wait("ac", ev, tok_a1i)
            t_ai = P.op("ac", lambda e, sl=sl: e.activation(
                out=i16a[:, 2048:3072], in_=x2t_ap(sl), func=AF.Identity,
                scale=stats[:, sb + 61 : sb + 62], bias=stats[:, sb + 63 : sb + 64]), ea)
            tok_x2t[sl] = t_ai
            tok_scr_ac = t_ai
            P.wait("dv", ea, t_ai)
            P.wait("dv", ep, tok_a1b)
            t_a1b = P.op("dv", lambda e: e.tensor_copy(out=wdT[:, 0:1024], in_=i16a[:, 2048:3072]), ev)
            tok_a1i = t_a1b
            P.wait("pe", ev, t_a1b)
            P.wait("pe", ev, tok_psB[0])
            t_tr = None
            for fc in range(8):
                t_tr = P.op("pe", lambda e, fc=fc: e.transpose(
                    out=psB[0][:, 128 * fc : 128 * fc + 128],
                    in_=wdT[:, 128 * fc : 128 * fc + 128], identity=identb[:]), ep)
            tok_a1b = t_tr
            P.wait("dv", ep, t_tr)
            if tt == 0:
                P.wait("dv", ep, t_icd)
            t_fm = P.op("dv", lambda e, tt=tt: e.tensor_copy(
                out=bass.AP(tensor=df[:].tensor, offset=128 * tt,
                            ap=[list(df[:].ap[0]), [P3S, 8], [1, 128]]),
                in_=psB[0][:].rearrange("p (a b) -> p a b", a=8)), ev)
            tok_psB[0] = t_fm
            t_fm3[tt] = t_fm


        for F in range(NSUP3 if not _stopped3 else 0):
            TF = F * P3S
            sb = 64 * (F % 2)
            if F == 0 or _NO_ILV:
                for tt in range(NTT):
                    ln3_tt(F, tt, t_ic_done_prev)
            t_a1_all = P.n(ev)
            for ic in range(8 if p3_lvl >= 2 else 0):
                wsl = ic % 2
                P.wait("sp", ep, tok_gw[wsl])
                for r0, r1 in splits(2):
                    t_gl = P.dma("sp", lambda e, ic=ic, wsl=wsl, r0=r0, r1=r1: e.dma_start(
                        out=fp8a[r0:r1, 32768 + 4096 * wsl : 32768 + 4096 * wsl + 4096],
                        in_=bass.AP(tensor=wg8, offset=512 * ic + INNER * r0,
                                    ap=[[INNER, r1 - r0], [128 * INNER, 8], [1, 512]])), dsem(f"wg{wsl}"))
                P.wait("sp", ep, tok_uw[wsl])
                for r0, r1 in splits(2):
                    t_ul = P.dma("sp", lambda e, ic=ic, wsl=wsl, r0=r0, r1=r1: e.dma_start(
                        out=fp8a[r0:r1, 40960 + 4096 * wsl : 40960 + 4096 * wsl + 4096],
                        in_=bass.AP(tensor=wu8, offset=512 * ic + INNER * r0,
                                    ap=[[INNER, r1 - r0], [128 * INNER, 8], [1, 512]])), dsem(f"wu{wsl}"))
                t_gmm = t_umm = None
                for tt in range(NTT):
                    ba = tt % 2
                    P.wait("pe", ev, t_fm3[tt] if ic == 0 else t_a1_all)
                    P.wait("pe", dsem(f"wg{wsl}"), t_gl)
                    P.wait("pe", ea, tok_gA[ba])
                    for fc in range(8):
                        P.op("pe", lambda e, tt=tt, fc=fc, ba=ba, wsl=wsl: e.matmul(
                            out=psA[ba][:, :],
                            lhsT=bass.AP(tensor=df[:].tensor, offset=128 * tt + P3S * fc,
                                         ap=[list(df[:].ap[0]), [1, 128]]),
                            rhs=fp8a[:, 32768 + 4096 * wsl + 512 * fc : 32768 + 4096 * wsl + 512 * fc + 512],
                            start=(fc == 0), stop=(fc == 7)), ep)
                    t_gmm = P.n(ep)
                    P.wait("pe", dsem(f"wu{wsl}"), t_ul)
                    P.wait("pe", ev, tok_uA[ba])
                    for fc in range(8):
                        P.op("pe", lambda e, tt=tt, fc=fc, ba=ba, wsl=wsl: e.matmul(
                            out=psA[2 + ba][:, :],
                            lhsT=bass.AP(tensor=df[:].tensor, offset=128 * tt + P3S * fc,
                                         ap=[list(df[:].ap[0]), [1, 128]]),
                            rhs=fp8a[:, 40960 + 4096 * wsl + 512 * fc : 40960 + 4096 * wsl + 512 * fc + 512],
                            start=(fc == 0), stop=(fc == 7)), ep)
                    t_umm = P.n(ep)
                    P.wait("ac", ep, t_gmm)
                    P.wait("ac", ev, tok_sil)
                    if ic == 0:
                        P.wait("ac", ev, t_sg3[tt])
                    t_si = P.op("ac", lambda e, tt=tt, sb=sb: e.activation(
                        out=wdT[:, 1024:1536], in_=psA[tt % 2][:],
                        func=AF.Silu, scale=stats[:, sb + 24 + tt : sb + 25 + tt]), ea)
                    tok_gA[tt % 2] = t_si
                    P.wait("dv", ep, t_umm)
                    P.wait("dv", ea, t_si)
                    if ic == 0 and tt in (2, 3):
                        P.wait("dv", dsem("obx"), tok_obx)
                    if ic == 0 and tt in (4, 5):
                        P.wait("dv", dsem("xtb"), tok_xtb)
                    t_g = P.op("dv", lambda e, tt=tt, ic=ic: e.tensor_tensor(
                        out=big[:, 4096 * tt + 512 * ic : 4096 * tt + 512 * ic + 512],
                        in0=wdT[:, 1024:1536], in1=psA[2 + tt % 2][:], op=ALU.mult), ev)
                    tok_sil = t_g
                    tok_uA[tt % 2] = t_g
                tok_gw[wsl] = t_gmm
                tok_uw[wsl] = t_umm
            t_ic_done = P.n(ep)
            t_ic_done_prev = t_ic_done
            t_g_all = P.n(ev)
            if dbg and F == 0:
                P.wait("pl", ev, t_g_all)
                t_dg = P.dma("pl", lambda e: e.dma_start(out=dbg_t["d_g"], in_=big[:, 0 : NTT * INNER]), dsem("dbg"))
                P.wait("dv", dsem("dbg"), t_dg)
            for tt in range(NTT if p3_lvl >= 3 else 0):
                P.wait("dv", ea, tok_scr_ac)
                t_mg = P.op("dv", lambda e, tt=tt, sb=sb: e.tensor_reduce(
                    out=stats[:, sb + 56 : sb + 57], in_=big[:, 4096 * tt : 4096 * tt + 4096],
                    axis=AX.X, op=ALU.max, apply_absolute_value=True), ev)
                P.op("dv", lambda e, sb=sb: e.tensor_scalar(out=stats[:, sb + 57 : sb + 58], in0=stats[:, sb + 56 : sb + 57],
                                                     scalar1=1e-30, scalar2=None, op0=ALU.max), ev)
                P.op("dv", lambda e, sb=sb: e.reciprocal(out=stats[:, sb + 58 : sb + 59], in_=stats[:, sb + 57 : sb + 58]), ev)
                t_s2 = P.op("dv", lambda e, sb=sb: e.tensor_scalar(out=stats[:, sb + 59 : sb + 60], in0=stats[:, sb + 58 : sb + 59],
                                                            scalar1=127.0, scalar2=None, op0=ALU.mult), ev)
                P.op("dv", lambda e, tt=tt, sb=sb: e.tensor_tensor(out=stats[:, sb + 32 + tt : sb + 33 + tt],
                                                            in0=stats[:, sb + 32 + tt : sb + 33 + tt],
                                                            in1=stats[:, sb + 56 : sb + 57], op=ALU.mult), ev)
                t_c = None
                for pair in range(2):
                    for hf2 in range(2):
                        hf = 2 * pair + hf2
                        P.wait("ac", ev, t_s2)
                        P.wait("ac", ev, tok_a2i)
                        t_c = P.op("ac", lambda e, tt=tt, hf=hf, hf2=hf2, sb=sb: e.activation(
                            out=i16a[:, 1024 * hf2 : 1024 * hf2 + 1024],
                            in_=big[:, 4096 * tt + 1024 * hf : 4096 * tt + 1024 * hf + 1024],
                            func=AF.Identity, scale=stats[:, sb + 59 : sb + 60]), ea)
                    P.wait("dv", ea, t_c)
                    P.wait("dv", ep, tok_a2tm)
                    t_cp3 = None
                    for hf2 in range(2):
                        hf = 2 * pair + hf2
                        t_cp3 = P.op("dv", lambda e, hf=hf, hf2=hf2: e.tensor_copy(
                            out=wvT[:, 1024 * hf : 1024 * hf + 1024],
                            in_=i16a[:, 1024 * hf2 : 1024 * hf2 + 1024]), ev)
                    tok_a2i = t_cp3
                tok_scr_ac = t_c
                t_a2tm = tok_a2i
                P.wait("pe", ev, t_a2tm)
                P.wait("pe", ep, tok_a2fm)
                t_fm2 = None
                for w4 in range(2):
                    P.wait("pe", ev, tok_psB[0])
                    P.wait("pe", ev, tok_psB[1])
                    t_tr = None
                    for kk in range(16):
                        icc = 16 * w4 + kk
                        t_tr = P.op("pe", lambda e, icc=icc, kk=kk: e.transpose(
                            out=psB[kk % 2][:, 128 * (kk // 2) : 128 * (kk // 2) + 128],
                            in_=wvT[:, 128 * icc : 128 * icc + 128], identity=identb[:]), ep)
                    if w4 == 1:
                        tok_a2tm = t_tr
                    P.wait("dv", ep, t_tr)
                    for b in range(2):
                        t_fm2 = P.op("dv", lambda e, w4=w4, b=b: e.tensor_copy(
                            out=bass.AP(tensor=wvT[:].tensor,
                                        offset=4096 + 2048 * w4 + 128 * b,
                                        ap=[list(wvT[:].ap[0]), [256, 8], [1, 128]]),
                            in_=psB[b][:]), ev)
                    tok_psB[0] = t_fm2
                    tok_psB[1] = t_fm2
                t_a2f = t_fm2
                if F + 1 < NSUP3 and p3_lvl >= 4 and not _NO_ILV:
                    ln3_tt(F + 1, tt, t_ic_done)
                if tt > 0:
                    ln4_tt(tt - 1, sb)
                P.wait("pe", ev, t_a2f)
                P.wait("pe", dsem("wd"), t_wd)
                P.wait("pe", ev, tok_ybuf)
                P.wait("pe", ea, tok_psC_ac)
                for ncx in range(2):
                    for icc in range(32):
                        P.op("pe", lambda e, icc=icc, ncx=ncx: e.matmul(
                            out=psC[ncx][:, :],
                            lhsT=wvT[:, 4096 + 128 * icc : 4096 + 128 * icc + 128],
                            rhs=fp8a[:, 1024 * icc + 512 * ncx : 1024 * icc + 512 * ncx + 512],
                            start=(icc == 0), stop=(icc == 31)), ep)
                t_dn = P.n(ep)
                tok_a2fm = t_dn
                t_dn3[tt] = t_dn
            if NTT and p3_lvl >= 3:
                ln4_tt(NTT - 1, sb)
            t_q_ev = P.n(ev)
            t_q_ea = P.n(ea)
            if p3_lvl < 4:
                continue
            P.op("dv", lambda e, sb=sb: e.tensor_tensor(out=stats[:, sb + 56 : sb + 56 + NTT],
                                                 in0=stats[:, sb + 32 : sb + 32 + NTT],
                                                 in1=stats[:, sb + 32 : sb + 32 + NTT], op=ALU.mult), ev)
            t_vt = P.op("dv", lambda e, sb=sb: e.tensor_tensor(out=stats[:, sb + 48 : sb + 48 + NTT],
                                                        in0=stats[:, sb + 48 : sb + 48 + NTT],
                                                        in1=stats[:, sb + 56 : sb + 56 + NTT], op=ALU.mult), ev)
            P.wait("ac", ev, t_vt)
            P.op("ac", lambda e, sb=sb: e.activation(out=stats[:, sb + 56 : sb + 56 + NTT], in_=stats[:, sb + 48 : sb + 48 + NTT],
                                              func=AF.Ln, bias=epscol[:, 0:1]), ea)
            t_r4 = P.op("ac", lambda e, sb=sb: e.activation(out=stats[:, sb + 48 : sb + 48 + NTT], in_=stats[:, sb + 56 : sb + 56 + NTT],
                                                     func=AF.Exp, scale=-0.5), ea)
            P.wait("dv", ea, t_r4)
            t_cr = P.op("dv", lambda e, sb=sb: e.tensor_tensor(out=stats[:, sb + 48 : sb + 48 + NTT],
                                                        in0=stats[:, sb + 48 : sb + 48 + NTT],
                                                        in1=stats[:, sb + 32 : sb + 32 + NTT], op=ALU.mult), ev)
            P.wait("sp", ev, t_q_ev)
            P.wait("sp", ea, t_q_ea)
            P.wait("sp", dsem("obx"), tok_obx)
            for r0, r1 in splits(4):
                t_xb = P.dma("sp", lambda e, TF=TF, r0=r0, r1=r1: e.dma_start(
                    out=big[r0:r1, 16384 : 16384 + NTT * 1024],
                    in_=bass.AP(tensor=x2s, offset=(TF + r0) * C,
                                ap=[[C, r1 - r0], [128 * C, NTT], [1, C]])), dsem("xtb"))
            tok_xtb = t_xb
            P.wait("dv", dsem("xtb"), t_xb)
            t_ou = None
            for tt in range(NTT):
                P.op("dv", lambda e, tt=tt, sb=sb: e.tensor_scalar(
                    out=f32a[:, F_sc : F_sc + 1024], in0=cpT[:, 1024 * tt : 1024 * tt + 1024],
                    scalar1=stats[:, sb + 40 + tt : sb + 41 + tt], scalar2=stats[:, sb + 48 + tt : sb + 49 + tt],
                    op0=ALU.add, op1=ALU.mult), ev)
                t_ou = P.op("dv", lambda e, tt=tt: e.tensor_tensor(
                    out=big[:, 8192 + 1024 * tt : 8192 + 1024 * tt + 1024],
                    in0=f32a[:, F_sc : F_sc + 1024],
                    in1=big[:, 16384 + 1024 * tt : 16384 + 1024 * tt + 1024], op=ALU.add), ev)
            P.wait("pl", ev, t_ou)
            for r0, r1 in splits(4):
                tok_obx = P.dma("pl", lambda e, TF=TF, r0=r0, r1=r1: e.dma_start(
                    out=bass.AP(tensor=out_t, offset=(TF + r0) * C,
                                ap=[[C, r1 - r0], [128 * C, NTT], [1, C]]),
                    in_=big[r0:r1, 8192 : 8192 + NTT * 1024]), dsem("obx"))

            tok_ybuf = 0
            tok_a1fm = tok_gw[0] if tok_gw[0] else 0
        fence()

        with nc.Block() as block:
            @block.sync
            def _(e):
                P.replay("sp", e)

            @block.tensor
            def _(e):
                P.replay("pe", e)

            @block.vector
            def _(e):
                P.replay("dv", e)

            @block.scalar
            def _(e):
                P.replay("ac", e)

            @block.gpsimd
            def _(e):
                P.replay("pl", e)

    return nc



def _host_consts(inputs):
    sfw = np.asarray(inputs["L0_attn_strain_fused_w"], np.float32)
    sfb = np.asarray(inputs["L0_attn_strain_fused_b"], np.float32)
    bondw = np.asarray(inputs["L0_attn_bond_out_w"], np.float32)
    dmgw = np.asarray(inputs["L0_attn_damage_out_w"], np.float32)
    dmgb = np.asarray(inputs["L0_attn_damage_out_b"], np.float32)

    identb = np.eye(128, dtype=ml_dtypes.bfloat16)
    maskneg = np.zeros((128, WIN), np.float32)
    r = np.arange(128)[:, None]
    wv = np.arange(WIN)[None, :]
    maskneg[(r + wv) < (WIN - 1)] = -1e9

    logitw = np.zeros((128, 8), np.float32)
    for h in range(4):
        logitw[32 * h : 32 * h + BD, h] = bondw[0]
        logitw[32 * h + BD : 32 * h + 2 * BD, 4 + h] = dmgw[0]

    dfw32 = np.zeros((128, 256), np.float32)
    for h in range(8):
        c0 = (h % 4) * 32 + (h // 4) * 128
        dfw32[16 * h : 16 * h + BD, c0 : c0 + 2 * BD] = sfw.T
    csfb = np.repeat(np.tile(sfb, 4)[:, None], WIN, axis=1).astype(np.float32)
    dbrep = np.full((128, 1), float(dmgb[0]), np.float32)
    return {
        "identb": identb, "maskneg": maskneg.astype(ml_dtypes.bfloat16), "logitw32": logitw,
        "dfw32": dfw32, "csfb": csfb, "dbrep": dbrep,
    }


def kernel(**inputs):
    T_full = int(np.asarray(inputs["L0_attn_x"]).shape[1])
    nc = build(T=T_full, dbg=False)

    x = np.ascontiguousarray(np.asarray(inputs["L0_attn_x"], np.float32))
    consts = _host_consts(inputs)
    shared = {
        "wdisp": np.ascontiguousarray(np.asarray(inputs["L0_attn_W_disp"], np.float32)),
        "wval": np.ascontiguousarray(np.asarray(inputs["L0_attn_W_val"], np.float32)),
        "relpos": np.ascontiguousarray(np.asarray(inputs["L0_attn_rel_pos_emb"], np.float32)[:WIN]),
        "pposw": np.ascontiguousarray(np.asarray(inputs["L0_attn_pos_proj_w"], np.float32)),
        "cprojw": np.ascontiguousarray(np.asarray(inputs["L0_attn_c_proj_w"], np.float32)),
        "wgate": np.ascontiguousarray(np.asarray(inputs["L0_ffn_w_gate"], np.float32)),
        "wup": np.ascontiguousarray(np.asarray(inputs["L0_ffn_w_up"], np.float32)),
        "wdown": np.ascontiguousarray(np.asarray(inputs["L0_ffn_w_down"], np.float32)),
        **consts,
    }
    ncores = x.shape[0]
    in_maps = [{"x": np.ascontiguousarray(x[c]), **shared} for c in range(ncores)]
    res = run_bass_kernel_spmd(nc, in_maps, core_ids=list(range(ncores)))
    outs = [np.asarray(r["out"], np.float32) for r in res.results]
    return np.stack(outs, axis=0)


# version 20
# speedup vs baseline: 1.0286x; 1.0055x over previous
import numpy as np
import ml_dtypes
import concourse.bass as bass
import concourse.mybir as mybir
from concourse.bass_utils import run_bass_kernel_spmd
from contextlib import ExitStack

F32 = mybir.dt.float32
BF16 = mybir.dt.bfloat16
I16 = mybir.dt.int16
FP8 = mybir.dt.float8e4
AF = mybir.ActivationFunctionType
ALU = mybir.AluOpType
AX = mybir.AxisListType

C = 1024
NH = 16
BD = 16
HS = 64
WIN = 64
INNER = 4096
PAD = 256
EPS = 1e-5


class Planner:
    def __init__(self):
        self.steps = []
        self.counts = {}
        self.engs = ("sp", "pe", "dv", "ac", "pl")
        self.chain = ("dv", "ac")
        self.last_chain = {}

    def reg(self, s):
        self.counts[id(s)] = 0
        return s

    def n(self, s):
        return self.counts[id(s)]

    def wait(self, eng, s, val=None):
        v = self.counts[id(s)] if val is None else val
        if v > 0:
            self.steps.append((eng, "w", (s, v)))

    def op(self, eng, fn, s=None, inc=1):
        if eng in self.chain and s is not None:
            cur = self.counts[id(s)]
            if cur > 0 and self.last_chain.get(eng, -1) < cur:
                self.steps.append((eng, "w", (s, cur)))
                self.last_chain[eng] = cur
        self.steps.append((eng, "o", (fn, s, inc)))
        if s is not None:
            self.counts[id(s)] += inc
            return self.counts[id(s)]
        return None

    def dma(self, eng, fn, s):
        return self.op(eng, fn, s, 16)

    def replay(self, key, eng):
        for k, kind, pl in self.steps:
            if k != key:
                continue
            if kind == "w":
                eng.wait_ge(pl[0], pl[1])
            else:
                fn, s, inc = pl
                r = fn(eng)
                if s is not None:
                    r.then_inc(s, inc)


def build(T=4096, dbg=False, stop_after=None, p1_lvl=8, p2_lvl=8, p3_lvl=4):
    NT = T // 128
    P2S = min(2048, T)
    NSUP2 = T // P2S
    NSUB = P2S // 128
    NQC = NSUB // 4
    P3S = min(1024, T)
    NSUP3 = T // P3S
    NTT = P3S // 128
    DF1 = T + 128

    nc = bass.Bass("TRN2", target_bir_lowering=False)

    x_in = nc.dram_tensor("x", [T, C], F32, kind="ExternalInput").ap()
    wdisp = nc.dram_tensor("wdisp", [NH * BD, C], F32, kind="ExternalInput").ap()
    wval = nc.dram_tensor("wval", [C, C], F32, kind="ExternalInput").ap()
    relpos = nc.dram_tensor("relpos", [WIN, BD], F32, kind="ExternalInput").ap()
    pposw = nc.dram_tensor("pposw", [BD, BD], F32, kind="ExternalInput").ap()
    cprojw = nc.dram_tensor("cprojw", [C, C], F32, kind="ExternalInput").ap()
    wgate = nc.dram_tensor("wgate", [INNER, C], F32, kind="ExternalInput").ap()
    wup = nc.dram_tensor("wup", [INNER, C], F32, kind="ExternalInput").ap()
    wdown = nc.dram_tensor("wdown", [C, INNER], F32, kind="ExternalInput").ap()
    identb_in = nc.dram_tensor("identb", [128, 128], BF16, kind="ExternalInput").ap()
    maskneg_in = nc.dram_tensor("maskneg", [128, WIN], BF16, kind="ExternalInput").ap()
    logitw_in = nc.dram_tensor("logitw32", [128, 8], F32, kind="ExternalInput").ap()
    dfw_in = nc.dram_tensor("dfw32", [128, 256], F32, kind="ExternalInput").ap()
    csfb_in = nc.dram_tensor("csfb", [128, WIN], F32, kind="ExternalInput").ap()
    dbrep_in = nc.dram_tensor("dbrep", [128, 1], F32, kind="ExternalInput").ap()

    wp = nc.dram_tensor("wp", [NH * T * PAD], BF16, kind="Internal")
    x2s = nc.dram_tensor("x2s", [T * C], BF16, kind="Internal")
    vals = nc.dram_tensor("vals", [T * C], BF16, kind="Internal")
    wg8 = nc.dram_tensor("wg8", [8 * 128 * INNER], FP8, kind="Internal")
    wu8 = nc.dram_tensor("wu8", [8 * 128 * INNER], FP8, kind="Internal")
    wd8 = nc.dram_tensor("wd8", [32 * 128 * C], FP8, kind="Internal")

    out_t = nc.dram_tensor("out", [T, C], F32, kind="ExternalOutput")
    out = out_t.ap()
    dbg_t = {}
    if dbg:
        for nm, shp in [("d_df", [128, 4 * DF1]), ("d_val", [T, C]),
                        ("d_ltm", [128, 16 * 512]), ("d_x2", [T, C]),
                        ("d_g", [128, min(8, NTT) * INNER]), ("d_ptm", [128, 16 * 256])]:
            dbg_t[nm] = nc.dram_tensor(nm, shp, F32, kind="ExternalOutput").ap()

    P = Planner()

    with ExitStack() as es:
        def sb(name, shape, dtype=F32):
            return es.enter_context(nc.sbuf_tensor(name, shape, dtype))

        def psum(name, shape, dtype=F32):
            return es.enter_context(nc.psum_tensor(name, shape, dtype))

        def sem(name):
            return P.reg(es.enter_context(nc.semaphore(name)))

        df = sb("df", [128, max(4 * DF1, 8 * P3S)], BF16)
        wvT = sb("wvT", [128, 8 * C], BF16)
        cpT = sb("cpT", [128, 8 * C], BF16)
        wdT = sb("wdT", [128, 2048], BF16)
        big = sb("big", [128, 32768], BF16)
        f32a = sb("f32a", [128, 3088], F32)
        i16a = sb("i16a", [128, 3072], I16)
        fp8a = sb("fp8a", [128, 8 * INNER + 16384], FP8)
        junk = sb("junk", [128, 1024], BF16)
        identb = sb("identb_t", [128, 128], BF16)
        dfwb = sb("dfwb", [128, 256], BF16)
        logit_w = sb("logit_w", [128, 8], BF16)
        c_all = sb("c_all", [128, WIN])
        maskneg = sb("maskneg_t", [128, WIN], BF16)
        dbcol = sb("dbcol", [128, 1])
        stats = sb("stats", [128, 128])
        scal = sb("scal", [128, 16])
        onesb = sb("onesb", [1, 128], BF16)
        ones32 = sb("ones32", [1, 128], F32)
        onescol = sb("onescol", [128, 1], F32)
        epscol = sb("epscol", [128, 1], F32)
        absac = sb("absac", [128, 32])
        ternb = sb("ternb", [128, 2048], BF16)

        w32 = f32a

        psA = [psum(f"psA{i}", [128, 512], F32) for i in range(4)]
        psB = [psum(f"psB{i}", [128, 1024], BF16) for i in range(2)]
        psC = [psum(f"psC{i}", [128, 512], F32) for i in range(2)]

        ev = sem("ev")
        ea = sem("ea")
        ep = sem("ep")
        ALLS = [ev, ea, ep]
        _streams = {}

        def dsem(name):
            if name not in _streams:
                s = sem("q_" + name)
                _streams[name] = s
                ALLS.append(s)
            return _streams[name]

        def fence():
            for e in P.engs:
                for s in ALLS:
                    P.wait(e, s)

        def ap_rows(a, r0, r1):
            return bass.AP(tensor=a.tensor, offset=a.offset + a.ap[0][0] * r0,
                           ap=[[a.ap[0][0], r1 - r0]] + [list(d) for d in a.ap[1:]])

        def splits(n):
            return [(128 * i // n, 128 * (i + 1) // n) for i in range(n)]

        B_h1buf, B_h1fm, B_disp, B_vstg = 0, 1024, 2048, 3072
        B_args, B_gargs, B_ptm = 0, 8192, 14336
        B_ebuf, B_btb, B_btb2 = 14848, 15360, 19456
        B_bb, B_attn, B_x2b, B_vring = 23552, 25600, 27648, 29696
        B_blob = B_x2b
        F_xb, F_sc = 0, 2048

        P.dma("sp", lambda e: e.dma_start(out=identb[:], in_=identb_in), dsem("c"))
        P.dma("sp", lambda e: e.dma_start(out=maskneg[:], in_=maskneg_in), dsem("c"))
        P.dma("sp", lambda e: e.dma_start(out=c_all[:], in_=csfb_in), dsem("c"))
        P.dma("sp", lambda e: e.dma_start(out=stats[:, 0:1], in_=dbrep_in), dsem("c"))
        P.dma("sp", lambda e: e.dma_start(out=f32a[:, 0:8], in_=logitw_in), dsem("c"))
        P.dma("sp", lambda e: e.dma_start(out=f32a[:, 8:264], in_=dfw_in), dsem("c"))
        P.dma("sp", lambda e: e.dma_start(out=f32a[0:WIN, 264:280], in_=relpos), dsem("c"))
        P.dma("sp", lambda e: e.dma_start(out=f32a[0:BD, 280:296], in_=pposw), dsem("c"))
        t_cl = P.n(dsem("c"))

        P.wait("dv", dsem("c"), t_cl)
        P.op("dv", lambda e: e.tensor_copy(out=logit_w[:], in_=f32a[:, 0:8]), ev)
        P.op("dv", lambda e: e.tensor_copy(out=dfwb[:], in_=f32a[:, 8:264]), ev)
        P.op("dv", lambda e: e.tensor_copy(out=ternb[0:WIN, 0:BD], in_=f32a[0:WIN, 264:280]), ev)
        P.op("dv", lambda e: e.tensor_copy(out=ternb[0:BD, 16:32], in_=f32a[0:BD, 280:296]), ev)
        P.op("dv", lambda e: e.memset(onesb[:], 1.0), ev)
        P.op("dv", lambda e: e.memset(ones32[:], 1.0), ev)
        P.op("dv", lambda e: e.memset(onescol[:], 1.0), ev)
        P.op("dv", lambda e: e.memset(epscol[:], float(EPS)), ev)
        P.op("dv", lambda e: e.memset(df[:], 0.0), ev)
        P.op("dv", lambda e: e.tensor_scalar(out=dbcol[:], in0=stats[:, 0:1],
                                             scalar1=0.5, scalar2=None, op0=ALU.mult), ev)
        t_cv = P.n(ev)

        for _pi in range(4):
            P.op("dv", lambda e, _pi=_pi: e.memset(psA[_pi][:], 0.0), ev)
        P.op("dv", lambda e: e.memset(psC[0][:], 0.0), ev)
        P.op("dv", lambda e: e.memset(psC[1][:], 0.0), ev)
        P.wait("pe", ev, t_cv)
        P.op("pe", lambda e: e.transpose(out=psB[0][0:BD, 0:WIN], in_=ternb[0:WIN, 0:BD],
                                         identity=identb[0:WIN, 0:WIN]), ep)
        P.op("pe", lambda e: e.transpose(out=psB[0][0:BD, 64:80], in_=ternb[0:BD, 16:32],
                                         identity=identb[0:BD, 0:BD]), ep)
        t_ct = P.n(ep)
        P.wait("dv", ep, t_ct)
        P.op("dv", lambda e: e.tensor_copy(out=ternb[0:BD, 32:96], in_=psB[0][0:BD, 0:WIN]), ev)
        P.op("dv", lambda e: e.tensor_copy(out=ternb[0:BD, 96:112], in_=psB[0][0:BD, 64:80]), ev)
        t_ce = P.n(ev)
        P.wait("pe", ev, t_ce)
        P.op("pe", lambda e: e.matmul(out=psA[0][0:BD, 0:WIN], lhsT=ternb[0:BD, 96:112],
                                      rhs=ternb[0:BD, 32:96], start=True, stop=True), ep)
        t_cp = P.n(ep)
        P.wait("dv", ep, t_cp)
        for hl in range(4):
            P.op("dv", lambda e, hl=hl: e.tensor_tensor(
                out=c_all[32 * hl : 32 * hl + BD, :], in0=c_all[32 * hl : 32 * hl + BD, :],
                in1=psA[0][0:BD, 0:WIN], op=ALU.add), ev)
        P.op("dv", lambda e: e.memset(big[:, B_x2b : B_x2b + 2048], 0.0), ev)
        P.op("dv", lambda e: e.memset(ternb[:], 0.0), ev)
        fence()

        tok_w32 = [0, 0]
        tok_tern = [0, 0]
        tok_psB = [0, 0]

        def prep_attn_w(src_fn, n_tiles, dest_tensor, dest_stride):
            for r in range(n_tiles):
                sl = r % 2
                P.wait("sp", ev, tok_w32[sl])
                for r0, r1 in splits(4):
                    t_ld = P.dma("sp", lambda e, r=r, sl=sl, src_fn=src_fn, r0=r0, r1=r1: e.dma_start(
                        out=w32[r0:r1, 1024 * sl : 1024 * sl + 1024],
                        in_=ap_rows(src_fn(r), r0, r1)), dsem(f"w{sl}"))
                P.wait("dv", dsem(f"w{sl}"), t_ld)
                P.wait("dv", ep, tok_tern[sl])
                t_cast = P.op("dv", lambda e, sl=sl: e.tensor_copy(
                    out=ternb[:, 1024 * sl : 1024 * sl + 1024],
                    in_=w32[:, 1024 * sl : 1024 * sl + 1024]), ev)
                tok_w32[sl] = t_cast
                P.wait("pe", ev, t_cast)
                P.wait("pe", ev, tok_psB[sl])
                t_tr = None
                for fc in range(8):
                    t_tr = P.op("pe", lambda e, fc=fc, sl=sl: e.transpose(
                        out=psB[sl][:, 128 * fc : 128 * fc + 128],
                        in_=ternb[:, 1024 * sl + 128 * fc : 1024 * sl + 128 * fc + 128],
                        identity=identb[:]), ep)
                tok_tern[sl] = t_tr
                P.wait("dv", ep, t_tr)
                t_ev = P.op("dv", lambda e, r=r, sl=sl, d=dest_tensor, ds=dest_stride: e.tensor_copy(
                    out=bass.AP(tensor=d[:].tensor, offset=128 * r,
                                ap=[list(d[:].ap[0]), [ds, 8], [1, 128]]),
                    in_=psB[sl][:].rearrange("p (a b) -> p a b", a=8)), ev)
                tok_psB[sl] = t_ev

        if stop_after != "consts":
            prep_attn_w(lambda r: wval[128 * r : 128 * r + 128, :], 8, wvT, C)
            prep_attn_w(lambda r: cprojw[128 * r : 128 * r + 128, :], 8, cpT, C)
            prep_attn_w(lambda r: wdisp[128 * r : 128 * r + 128, :], 2, wdT, 256)
        fence()
        tok_w32 = [0, 0]
        tok_psB = [0, 0]

        nzc = (NH * T * PAD) // (128 * 2048)
        t_zwp = 0
        for z in range(nzc):
            t_zwp = P.dma("sp", lambda e, z=z: e.dma_start(
                out=bass.AP(tensor=wp, offset=z * 128 * 2048, ap=[[2048, 128], [1, 2048]]),
                in_=df[:, 0:2048]), dsem("z"))

        def w_tile_ap(mat, ti):
            if mat == 0:
                return wgate[128 * ti : 128 * ti + 128, :]
            if mat == 1:
                return wup[128 * ti : 128 * ti + 128, :]
            r, q = divmod(ti, 4)
            return wdown[128 * r : 128 * r + 128, 1024 * q : 1024 * q + 1024]

        def w8_dst(mat, ti):
            if mat < 2:
                t8 = wg8 if mat == 0 else wu8
                return bass.AP(tensor=t8, offset=128 * ti,
                               ap=[[INNER, 128], [128 * INNER, 8], [1, 128]])
            r, q = divmod(ti, 4)
            return bass.AP(tensor=wd8, offset=(8 * q) * (128 * C) + 128 * r,
                           ap=[[C, 128], [128 * C, 8], [1, 128]])

        wb_on = stop_after != "consts" and stop_after != "wa"
        n_wbc = 192 if wb_on else 0
        t_red = 0
        t_scal = 0

        def wbc_chunk(k):
            nonlocal t_red, t_scal
            mat, ti, half = wcc_idx(k)
            csl = k % 2
            t_my = tld_wcc[csl]
            P.wait("ac", dsem(f"q{csl}"), t_my)
            if 2 * ti + half < 4:
                P.wait("ac", ev, t_red)
            t_abs = P.op("ac", lambda e, ti=ti, half=half, csl=csl: e.activation(
                out=junk[:, 0:512],
                in_=f32a[:, F_sc + 512 * csl : F_sc + 512 * csl + 512],
                func=AF.Abs, accum_out=stats[:, 64 + 2 * ti + half : 65 + 2 * ti + half]), ea)
            tok_w32c[csl] = t_abs
            if k + 2 < n_wbc:
                wcc_load(k + 2)
            elif k + 2 < n_wbc + 2 and n_wcc:
                wcc_load(k + 2 - n_wbc)
            if ti == 31 and half == 1:
                P.wait("dv", ea, t_abs)
                t_red = P.op("dv", lambda e: e.tensor_reduce(
                    out=stats[:, 30:31], in_=stats[:, 64:128], axis=AX.X, op=ALU.add), ev)
                P.wait("pe", ev, t_red)
                t_mm = P.op("pe", lambda e, mat=mat: e.matmul(
                    out=psC[0][0:1, mat : mat + 1], lhsT=onescol[:],
                    rhs=stats[:, 30:31], start=True, stop=True), ep)
                P.wait("dv", ep, t_mm)
                P.op("dv", lambda e, mat=mat: e.tensor_scalar(
                    out=stats[0:1, 60 + mat : 61 + mat], in0=psC[0][0:1, mat : mat + 1],
                    scalar1=1.0 / (INNER * C), scalar2=1e-5, op0=ALU.mult, op1=ALU.max), ev)
                if mat == 2:
                    t_mus = P.n(ev)
                    P.wait("pe", ev, t_mus)
                    t_mm2 = P.op("pe", lambda e: e.matmul(out=psC[1][0:128, 0:3], lhsT=ones32[0:1, :],
                                                          rhs=stats[0:1, 60:63],
                                                          start=True, stop=True), ep)
                    P.wait("dv", ep, t_mm2)
                    P.op("dv", lambda e: e.tensor_copy(out=scal[:, 6:9], in_=psC[1][0:128, 0:3]), ev)
                    P.op("dv", lambda e: e.tensor_scalar(out=scal[:, 0:3], in0=psC[1][0:128, 0:3],
                                                         scalar1=0.5, scalar2=None, op0=ALU.mult), ev)
                    P.op("dv", lambda e: e.tensor_scalar(out=scal[:, 3:6], in0=psC[1][0:128, 0:3],
                                                         scalar1=-0.5, scalar2=None, op0=ALU.mult), ev)
                    t_scal = P.n(ev)

        wc_on = stop_after not in ("consts", "wa", "wb")
        n_wcc = 192 if wc_on else 0
        tok_w32c = [0, 0]
        tok_terc = [0, 0]
        tok_psB1 = [0, 0]
        tok_stgc = [0, 0]
        tld_wcc = [0, 0]

        def ap_mid(a, i0, n):
            return bass.AP(tensor=a.tensor, offset=a.offset + a.ap[1][0] * i0,
                           ap=[list(a.ap[0]), [a.ap[1][0], n]] + [list(d) for d in a.ap[2:]])

        def wcc_idx(k):
            mat = k // 64
            ti = (k % 64) // 2
            half = k % 2
            return mat, ti, half

        def wcc_load(k):
            mat, ti, half = wcc_idx(k)
            csl = k % 2
            P.wait("pl", ea, tok_w32c[csl])
            for r0, r1 in splits(2):
                tld_wcc[csl] = P.dma("pl", lambda e, mat=mat, ti=ti, half=half, csl=csl, r0=r0, r1=r1: e.dma_start(
                    out=f32a[r0:r1, F_sc + 512 * csl : F_sc + 512 * csl + 512],
                    in_=ap_cols(ap_rows(w_tile_ap(mat, ti), r0, r1), half)), dsem(f"q{csl}"))

        def ap_cols(a, half):
            return bass.AP(tensor=a.tensor, offset=a.offset + 512 * half,
                           ap=[list(a.ap[0]), [1, 512]])

        def wcc_chunk(k):
            mat, ti, half = wcc_idx(k)
            csl = k % 2
            t_my_ld = tld_wcc[csl]
            P.wait("ac", dsem(f"q{csl}"), t_my_ld)
            if k < 2 and n_wbc:
                P.wait("ac", ev, t_scal)
            P.wait("ac", ep, tok_terc[csl])
            P.op("ac", lambda e, csl=csl, mat=mat: e.activation(
                out=ternb[:, 512 * csl : 512 * csl + 512],
                in_=f32a[:, F_sc + 512 * csl : F_sc + 512 * csl + 512],
                func=AF.Sign, bias=scal[:, 3 + mat : 4 + mat]), ea)
            t_b = P.op("ac", lambda e, csl=csl, mat=mat: e.activation(
                out=big[:, 5120 + 512 * csl : 5120 + 512 * csl + 512],
                in_=f32a[:, F_sc + 512 * csl : F_sc + 512 * csl + 512],
                func=AF.Sign, bias=scal[:, mat : mat + 1]), ea)
            tok_w32c[csl] = t_b
            if k + 2 < n_wcc:
                wcc_load(k + 2)
            P.wait("dv", ea, t_b)
            t_t = P.op("dv", lambda e, csl=csl: e.tensor_tensor(
                out=ternb[:, 512 * csl : 512 * csl + 512],
                in0=ternb[:, 512 * csl : 512 * csl + 512],
                in1=big[:, 5120 + 512 * csl : 5120 + 512 * csl + 512], op=ALU.add), ev)
            P.wait("pe", ev, t_t)
            P.wait("pe", ev, tok_psB1[csl])
            t_tr = None
            for fc2 in range(4):
                t_tr = P.op("pe", lambda e, csl=csl, fc2=fc2: e.transpose(
                    out=psB[1][:, 512 * csl + 128 * fc2 : 512 * csl + 128 * fc2 + 128],
                    in_=ternb[:, 512 * csl + 128 * fc2 : 512 * csl + 128 * fc2 + 128],
                    identity=identb[:]), ep)
            tok_terc[csl] = t_tr
            P.wait("dv", ep, t_tr)
            P.wait("dv", dsem(f"st{csl}"), tok_stgc[csl])
            t_ev = P.op("dv", lambda e, csl=csl: e.tensor_copy(
                out=fp8a[:, 32768 + 512 * csl : 32768 + 512 * csl + 512],
                in_=psB[1][:, 512 * csl : 512 * csl + 512]), ev)
            tok_psB1[csl] = t_ev
            P.wait("sp", ev, t_ev)
            for r0, r1 in splits(2):
                tok_stgc[csl] = P.dma("sp", lambda e, mat=mat, ti=ti, half=half, csl=csl, r0=r0, r1=r1: e.dma_start(
                    out=ap_rows(ap_mid(w8_dst(mat, ti), 4 * half, 4), r0, r1),
                    in_=fp8a[r0:r1, 32768 + 512 * csl : 32768 + 512 * csl + 512]), dsem(f"st{csl}"))

        fence()
        tok_w32 = [0, 0]
        tok_psB = [0, 0]

        tok_xb = [0, 0]
        tok_h1b = 0
        tok_h1fm = 0
        tok_psA01 = 0
        tok_psA2 = 0
        tok_psA3 = [0, 0]
        tok_dispb = 0
        tok_vstg = [0, 0]
        n_p1 = NT if stop_after not in ("consts", "wa", "wb", "wc") else 0
        tld_x = [0, 0]

        def p1_load(g):
            sl = g % 2
            P.wait("sp", ev, tok_xb[sl])
            for r0, r1 in splits(4):
                tld_x[sl] = P.dma("sp", lambda e, g=g, sl=sl, r0=r0, r1=r1: e.dma_start(
                    out=f32a[r0:r1, F_xb + 1024 * sl : F_xb + 1024 * sl + 1024],
                    in_=x_in[128 * g + r0 : 128 * g + r1, :]), dsem(f"xb{sl}"))

        t_rs_g = [0, 0]

        def p1_stats(g):
            sl = g % 2
            sp2 = 2 + 8 * (g % 2)
            t_x = tld_x[sl]

            def xt_ap(sl=sl):
                return f32a[:, F_xb + 1024 * sl : F_xb + 1024 * sl + 1024]

            P.wait("dv", dsem(f"xb{sl}"), t_x)
            P.op("dv", lambda e, sl=sl, sp2=sp2: e.tensor_reduce(
                out=stats[:, sp2 : sp2 + 1], in_=xt_ap(sl), axis=AX.X, op=ALU.add), ev)
            P.wait("ac", dsem(f"xb{sl}"), t_x)
            t_sq = P.op("ac", lambda e, sl=sl, sp2=sp2: e.activation(
                out=junk[:], in_=xt_ap(sl), func=AF.Square, accum_out=stats[:, sp2 + 1 : sp2 + 2]), ea)
            P.op("dv", lambda e, sp2=sp2: e.tensor_scalar(out=stats[:, sp2 + 2 : sp2 + 3], in0=stats[:, sp2 : sp2 + 1],
                                                 scalar1=-1.0 / C, scalar2=None, op0=ALU.mult), ev)
            P.wait("dv", ea, t_sq)
            P.op("dv", lambda e, sp2=sp2: e.tensor_scalar(out=stats[:, sp2 + 3 : sp2 + 4], in0=stats[:, sp2 + 1 : sp2 + 2],
                                                 scalar1=1.0 / C, scalar2=None, op0=ALU.mult), ev)
            P.op("dv", lambda e, sp2=sp2: e.tensor_tensor(out=stats[:, sp2 + 4 : sp2 + 5], in0=stats[:, sp2 + 2 : sp2 + 3],
                                                 in1=stats[:, sp2 + 2 : sp2 + 3], op=ALU.mult), ev)
            t_var = P.op("dv", lambda e, sp2=sp2: e.tensor_tensor(out=stats[:, sp2 + 5 : sp2 + 6], in0=stats[:, sp2 + 3 : sp2 + 4],
                                                         in1=stats[:, sp2 + 4 : sp2 + 5], op=ALU.subtract), ev)
            P.wait("ac", ev, t_var)
            P.op("ac", lambda e, sp2=sp2: e.activation(out=stats[:, sp2 + 6 : sp2 + 7], in_=stats[:, sp2 + 5 : sp2 + 6],
                                              func=AF.Ln, bias=epscol[:, 0:1]), ea)
            t_rs_g[g % 2] = P.op("ac", lambda e, sp2=sp2: e.activation(
                out=stats[:, sp2 + 7 : sp2 + 8], in_=stats[:, sp2 + 6 : sp2 + 7],
                func=AF.Exp, scale=-0.5), ea)

        n_units = n_wbc + n_wcc

        def emit_unit(u):
            if u < n_wbc:
                wbc_chunk(u)
            else:
                wcc_chunk(u - n_wbc)

        if n_p1:
            p1_load(0)
        if n_wbc:
            wcc_load(0)
            wcc_load(1)
        elif n_wcc:
            wcc_load(0)
            wcc_load(1)
        if n_p1:
            p1_stats(0)
        for g in range(n_p1):
            sl = g % 2
            sp2 = 2 + 8 * (g % 2)
            if g + 1 < n_p1:
                p1_load(g + 1)
                p1_stats(g + 1)
            for _j in range(12):
                if 12 * g + _j < n_units:
                    emit_unit(12 * g + _j)

            def xt_ap(sl=sl):
                return f32a[:, F_xb + 1024 * sl : F_xb + 1024 * sl + 1024]

            t_rs = t_rs_g[g % 2]
            if p1_lvl < 2:
                continue
            P.wait("dv", ea, t_rs)
            P.wait("dv", ep, tok_h1b)
            t_h1 = P.op("dv", lambda e, sl=sl, sp2=sp2: e.tensor_scalar(
                out=big[:, B_h1buf : B_h1buf + 1024], in0=xt_ap(sl),
                scalar1=stats[:, sp2 + 2 : sp2 + 3], scalar2=stats[:, sp2 + 7 : sp2 + 8],
                op0=ALU.add, op1=ALU.mult), ev)
            tok_xb[sl] = t_h1
            P.wait("pe", ev, t_h1)
            P.wait("pe", ev, tok_psB[0])
            t_tr = None
            for fc in range(8):
                t_tr = P.op("pe", lambda e, fc=fc: e.transpose(
                    out=psB[0][:, 128 * fc : 128 * fc + 128],
                    in_=big[:, B_h1buf + 128 * fc : B_h1buf + 128 * fc + 128],
                    identity=identb[:]), ep)
            tok_h1b = t_tr
            P.wait("dv", ep, t_tr)
            P.wait("dv", ep, tok_h1fm)
            t_fm = P.op("dv", lambda e: e.tensor_copy(
                out=big[:, B_h1fm : B_h1fm + 1024], in_=psB[0][:]), ev)
            tok_psB[0] = t_fm
            if p1_lvl < 3:
                continue
            P.wait("pe", ev, t_fm)
            P.wait("pe", ev, tok_psA01)
            for nck in range(2):
                for fc in range(8):
                    P.op("pe", lambda e, fc=fc, nck=nck: e.matmul(
                        out=psA[nck][:, :],
                        lhsT=big[:, B_h1fm + 128 * fc : B_h1fm + 128 * fc + 128],
                        rhs=wvT[:, C * fc + 512 * nck : C * fc + 512 * nck + 512],
                        start=(fc == 0), stop=(fc == 7)), ep)
            t_vmm = P.n(ep)
            if p1_lvl < 4:
                continue
            P.wait("pe", ev, tok_psA2)
            for oc in range(2):
                for fc in range(8):
                    P.op("pe", lambda e, oc=oc, fc=fc: e.matmul(
                        out=psA[2][:, 128 * oc : 128 * oc + 128],
                        lhsT=wdT[:, 256 * fc + 128 * oc : 256 * fc + 128 * oc + 128],
                        rhs=big[:, B_h1fm + 128 * fc : B_h1fm + 128 * fc + 128],
                        start=(fc == 0), stop=(fc == 7)), ep)
            t_dmm = P.n(ep)
            tok_h1fm = t_dmm
            P.wait("dv", ep, t_vmm)
            P.wait("dv", dsem(f"vs{sl}"), tok_vstg[sl])
            P.op("dv", lambda e, sl=sl: e.tensor_copy(
                out=big[:, B_vstg + 1024 * sl : B_vstg + 1024 * sl + 512], in_=psA[0][:]), ev)
            t_vst = P.op("dv", lambda e, sl=sl: e.tensor_copy(
                out=big[:, B_vstg + 1024 * sl + 512 : B_vstg + 1024 * sl + 1024],
                in_=psA[1][:]), ev)
            tok_psA01 = t_vst
            P.wait("sp", ev, t_vst)
            t_vd = P.dma("sp", lambda e, g=g, sl=sl: e.dma_start(
                out=bass.AP(tensor=vals, offset=g * 128 * C, ap=[[C, 128], [1, C]]),
                in_=big[:, B_vstg + 1024 * sl : B_vstg + 1024 * sl + 1024]), dsem(f"vs{sl}"))
            if dbg:
                P.wait("pl", ev, t_vst)
                t_dbg = P.dma("pl", lambda e, g=g, sl=sl: e.dma_start(
                    out=dbg_t["d_val"][128 * g : 128 * g + 128, :],
                    in_=big[:, B_vstg + 1024 * sl : B_vstg + 1024 * sl + 1024]), dsem("dbg"))
                P.wait("dv", dsem("dbg"), t_dbg)
            tok_vstg[sl] = t_vd
            if p1_lvl < 5:
                continue
            P.wait("dv", ep, t_dmm)
            P.wait("dv", ep, tok_dispb)
            t_de = P.op("dv", lambda e: e.tensor_copy(
                out=big[:, B_disp : B_disp + 256], in_=psA[2][:, 0:256]), ev)
            tok_psA2 = t_de
            t_mmdf_last = 0
            P.wait("pe", ev, t_de)
            P.wait("pe", ev, tok_psA3[0])
            for hg in range(4 if p1_lvl >= 6 else 0):
                oc, hh = hg // 2, hg % 2
                t_mmdf_last = P.op("pe", lambda e, oc=oc, hh=hh, hg=hg: e.matmul(
                    out=psA[3][:, 128 * hg : 128 * hg + 128],
                    lhsT=dfwb[:, 128 * hh : 128 * hh + 128],
                    rhs=big[:, B_disp + 128 * oc : B_disp + 128 * oc + 128],
                    start=True, stop=True), ep)
            if p1_lvl >= 7:
                P.wait("dv", ep, t_mmdf_last)
                if g == 0:
                    P.wait("dv", dsem("z"), t_zwp)
                t_dfe = P.op("dv", lambda e, g=g: e.tensor_copy(
                    out=bass.AP(tensor=df[:].tensor, offset=64 + 128 * g,
                                ap=[list(df[:].ap[0]), [DF1, 4], [1, 128]]),
                    in_=psA[3][:]), ev)
                tok_psA3[0] = t_dfe
            if p1_lvl >= 6:
                tok_dispb = t_mmdf_last
        for _k in range(12 * n_p1, n_units):
            emit_unit(_k)
        if dbg:
            P.wait("pl", ev)
            P.dma("pl", lambda e: e.dma_start(out=dbg_t["d_df"], in_=df[:, 0 : 4 * DF1]), dsem("dbg"))
        fence()

        tok_args = [0, 0, 0, 0]
        tok_gargs = [0, 0, 0]
        tok_psA_q = [0, 0, 0, 0]
        tok_psA_j0 = [0, 0, 0, 0]
        tok_blob = [0, 0]
        tok_ltm_w = 0
        tok_ltm_r = 0
        tok_ebuf = [0, 0]
        tok_ptm = [0, 0]
        tok_btb = [0, 0]
        tok_bb = [0, 0]
        tok_attn = [0, 0]
        tok_psC_ev = 0
        tok_psC_ac = 0
        tok_xres = [0, 0]
        tok_x2b = [0, 0]
        tok_vring = [0, 0, 0]

        if stop_after == "p2w":
            NSUBX = 0
        _stopped2 = stop_after in ("consts", "wa", "wb", "wc", "p1")
        for S in range(NSUP2 if not _stopped2 else 0):
            T0 = S * P2S
            for hg in range(4):
                dfo = DF1 * hg
                t_diff = [0, 0, 0, 0]
                t_eb_prev = 0

                def blob_ap(q2, r0, r1, c0, c1):
                    if q2 % 2 == 0:
                        return big[r0:r1, B_blob + c0 : B_blob + c1]
                    return ternb[r0:r1, c0:c1]

                def wl_tail(q2, t_eb2):
                    nonlocal tok_ltm_r
                    P.wait("pe", ev, t_eb2)
                    P.wait("pe", ev, tok_psB[0])
                    P.wait("pe", ev, tok_psB[1])
                    t_tr = None
                    for tc in range(NQC):
                        for i in range(4):
                            t_tr = P.op("pe", lambda e, tc=tc, i=i, q2=q2: e.transpose(
                                out=psB[tc % 2][:, 512 * (tc // 2) + 128 * i : 512 * (tc // 2) + 128 * i + 128],
                                in_=blob_ap(q2, 0, 128, 512 * tc + 128 * i, 512 * tc + 128 * i + 128),
                                identity=identb[:]), ep)
                    tok_blob[q2 % 2] = t_tr
                    P.wait("dv", ep, t_tr)
                    if q2 == 0:
                        P.wait("dv", ea, tok_ltm_r)
                    t_cp2 = None
                    for tc in range(NQC):
                        t_cp2 = P.op("dv", lambda e, tc=tc, q2=q2: e.tensor_copy(
                            out=bass.AP(tensor=wvT[:].tensor,
                                        offset=512 * (tc * 4) + 32 * q2,
                                        ap=[list(wvT[:].ap[0]), [512, 4], [8, 4], [1, 8]]),
                            in_=bass.AP(tensor=psB[tc % 2][:].tensor,
                                        offset=512 * (tc // 2),
                                        ap=[list(psB[tc % 2][:].ap[0]), [128, 4], [32, 4], [1, 8]])), ev)
                    tok_psB[0] = t_cp2
                    tok_psB[1] = t_cp2

                def w_diff2(q2, h2, dfo=dfo, T0=T0):
                    for j2 in (0, 1) if h2 == 0 else (2, 3):
                        w2 = 4 * q2 + j2
                        P.wait("dv", ea, tok_args[j2])
                        t_diff[j2] = P.op("dv", lambda e, w2=w2, j2=j2, dfo=dfo, T0=T0: e.tensor_tensor(
                            out=big[:, B_args + 2048 * j2 : B_args + 2048 * j2 + P2S],
                            in0=df[:, dfo + 1 + T0 + w2 : dfo + 1 + T0 + w2 + P2S],
                            in1=df[:, dfo + 64 + T0 : dfo + 64 + T0 + P2S],
                            op=ALU.subtract), ev)

                def w_diff(q2, dfo=dfo, T0=T0):
                    for j2 in range(4):
                        w2 = 4 * q2 + j2
                        P.wait("dv", ea, tok_args[j2])
                        t_diff[j2] = P.op("dv", lambda e, w2=w2, j2=j2, dfo=dfo, T0=T0: e.tensor_tensor(
                            out=big[:, B_args + 2048 * j2 : B_args + 2048 * j2 + P2S],
                            in0=df[:, dfo + 1 + T0 + w2 : dfo + 1 + T0 + w2 + P2S],
                            in1=df[:, dfo + 64 + T0 : dfo + 64 + T0 + P2S],
                            op=ALU.subtract), ev)

                if p2_lvl >= 0:
                    w_diff(0)
                for q in range(16 if p2_lvl >= 1 else 0):
                    t_lm = None
                    for j in range(4):
                        w = 4 * q + j
                        gsl = w % 3
                        P.wait("ac", ev, t_diff[j])
                        P.wait("ac", ep, tok_gargs[gsl])
                        t_gel = P.op("ac", lambda e, w=w, j=j, gsl=gsl: e.activation(
                            out=big[:, B_gargs + 2048 * gsl : B_gargs + 2048 * gsl + P2S],
                            in_=big[:, B_args + 2048 * j : B_args + 2048 * j + P2S],
                            func=AF.Gelu, bias=c_all[:, w : w + 1]), ea)
                        tok_args[j] = t_gel
                        if p2_lvl < 2:
                            tok_gargs[gsl] = 0
                            continue
                        P.wait("pe", ea, t_gel)
                        for tc in range(NQC):
                            if j == 0:
                                P.wait("pe", ev, tok_psA_j0[tc])
                            elif j == 1:
                                P.wait("pe", ev, tok_psA_q[tc])
                            t_lm = P.op("pe", lambda e, gsl=gsl, tc=tc, j=j: e.matmul(
                                out=psA[tc][32 * j : 32 * j + 8, :],
                                lhsT=logit_w[:, :],
                                rhs=big[:, B_gargs + 2048 * gsl + 512 * tc : B_gargs + 2048 * gsl + 512 * tc + 512],
                                start=True, stop=True, tile_position=(0, 32 * j)), ep)
                        tok_gargs[gsl] = t_lm
                    if q + 1 < 16:
                        w_diff2(q + 1, 0)
                    if p2_lvl >= 3:
                        P.wait("dv", ep, t_lm)
                        P.wait("dv", ep, tok_blob[q % 2])
                        if hg == 0 and q == 0:
                            P.wait("dv", dsem("x20"), tok_x2b[0])
                            P.wait("dv", dsem("x21"), tok_x2b[1])
                        t_eb = None
                        for tc in range(NQC):
                            t_eb = P.op("dv", lambda e, tc=tc, q=q: e.tensor_copy(
                                out=blob_ap(q, 0, 128, 512 * tc, 512 * tc + 512),
                                in_=psA[tc][:]), ev)
                        for tc in range(NQC):
                            tok_psA_j0[tc] = t_eb
                            tok_psA_q[tc] = t_eb
                        if q + 1 < 16:
                            w_diff2(q + 1, 1)
                        if q > 0:
                            wl_tail(q - 1, t_eb_prev)
                        t_eb_prev = t_eb
                    elif q + 1 < 16:
                        w_diff2(q + 1, 1)
                tok_ltm_w_pending = True
                if p2_lvl >= 3:
                    wl_tail(15, t_eb_prev)
                tok_ltm_w = P.n(ev)
                if dbg and S == 0 and hg == 0:
                    P.wait("pl", ev, tok_ltm_w)
                    t_dl = P.dma("pl", lambda e: e.dma_start(out=dbg_t["d_ltm"], in_=wvT[:, 0 : 16 * 512]), dsem("dbg"))
                    P.wait("dv", dsem("dbg"), t_dl)
                for s_sub in range(NSUB if stop_after != "p2w" else 0):
                    sl = s_sub % 2
                    lt_off = 512 * s_sub
                    P.wait("ac", ev, tok_ltm_w)
                    t_sg = P.op("ac", lambda e, lt_off=lt_off: e.activation(
                        out=f32a[:, F_sc + 4 : F_sc + 260],
                        in_=bass.AP(tensor=wvT[:].tensor, offset=lt_off + 4,
                                    ap=[list(wvT[:].ap[0]), [8, WIN], [1, 4]]),
                        func=AF.Tanh, scale=0.5, bias=dbcol[:]), ea)
                    tok_ltm_r = t_sg
                    P.wait("dv", ea, t_sg)
                    t_lg = P.op("dv", lambda e, lt_off=lt_off: e.scalar_tensor_tensor(
                        out=f32a[:, F_sc + 260 : F_sc + 516],
                        in0=f32a[:, F_sc + 4 : F_sc + 260],
                        scalar=-5.0, op0=ALU.mult,
                        in1=bass.AP(tensor=wvT[:].tensor, offset=lt_off,
                                    ap=[list(wvT[:].ap[0]), [8, WIN], [1, 4]]),
                        op1=ALU.add), ev)
                    if S == 0 and s_sub == 0:
                        t_lg = P.op("dv", lambda e: e.tensor_tensor(
                            out=f32a[:, F_sc + 260 : F_sc + 516],
                            in0=f32a[:, F_sc + 260 : F_sc + 516],
                            in1=bass.AP(tensor=maskneg[:].tensor, offset=0,
                                        ap=[list(maskneg[:].ap[0]), [1, WIN], [0, 4]]),
                            op=ALU.add), ev)
                    P.op("dv", lambda e: e.tensor_reduce(
                        out=stats[:, 16:20],
                        in_=bass.AP(tensor=f32a[:].tensor, offset=F_sc + 260,
                                    ap=[list(f32a[:].ap[0]), [1, 4], [4, WIN]]),
                        axis=AX.X, op=ALU.max, negate=True), ev)
                    t_ei = P.op("dv", lambda e: e.tensor_tensor(
                        out=f32a[:, F_sc + 260 : F_sc + 516],
                        in0=f32a[:, F_sc + 260 : F_sc + 516],
                        in1=bass.AP(tensor=stats[:].tensor, offset=16,
                                    ap=[list(stats[:].ap[0]), [0, WIN], [1, 4]]),
                        op=ALU.add), ev)
                    P.wait("ac", ev, t_ei)
                    P.wait("ac", ev, tok_ebuf[sl])
                    t_ex = P.op("ac", lambda e, sl=sl: e.activation(
                        out=big[:, B_ebuf + 256 * sl : B_ebuf + 256 * sl + 256],
                        in_=f32a[:, F_sc + 260 : F_sc + 516], func=AF.Exp), ea)
                    P.wait("dv", ea, t_ex)
                    P.op("dv", lambda e, sl=sl: e.tensor_reduce(
                        out=stats[:, 20:24],
                        in_=bass.AP(tensor=big[:].tensor, offset=B_ebuf + 256 * sl,
                                    ap=[list(big[:].ap[0]), [1, 4], [4, WIN]]),
                        axis=AX.X, op=ALU.add), ev)
                    P.op("dv", lambda e: e.reciprocal(out=stats[:, 24:28], in_=stats[:, 20:24]), ev)
                    P.wait("dv", dsem(f"sk{sl}"), tok_ptm[sl])
                    t_p = P.op("dv", lambda e, sl=sl: e.tensor_tensor(
                        out=bass.AP(tensor=big[:].tensor, offset=B_ptm + 256 * sl,
                                    ap=[list(big[:].ap[0]), [1, WIN], [WIN, 4]]),
                        in0=big[:, B_ebuf + 256 * sl : B_ebuf + 256 * sl + 256],
                        in1=bass.AP(tensor=stats[:].tensor, offset=24,
                                    ap=[list(stats[:].ap[0]), [0, WIN], [1, 4]]),
                        op=ALU.mult), ev)
                    tok_ebuf[sl] = t_p
                    if dbg and S == 0 and hg == 0:
                        P.wait("pl", ev, t_p)
                        t_dbg = P.dma("pl", lambda e, s_sub=s_sub, sl=sl: e.dma_start(
                            out=dbg_t["d_ptm"][:, 256 * s_sub : 256 * s_sub + 256],
                            in_=big[:, B_ptm + 256 * sl : B_ptm + 256 * sl + 256]), dsem("dbg"))
                        P.wait("dv", dsem("dbg"), t_dbg)
                    P.wait("sp", ev, t_p)
                    if S == 0 and hg == 0 and s_sub == sl:
                        P.wait("sp", dsem("z"), t_zwp)
                    for r0, r1 in splits(4):
                        tok_ptm[sl] = P.dma("sp", lambda e, hg=hg, s_sub=s_sub, sl=sl, T0=T0, r0=r0, r1=r1: e.dma_start(
                            out=bass.AP(tensor=wp,
                                        offset=(T0 + 128 * s_sub + r0) * NH * PAD + (hg * 4) * PAD + 65 + r0,
                                        ap=[[NH * PAD + 1, r1 - r0], [PAD, 4], [1, WIN]]),
                            in_=big[r0:r1, B_ptm + 256 * sl : B_ptm + 256 * sl + 256].rearrange(
                                "p (a b) -> p a b", a=4)), dsem(f"sk{sl}"))
            t_skew_all = (P.n(dsem("sk0")), P.n(dsem("sk1")))
            nsub_b = NSUB if stop_after not in ("p2a", "p2w") else 0
            tbt = [0, 0]
            tvl = [0, 0, 0]
            txr = [0, 0]

            def bl_load(bs2, T0=T0):
                gb = (T0 // 128) + bs2
                t0 = 128 * gb
                bb_ = gb % 2
                btbase = B_btb if bb_ == 0 else B_btb2
                P.wait("sp", dsem("sk0"), t_skew_all[0])
                P.wait("sp", dsem("sk1"), t_skew_all[1])
                P.wait("sp", ep, tok_btb[bb_])
                for r0, r1 in splits(2):
                    tbt[bb_] = P.dma("sp", lambda e, t0=t0, r0=r0, r1=r1, btbase=btbase: e.dma_start(
                        out=big[r0:r1, btbase : btbase + 4096],
                        in_=bass.AP(tensor=wp, offset=(t0 + r0) * NH * PAD,
                                    ap=[[NH * PAD, r1 - r0], [1, NH * PAD]])), dsem(f"bt{bb_}"))
                vsl = gb % 3
                P.wait("sp", ep, tok_vring[vsl])
                for r0, r1 in splits(2):
                    tvl[vsl] = P.dma("sp", lambda e, gb=gb, vsl=vsl, r0=r0, r1=r1: e.dma_start(
                        out=big[r0:r1, B_vring + 1024 * vsl : B_vring + 1024 * vsl + 1024],
                        in_=bass.AP(tensor=vals, offset=(gb * 128 + r0) * C,
                                    ap=[[C, r1 - r0], [1, C]])), dsem(f"vr{vsl}"))
                xsl = gb % 2
                P.wait("sp", ev, tok_xres[xsl])
                for r0, r1 in splits(4):
                    txr[xsl] = P.dma("sp", lambda e, t0=t0, xsl=xsl, r0=r0, r1=r1: e.dma_start(
                        out=f32a[r0:r1, F_xb + 1024 * xsl : F_xb + 1024 * xsl + 1024],
                        in_=x_in[t0 + r0 : t0 + r1, :]), dsem(f"xr{xsl}"))

            if nsub_b:
                bl_load(0)
            for bs_ in range(nsub_b):
                gb = (T0 // 128) + bs_
                t0 = 128 * gb
                bb_ = gb % 2
                btbase = B_btb if bb_ == 0 else B_btb2
                if bs_ + 1 < nsub_b:
                    bl_load(bs_ + 1)
                vsl = gb % 3
                xsl = gb % 2
                t_bt = tbt[bb_]
                t_vl = tvl[vsl]
                t_xr = txr[xsl]
                t_band = None
                for g2 in range(2):
                    P.wait("pe", dsem(f"bt{bb_}"), t_bt)
                    P.wait("pe", ev, tok_psB[0])
                    P.wait("pe", ev, tok_psB[1])
                    t_tr = None
                    for k in range(8):
                        h = 8 * g2 + k
                        for half in range(2):
                            t_tr = P.op("pe", lambda e, h=h, k=k, half=half, btbase=btbase: e.transpose(
                                out=psB[k % 2][:, 256 * (k // 2) + 128 * half : 256 * (k // 2) + 128 * half + 128],
                                in_=big[:, btbase + 256 * h + 128 * half : btbase + 256 * h + 128 * half + 128],
                                identity=identb[:]), ep)
                    if g2 == 1:
                        tok_btb[bb_] = t_tr
                    P.wait("dv", ep, t_tr)
                    P.wait("dv", ep, tok_bb[0])
                    P.op("dv", lambda e: e.tensor_copy(
                        out=big[:, B_bb : B_bb + 1024], in_=psB[0][:]), ev)
                    t_bbe = P.op("dv", lambda e: e.tensor_copy(
                        out=big[:, B_bb + 1024 : B_bb + 2048], in_=psB[1][:]), ev)
                    tok_psB[0] = t_bbe
                    tok_psB[1] = t_bbe
                    if p2_lvl < 5:
                        continue
                    P.wait("pe", ev, t_bbe)
                    P.wait("pe", dsem(f"vr{vsl}"), t_vl)
                    if g2 == 0:
                        P.wait("pe", ev, tok_psC_ev)
                        P.wait("pe", ea, tok_psC_ac)
                    for k in range(8):
                        h = 8 * g2 + k
                        bank = psC[g2]
                        r0 = 64 * (h % 2)
                        c0 = 128 * ((h // 2) % 4)
                        boff = B_bb + 1024 * (k % 2) + 256 * (k // 2)
                        if gb == 0:
                            t_band = P.op("pe", lambda e, h=h, bank=bank, r0=r0, c0=c0, vsl=vsl, boff=boff: e.matmul(
                                out=bank[r0 : r0 + 64, c0 : c0 + 128],
                                lhsT=big[:, B_vring + 1024 * vsl + 64 * h : B_vring + 1024 * vsl + 64 * h + 64],
                                rhs=big[:, boff + 128 : boff + 256],
                                start=True, stop=True), ep)
                        else:
                            pv = (gb + 2) % 3
                            P.op("pe", lambda e, h=h, bank=bank, r0=r0, c0=c0, pv=pv, boff=boff: e.matmul(
                                out=bank[r0 : r0 + 64, c0 : c0 + 128],
                                lhsT=big[:, B_vring + 1024 * pv + 64 * h : B_vring + 1024 * pv + 64 * h + 64],
                                rhs=big[:, boff : boff + 128],
                                start=True, stop=False), ep)
                            t_band = P.op("pe", lambda e, h=h, bank=bank, r0=r0, c0=c0, vsl=vsl, boff=boff: e.matmul(
                                out=bank[r0 : r0 + 64, c0 : c0 + 128],
                                lhsT=big[:, B_vring + 1024 * vsl + 64 * h : B_vring + 1024 * vsl + 64 * h + 64],
                                rhs=big[:, boff + 128 : boff + 256],
                                start=False, stop=True), ep)
                    tok_bb[0] = t_band
                tok_vring[vsl] = t_band
                if gb > 0:
                    tok_vring[(gb + 2) % 3] = t_band
                if p2_lvl < 6:
                    continue
                asl = gb % 2
                P.wait("dv", ep, t_band)
                P.wait("dv", ep, tok_attn[asl])
                P.op("dv", lambda e, asl=asl: e.tensor_copy(
                    out=big[:, B_attn + 1024 * asl : B_attn + 1024 * asl + 512], in_=psC[0][:]), ev)
                t_ae2 = P.op("dv", lambda e, asl=asl: e.tensor_copy(
                    out=big[:, B_attn + 1024 * asl + 512 : B_attn + 1024 * asl + 1024], in_=psC[1][:]), ev)
                P.wait("pe", ev, t_ae2)
                for ncx in range(2):
                    for ch in range(8):
                        P.op("pe", lambda e, ncx=ncx, ch=ch, asl=asl: e.matmul(
                            out=psC[ncx][:, :],
                            lhsT=big[:, B_attn + 1024 * asl + 128 * ch : B_attn + 1024 * asl + 128 * ch + 128],
                            rhs=cpT[:, C * ch + 512 * ncx : C * ch + 512 * ncx + 512],
                            start=(ch == 0), stop=(ch == 7)), ep)
                t_cpj = P.n(ep)
                tok_attn[asl] = t_cpj
                if p2_lvl < 7:
                    continue
                P.wait("dv", ep, t_cpj)
                P.op("dv", lambda e: e.tensor_copy(out=f32a[:, F_sc : F_sc + 512], in_=psC[0][:]), ev)
                t_yev = P.op("dv", lambda e: e.tensor_copy(out=f32a[:, F_sc + 512 : F_sc + 1024], in_=psC[1][:]), ev)
                tok_psC_ev = t_yev
                tok_psC_ac = 0
                P.op("dv", lambda e: e.tensor_reduce(out=stats[:, 36:37],
                                                     in_=f32a[:, F_sc : F_sc + 1024],
                                                     axis=AX.X, op=ALU.add), ev)
                P.wait("ac", ev, t_yev)
                t_sq2 = P.op("ac", lambda e: e.activation(out=junk[:], in_=f32a[:, F_sc : F_sc + 1024],
                                                          func=AF.Square, accum_out=stats[:, 37:38]), ea)
                P.wait("dv", ea, t_sq2)
                P.op("dv", lambda e: e.tensor_scalar(out=stats[:, 38:39], in0=stats[:, 36:37],
                                                     scalar1=-1.0 / C, scalar2=None, op0=ALU.mult), ev)
                P.op("dv", lambda e: e.tensor_scalar(out=stats[:, 39:40], in0=stats[:, 37:38],
                                                     scalar1=1.0 / C, scalar2=None, op0=ALU.mult), ev)
                P.op("dv", lambda e: e.tensor_tensor(out=stats[:, 40:41], in0=stats[:, 38:39],
                                                     in1=stats[:, 38:39], op=ALU.mult), ev)
                t_v2 = P.op("dv", lambda e: e.tensor_tensor(out=stats[:, 41:42], in0=stats[:, 39:40],
                                                            in1=stats[:, 40:41], op=ALU.subtract), ev)
                P.wait("ac", ev, t_v2)
                P.op("ac", lambda e: e.activation(out=stats[:, 42:43], in_=stats[:, 41:42],
                                                  func=AF.Ln, bias=epscol[:, 0:1]), ea)
                t_r2 = P.op("ac", lambda e: e.activation(out=stats[:, 43:44], in_=stats[:, 42:43],
                                                         func=AF.Exp, scale=-0.5), ea)
                if p2_lvl < 8:
                    continue
                P.wait("dv", ea, t_r2)
                P.wait("dv", dsem(f"xr{xsl}"), t_xr)
                P.wait("dv", dsem(f"x2{asl}"), tok_x2b[asl])
                t_y1 = P.op("dv", lambda e: e.tensor_scalar(
                    out=f32a[:, F_sc : F_sc + 1024], in0=f32a[:, F_sc : F_sc + 1024],
                    scalar1=stats[:, 38:39], scalar2=stats[:, 43:44],
                    op0=ALU.add, op1=ALU.mult), ev)
                P.wait("dv", ep, tok_blob[0])
                t_x2 = P.op("dv", lambda e, xsl=xsl, asl=asl: e.tensor_tensor(
                    out=big[:, B_x2b + 1024 * asl : B_x2b + 1024 * asl + 1024],
                    in0=f32a[:, F_sc : F_sc + 1024],
                    in1=f32a[:, F_xb + 1024 * xsl : F_xb + 1024 * xsl + 1024],
                    op=ALU.add), ev)
                tok_xres[xsl] = t_x2
                P.wait("sp", ev, t_x2)
                t_x2d = P.dma("sp", lambda e, t0=t0, asl=asl: e.dma_start(
                    out=bass.AP(tensor=x2s, offset=t0 * C, ap=[[C, 128], [1, C]]),
                    in_=big[:, B_x2b + 1024 * asl : B_x2b + 1024 * asl + 1024]), dsem(f"x2{asl}"))
                if dbg:
                    P.wait("pl", ev, t_x2)
                    t_dbg = P.dma("pl", lambda e, t0=t0, asl=asl: e.dma_start(
                        out=dbg_t["d_x2"][t0 : t0 + 128, :],
                        in_=big[:, B_x2b + 1024 * asl : B_x2b + 1024 * asl + 1024]), dsem("dbg"))
                    P.wait("dv", dsem("dbg"), t_dbg)
                tok_x2b[asl] = t_x2d
        fence()

        for r0, r1 in splits(4):
            t_wd = P.dma("sp", lambda e, r0=r0, r1=r1: e.dma_start(
                out=fp8a[r0:r1, 0:32768],
                in_=bass.AP(tensor=wd8, offset=C * r0,
                            ap=[[C, r1 - r0], [128 * C, 32], [1, C]])), dsem("wd"))

        tok_x2t = [0, 0]
        tok_a1i = 0
        tok_a1b = 0
        tok_a1fm = 0
        tok_gw = [0, 0]
        tok_uw = [0, 0]
        tok_gA = [0, 0]
        tok_uA = [0, 0]
        tok_sil = 0
        tok_a2tm = 0
        tok_a2i = 0
        tok_a2fm = 0
        tok_ybuf = 0
        tok_scr_ac = 0
        tok_ob = [0, 0]
        tok_obx = 0
        tok_xtb = 0
        tok_g = 0

        _stopped3 = stop_after in ("consts", "wa", "wb", "wc", "p1", "p2", "p2w", "p2a")
        _NO_ILV = False
        t_ic_done_prev = 0
        t_sg3 = [0] * 8
        t_fm3 = [0] * 8
        t_dn3 = [0] * 8

        def ln4_tt(tt, sb):
            nonlocal tok_psC_ac, tok_ybuf
            P.wait("dv", ep, t_dn3[tt])
            P.op("dv", lambda e, tt=tt: e.tensor_copy(
                out=cpT[:, 1024 * tt : 1024 * tt + 512], in_=psC[0][:]), ev)
            t_ye2 = P.op("dv", lambda e, tt=tt: e.tensor_copy(
                out=cpT[:, 1024 * tt + 512 : 1024 * tt + 1024], in_=psC[1][:]), ev)
            P.op("dv", lambda e, tt=tt, sb=sb: e.tensor_reduce(
                out=stats[:, sb + 56 : sb + 57], in_=cpT[:, 1024 * tt : 1024 * tt + 1024],
                axis=AX.X, op=ALU.add), ev)
            P.wait("ac", ev, t_ye2)
            t_sq4 = P.op("ac", lambda e, tt=tt, sb=sb: e.activation(
                out=junk[:], in_=cpT[:, 1024 * tt : 1024 * tt + 1024],
                func=AF.Square, accum_out=stats[:, sb + 58 : sb + 59]), ea)
            tok_psC_ac = t_sq4
            P.op("dv", lambda e, tt=tt, sb=sb: e.tensor_scalar(out=stats[:, sb + 40 + tt : sb + 41 + tt],
                                                        in0=stats[:, sb + 56 : sb + 57],
                                                        scalar1=-1.0 / C, scalar2=None, op0=ALU.mult), ev)
            P.wait("dv", ea, t_sq4)
            P.op("dv", lambda e, tt=tt, sb=sb: e.tensor_scalar(out=stats[:, sb + 48 + tt : sb + 49 + tt],
                                                        in0=stats[:, sb + 58 : sb + 59],
                                                        scalar1=1.0 / C, scalar2=None, op0=ALU.mult), ev)
            P.op("dv", lambda e, tt=tt, sb=sb: e.tensor_tensor(out=stats[:, sb + 56 : sb + 57],
                                                        in0=stats[:, sb + 40 + tt : sb + 41 + tt],
                                                        in1=stats[:, sb + 40 + tt : sb + 41 + tt], op=ALU.mult), ev)
            P.op("dv", lambda e, tt=tt, sb=sb: e.tensor_tensor(out=stats[:, sb + 48 + tt : sb + 49 + tt],
                                                        in0=stats[:, sb + 48 + tt : sb + 49 + tt],
                                                        in1=stats[:, sb + 56 : sb + 57], op=ALU.subtract), ev)
            tok_ybuf = t_ye2

        def ln3_tt(F2, tt, t_icd):
            nonlocal tok_scr_ac, tok_a1i, tok_a1b
            sb = 64 * (F2 % 2)
            TF2 = F2 * P3S
            sl = tt % 2
            P.wait("sp", ea, tok_x2t[sl])
            for r0, r1 in splits(2):
                t_xl = P.dma("sp", lambda e, TF2=TF2, tt=tt, sl=sl, r0=r0, r1=r1: e.dma_start(
                    out=ternb[r0:r1, 1024 * sl : 1024 * sl + 1024],
                    in_=bass.AP(tensor=x2s, offset=(TF2 + 128 * tt + r0) * C,
                                ap=[[C, r1 - r0], [1, C]])), dsem(f"xt{sl}"))

            def x2t_ap(sl=sl):
                return ternb[:, 1024 * sl : 1024 * sl + 1024]

            P.wait("dv", dsem(f"xt{sl}"), t_xl)
            P.wait("dv", ea, tok_scr_ac)
            P.op("dv", lambda e, sl=sl: e.tensor_reduce(
                out=stats[:, sb + 56 : sb + 57], in_=x2t_ap(sl), axis=AX.X, op=ALU.add), ev)
            P.op("dv", lambda e, sl=sl: e.tensor_reduce(
                out=stats[:, sb + 58 : sb + 59], in_=x2t_ap(sl), axis=AX.X, op=ALU.max), ev)
            P.op("dv", lambda e, sl=sl: e.tensor_reduce(
                out=stats[:, sb + 59 : sb + 60], in_=x2t_ap(sl), axis=AX.X, op=ALU.min, negate=True), ev)
            P.wait("ac", dsem(f"xt{sl}"), t_xl)
            t_sq = P.op("ac", lambda e, sl=sl: e.activation(
                out=junk[:], in_=x2t_ap(sl), func=AF.Square, accum_out=stats[:, sb + 57 : sb + 58]), ea)
            P.op("dv", lambda e, tt=tt: e.tensor_scalar(
                out=stats[:, sb + 0 + tt : sb + 1 + tt], in0=stats[:, sb + 56 : sb + 57],
                scalar1=-1.0 / C, scalar2=None, op0=ALU.mult), ev)
            P.wait("dv", ea, t_sq)
            P.op("dv", lambda e: e.tensor_scalar(out=stats[:, sb + 60 : sb + 61], in0=stats[:, sb + 57 : sb + 58],
                                                 scalar1=1.0 / C, scalar2=None, op0=ALU.mult), ev)
            P.op("dv", lambda e, tt=tt: e.tensor_tensor(out=stats[:, sb + 61 : sb + 62],
                                                        in0=stats[:, sb + 0 + tt : sb + 1 + tt],
                                                        in1=stats[:, sb + 0 + tt : sb + 1 + tt], op=ALU.mult), ev)
            t_v3 = P.op("dv", lambda e: e.tensor_tensor(out=stats[:, sb + 62 : sb + 63], in0=stats[:, sb + 60 : sb + 61],
                                                        in1=stats[:, sb + 61 : sb + 62], op=ALU.subtract), ev)
            P.wait("ac", ev, t_v3)
            P.op("ac", lambda e: e.activation(out=stats[:, sb + 63 : sb + 64], in_=stats[:, sb + 62 : sb + 63],
                                              func=AF.Ln, bias=epscol[:, 0:1]), ea)
            t_r3 = P.op("ac", lambda e, tt=tt: e.activation(
                out=stats[:, sb + 8 + tt : sb + 9 + tt], in_=stats[:, sb + 63 : sb + 64],
                func=AF.Exp, scale=-0.5), ea)
            P.wait("dv", ea, t_r3)
            P.op("dv", lambda e, tt=tt: e.tensor_scalar(out=stats[:, sb + 58 : sb + 59], in0=stats[:, sb + 58 : sb + 59],
                                                        scalar1=stats[:, sb + 0 + tt : sb + 1 + tt],
                                                        scalar2=None, op0=ALU.add), ev)
            P.op("dv", lambda e, tt=tt: e.tensor_scalar(out=stats[:, sb + 59 : sb + 60], in0=stats[:, sb + 59 : sb + 60],
                                                        scalar1=stats[:, sb + 0 + tt : sb + 1 + tt],
                                                        scalar2=None, op0=ALU.add), ev)
            P.op("dv", lambda e: e.tensor_tensor(out=stats[:, sb + 58 : sb + 59], in0=stats[:, sb + 58 : sb + 59],
                                                 in1=stats[:, sb + 59 : sb + 60], op=ALU.max), ev)
            P.op("dv", lambda e, tt=tt: e.tensor_tensor(out=stats[:, sb + 16 + tt : sb + 17 + tt],
                                                        in0=stats[:, sb + 58 : sb + 59],
                                                        in1=stats[:, sb + 8 + tt : sb + 9 + tt], op=ALU.mult), ev)
            P.op("dv", lambda e, tt=tt: e.tensor_scalar(out=stats[:, sb + 60 : sb + 61],
                                                        in0=stats[:, sb + 16 + tt : sb + 17 + tt],
                                                        scalar1=1e-5, scalar2=None, op0=ALU.max), ev)
            P.op("dv", lambda e: e.reciprocal(out=stats[:, sb + 61 : sb + 62], in_=stats[:, sb + 60 : sb + 61]), ev)
            P.op("dv", lambda e: e.tensor_scalar(out=stats[:, sb + 62 : sb + 63], in0=stats[:, sb + 61 : sb + 62],
                                                 scalar1=127.0, scalar2=None, op0=ALU.mult), ev)
            P.op("dv", lambda e, tt=tt: e.tensor_tensor(out=stats[:, sb + 61 : sb + 62], in0=stats[:, sb + 62 : sb + 63],
                                                        in1=stats[:, sb + 8 + tt : sb + 9 + tt], op=ALU.mult), ev)
            t_ab = P.op("dv", lambda e, tt=tt: e.tensor_tensor(out=stats[:, sb + 63 : sb + 64],
                                                               in0=stats[:, sb + 61 : sb + 62],
                                                               in1=stats[:, sb + 0 + tt : sb + 1 + tt],
                                                               op=ALU.mult), ev)
            P.op("dv", lambda e, tt=tt: e.tensor_tensor(out=stats[:, sb + 24 + tt : sb + 25 + tt],
                                                        in0=stats[:, sb + 16 + tt : sb + 17 + tt],
                                                        in1=scal[:, 6:7], op=ALU.mult), ev)
            P.op("dv", lambda e, tt=tt: e.tensor_scalar(out=stats[:, sb + 24 + tt : sb + 25 + tt],
                                                        in0=stats[:, sb + 24 + tt : sb + 25 + tt],
                                                        scalar1=1.0 / 254.0, scalar2=None, op0=ALU.mult), ev)
            t_sg3[tt] = P.n(ev)
            P.op("dv", lambda e, tt=tt: e.tensor_tensor(out=stats[:, sb + 32 + tt : sb + 33 + tt],
                                                        in0=stats[:, sb + 16 + tt : sb + 17 + tt],
                                                        in1=scal[:, 7:8], op=ALU.mult), ev)
            P.op("dv", lambda e, tt=tt: e.tensor_tensor(out=stats[:, sb + 32 + tt : sb + 33 + tt],
                                                        in0=stats[:, sb + 32 + tt : sb + 33 + tt],
                                                        in1=scal[:, 8:9], op=ALU.mult), ev)
            P.op("dv", lambda e, tt=tt: e.tensor_scalar(out=stats[:, sb + 32 + tt : sb + 33 + tt],
                                                        in0=stats[:, sb + 32 + tt : sb + 33 + tt],
                                                        scalar1=1.0 / 64516.0, scalar2=None, op0=ALU.mult), ev)
            P.wait("ac", ev, t_ab)
            P.wait("ac", ev, tok_a1i)
            t_ai = P.op("ac", lambda e, sl=sl: e.activation(
                out=i16a[:, 2048:3072], in_=x2t_ap(sl), func=AF.Identity,
                scale=stats[:, sb + 61 : sb + 62], bias=stats[:, sb + 63 : sb + 64]), ea)
            tok_x2t[sl] = t_ai
            tok_scr_ac = t_ai
            P.wait("dv", ea, t_ai)
            P.wait("dv", ep, tok_a1b)
            t_a1b = P.op("dv", lambda e: e.tensor_copy(out=wdT[:, 0:1024], in_=i16a[:, 2048:3072]), ev)
            tok_a1i = t_a1b
            P.wait("pe", ev, t_a1b)
            P.wait("pe", ev, tok_psB[0])
            t_tr = None
            for fc in range(8):
                t_tr = P.op("pe", lambda e, fc=fc: e.transpose(
                    out=psB[0][:, 128 * fc : 128 * fc + 128],
                    in_=wdT[:, 128 * fc : 128 * fc + 128], identity=identb[:]), ep)
            tok_a1b = t_tr
            P.wait("dv", ep, t_tr)
            if tt == 0:
                P.wait("dv", ep, t_icd)
            t_fm = P.op("dv", lambda e, tt=tt: e.tensor_copy(
                out=bass.AP(tensor=df[:].tensor, offset=128 * tt,
                            ap=[list(df[:].ap[0]), [P3S, 8], [1, 128]]),
                in_=psB[0][:].rearrange("p (a b) -> p a b", a=8)), ev)
            tok_psB[0] = t_fm
            t_fm3[tt] = t_fm


        for F in range(NSUP3 if not _stopped3 else 0):
            TF = F * P3S
            sb = 64 * (F % 2)
            if F == 0 or _NO_ILV:
                for tt in range(NTT):
                    ln3_tt(F, tt, t_ic_done_prev)
            t_a1_all = P.n(ev)
            for ic in range(8 if p3_lvl >= 2 else 0):
                wsl = ic % 2
                P.wait("sp", ep, tok_gw[wsl])
                for r0, r1 in splits(2):
                    t_gl = P.dma("sp", lambda e, ic=ic, wsl=wsl, r0=r0, r1=r1: e.dma_start(
                        out=fp8a[r0:r1, 32768 + 4096 * wsl : 32768 + 4096 * wsl + 4096],
                        in_=bass.AP(tensor=wg8, offset=512 * ic + INNER * r0,
                                    ap=[[INNER, r1 - r0], [128 * INNER, 8], [1, 512]])), dsem(f"wg{wsl}"))
                P.wait("sp", ep, tok_uw[wsl])
                for r0, r1 in splits(2):
                    t_ul = P.dma("sp", lambda e, ic=ic, wsl=wsl, r0=r0, r1=r1: e.dma_start(
                        out=fp8a[r0:r1, 40960 + 4096 * wsl : 40960 + 4096 * wsl + 4096],
                        in_=bass.AP(tensor=wu8, offset=512 * ic + INNER * r0,
                                    ap=[[INNER, r1 - r0], [128 * INNER, 8], [1, 512]])), dsem(f"wu{wsl}"))
                t_gmm = t_umm = None
                for tt in range(NTT):
                    ba = tt % 2
                    P.wait("pe", ev, t_fm3[tt] if ic == 0 else t_a1_all)
                    P.wait("pe", dsem(f"wg{wsl}"), t_gl)
                    P.wait("pe", ea, tok_gA[ba])
                    for fc in range(8):
                        P.op("pe", lambda e, tt=tt, fc=fc, ba=ba, wsl=wsl: e.matmul(
                            out=psA[ba][:, :],
                            lhsT=bass.AP(tensor=df[:].tensor, offset=128 * tt + P3S * fc,
                                         ap=[list(df[:].ap[0]), [1, 128]]),
                            rhs=fp8a[:, 32768 + 4096 * wsl + 512 * fc : 32768 + 4096 * wsl + 512 * fc + 512],
                            start=(fc == 0), stop=(fc == 7)), ep)
                    t_gmm = P.n(ep)
                    P.wait("pe", dsem(f"wu{wsl}"), t_ul)
                    P.wait("pe", ev, tok_uA[ba])
                    for fc in range(8):
                        P.op("pe", lambda e, tt=tt, fc=fc, ba=ba, wsl=wsl: e.matmul(
                            out=psA[2 + ba][:, :],
                            lhsT=bass.AP(tensor=df[:].tensor, offset=128 * tt + P3S * fc,
                                         ap=[list(df[:].ap[0]), [1, 128]]),
                            rhs=fp8a[:, 40960 + 4096 * wsl + 512 * fc : 40960 + 4096 * wsl + 512 * fc + 512],
                            start=(fc == 0), stop=(fc == 7)), ep)
                    t_umm = P.n(ep)
                    P.wait("ac", ep, t_gmm)
                    P.wait("ac", ev, tok_sil)
                    if ic == 0:
                        P.wait("ac", ev, t_sg3[tt])
                    t_si = P.op("ac", lambda e, tt=tt, sb=sb: e.activation(
                        out=wdT[:, 1024:1536], in_=psA[tt % 2][:],
                        func=AF.Silu, scale=stats[:, sb + 24 + tt : sb + 25 + tt]), ea)
                    tok_gA[tt % 2] = t_si
                    P.wait("dv", ep, t_umm)
                    P.wait("dv", ea, t_si)
                    if ic == 0 and tt in (2, 3):
                        P.wait("dv", dsem("obx"), tok_obx)
                    if ic == 0 and tt in (4, 5):
                        P.wait("dv", dsem("xtb"), tok_xtb)
                    t_g = P.op("dv", lambda e, tt=tt, ic=ic: e.tensor_tensor(
                        out=big[:, 4096 * tt + 512 * ic : 4096 * tt + 512 * ic + 512],
                        in0=wdT[:, 1024:1536], in1=psA[2 + tt % 2][:], op=ALU.mult), ev)
                    tok_sil = t_g
                    tok_uA[tt % 2] = t_g
                tok_gw[wsl] = t_gmm
                tok_uw[wsl] = t_umm
            t_ic_done = P.n(ep)
            t_ic_done_prev = t_ic_done
            t_g_all = P.n(ev)
            if dbg and F == 0:
                P.wait("pl", ev, t_g_all)
                t_dg = P.dma("pl", lambda e: e.dma_start(out=dbg_t["d_g"], in_=big[:, 0 : NTT * INNER]), dsem("dbg"))
                P.wait("dv", dsem("dbg"), t_dg)
            for tt in range(NTT if p3_lvl >= 3 else 0):
                P.wait("dv", ea, tok_scr_ac)
                t_mg = P.op("dv", lambda e, tt=tt, sb=sb: e.tensor_reduce(
                    out=stats[:, sb + 56 : sb + 57], in_=big[:, 4096 * tt : 4096 * tt + 4096],
                    axis=AX.X, op=ALU.max, apply_absolute_value=True), ev)
                P.op("dv", lambda e, sb=sb: e.tensor_scalar(out=stats[:, sb + 57 : sb + 58], in0=stats[:, sb + 56 : sb + 57],
                                                     scalar1=1e-30, scalar2=None, op0=ALU.max), ev)
                P.op("dv", lambda e, sb=sb: e.reciprocal(out=stats[:, sb + 58 : sb + 59], in_=stats[:, sb + 57 : sb + 58]), ev)
                t_s2 = P.op("dv", lambda e, sb=sb: e.tensor_scalar(out=stats[:, sb + 59 : sb + 60], in0=stats[:, sb + 58 : sb + 59],
                                                            scalar1=127.0, scalar2=None, op0=ALU.mult), ev)
                P.op("dv", lambda e, tt=tt, sb=sb: e.tensor_tensor(out=stats[:, sb + 32 + tt : sb + 33 + tt],
                                                            in0=stats[:, sb + 32 + tt : sb + 33 + tt],
                                                            in1=stats[:, sb + 56 : sb + 57], op=ALU.mult), ev)
                t_c = None
                for pair in range(2):
                    for hf2 in range(2):
                        hf = 2 * pair + hf2
                        P.wait("ac", ev, t_s2)
                        P.wait("ac", ev, tok_a2i)
                        t_c = P.op("ac", lambda e, tt=tt, hf=hf, hf2=hf2, sb=sb: e.activation(
                            out=i16a[:, 1024 * hf2 : 1024 * hf2 + 1024],
                            in_=big[:, 4096 * tt + 1024 * hf : 4096 * tt + 1024 * hf + 1024],
                            func=AF.Identity, scale=stats[:, sb + 59 : sb + 60]), ea)
                    P.wait("dv", ea, t_c)
                    P.wait("dv", ep, tok_a2tm)
                    t_cp3 = None
                    for hf2 in range(2):
                        hf = 2 * pair + hf2
                        t_cp3 = P.op("dv", lambda e, hf=hf, hf2=hf2: e.tensor_copy(
                            out=wvT[:, 1024 * hf : 1024 * hf + 1024],
                            in_=i16a[:, 1024 * hf2 : 1024 * hf2 + 1024]), ev)
                    tok_a2i = t_cp3
                tok_scr_ac = t_c
                t_a2tm = tok_a2i
                P.wait("pe", ev, t_a2tm)
                P.wait("pe", ep, tok_a2fm)
                t_fm2 = None
                for w4 in range(2):
                    P.wait("pe", ev, tok_psB[0])
                    P.wait("pe", ev, tok_psB[1])
                    t_tr = None
                    for kk in range(16):
                        icc = 16 * w4 + kk
                        t_tr = P.op("pe", lambda e, icc=icc, kk=kk: e.transpose(
                            out=psB[kk % 2][:, 128 * (kk // 2) : 128 * (kk // 2) + 128],
                            in_=wvT[:, 128 * icc : 128 * icc + 128], identity=identb[:]), ep)
                    if w4 == 1:
                        tok_a2tm = t_tr
                    P.wait("dv", ep, t_tr)
                    for b in range(2):
                        t_fm2 = P.op("dv", lambda e, w4=w4, b=b: e.tensor_copy(
                            out=bass.AP(tensor=wvT[:].tensor,
                                        offset=4096 + 2048 * w4 + 128 * b,
                                        ap=[list(wvT[:].ap[0]), [256, 8], [1, 128]]),
                            in_=psB[b][:]), ev)
                    tok_psB[0] = t_fm2
                    tok_psB[1] = t_fm2
                t_a2f = t_fm2
                if F + 1 < NSUP3 and p3_lvl >= 4 and not _NO_ILV:
                    ln3_tt(F + 1, tt, t_ic_done)
                if tt > 0:
                    ln4_tt(tt - 1, sb)
                P.wait("pe", ev, t_a2f)
                P.wait("pe", dsem("wd"), t_wd)
                P.wait("pe", ev, tok_ybuf)
                P.wait("pe", ea, tok_psC_ac)
                for ncx in range(2):
                    for icc in range(32):
                        P.op("pe", lambda e, icc=icc, ncx=ncx: e.matmul(
                            out=psC[ncx][:, :],
                            lhsT=wvT[:, 4096 + 128 * icc : 4096 + 128 * icc + 128],
                            rhs=fp8a[:, 1024 * icc + 512 * ncx : 1024 * icc + 512 * ncx + 512],
                            start=(icc == 0), stop=(icc == 31)), ep)
                t_dn = P.n(ep)
                tok_a2fm = t_dn
                t_dn3[tt] = t_dn
            if NTT and p3_lvl >= 3:
                ln4_tt(NTT - 1, sb)
            t_q_ev = P.n(ev)
            t_q_ea = P.n(ea)
            if p3_lvl < 4:
                continue
            P.op("dv", lambda e, sb=sb: e.tensor_tensor(out=stats[:, sb + 56 : sb + 56 + NTT],
                                                 in0=stats[:, sb + 32 : sb + 32 + NTT],
                                                 in1=stats[:, sb + 32 : sb + 32 + NTT], op=ALU.mult), ev)
            t_vt = P.op("dv", lambda e, sb=sb: e.tensor_tensor(out=stats[:, sb + 48 : sb + 48 + NTT],
                                                        in0=stats[:, sb + 48 : sb + 48 + NTT],
                                                        in1=stats[:, sb + 56 : sb + 56 + NTT], op=ALU.mult), ev)
            P.wait("ac", ev, t_vt)
            P.op("ac", lambda e, sb=sb: e.activation(out=stats[:, sb + 56 : sb + 56 + NTT], in_=stats[:, sb + 48 : sb + 48 + NTT],
                                              func=AF.Ln, bias=epscol[:, 0:1]), ea)
            t_r4 = P.op("ac", lambda e, sb=sb: e.activation(out=stats[:, sb + 48 : sb + 48 + NTT], in_=stats[:, sb + 56 : sb + 56 + NTT],
                                                     func=AF.Exp, scale=-0.5), ea)
            P.wait("dv", ea, t_r4)
            t_cr = P.op("dv", lambda e, sb=sb: e.tensor_tensor(out=stats[:, sb + 48 : sb + 48 + NTT],
                                                        in0=stats[:, sb + 48 : sb + 48 + NTT],
                                                        in1=stats[:, sb + 32 : sb + 32 + NTT], op=ALU.mult), ev)
            P.wait("sp", ev, t_q_ev)
            P.wait("sp", ea, t_q_ea)
            P.wait("sp", dsem("obx"), tok_obx)
            for r0, r1 in splits(4):
                t_xb = P.dma("sp", lambda e, TF=TF, r0=r0, r1=r1: e.dma_start(
                    out=big[r0:r1, 16384 : 16384 + NTT * 1024],
                    in_=bass.AP(tensor=x2s, offset=(TF + r0) * C,
                                ap=[[C, r1 - r0], [128 * C, NTT], [1, C]])), dsem("xtb"))
            tok_xtb = t_xb
            P.wait("dv", dsem("xtb"), t_xb)
            t_ou = None
            for tt in range(NTT):
                P.op("dv", lambda e, tt=tt, sb=sb: e.tensor_scalar(
                    out=f32a[:, F_sc : F_sc + 1024], in0=cpT[:, 1024 * tt : 1024 * tt + 1024],
                    scalar1=stats[:, sb + 40 + tt : sb + 41 + tt], scalar2=stats[:, sb + 48 + tt : sb + 49 + tt],
                    op0=ALU.add, op1=ALU.mult), ev)
                t_ou = P.op("dv", lambda e, tt=tt: e.tensor_tensor(
                    out=big[:, 8192 + 1024 * tt : 8192 + 1024 * tt + 1024],
                    in0=f32a[:, F_sc : F_sc + 1024],
                    in1=big[:, 16384 + 1024 * tt : 16384 + 1024 * tt + 1024], op=ALU.add), ev)
            P.wait("pl", ev, t_ou)
            for r0, r1 in splits(4):
                tok_obx = P.dma("pl", lambda e, TF=TF, r0=r0, r1=r1: e.dma_start(
                    out=bass.AP(tensor=out_t, offset=(TF + r0) * C,
                                ap=[[C, r1 - r0], [128 * C, NTT], [1, C]]),
                    in_=big[r0:r1, 8192 : 8192 + NTT * 1024]), dsem("obx"))

            tok_ybuf = 0
            tok_a1fm = tok_gw[0] if tok_gw[0] else 0
        fence()

        with nc.Block() as block:
            @block.sync
            def _(e):
                P.replay("sp", e)

            @block.tensor
            def _(e):
                P.replay("pe", e)

            @block.vector
            def _(e):
                P.replay("dv", e)

            @block.scalar
            def _(e):
                P.replay("ac", e)

            @block.gpsimd
            def _(e):
                P.replay("pl", e)

    return nc



def _host_consts(inputs):
    sfw = np.asarray(inputs["L0_attn_strain_fused_w"], np.float32)
    sfb = np.asarray(inputs["L0_attn_strain_fused_b"], np.float32)
    bondw = np.asarray(inputs["L0_attn_bond_out_w"], np.float32)
    dmgw = np.asarray(inputs["L0_attn_damage_out_w"], np.float32)
    dmgb = np.asarray(inputs["L0_attn_damage_out_b"], np.float32)

    identb = np.eye(128, dtype=ml_dtypes.bfloat16)
    maskneg = np.zeros((128, WIN), np.float32)
    r = np.arange(128)[:, None]
    wv = np.arange(WIN)[None, :]
    maskneg[(r + wv) < (WIN - 1)] = -1e9

    logitw = np.zeros((128, 8), np.float32)
    for h in range(4):
        logitw[32 * h : 32 * h + BD, h] = bondw[0]
        logitw[32 * h + BD : 32 * h + 2 * BD, 4 + h] = dmgw[0]

    dfw32 = np.zeros((128, 256), np.float32)
    for h in range(8):
        c0 = (h % 4) * 32 + (h // 4) * 128
        dfw32[16 * h : 16 * h + BD, c0 : c0 + 2 * BD] = sfw.T
    csfb = np.repeat(np.tile(sfb, 4)[:, None], WIN, axis=1).astype(np.float32)
    dbrep = np.full((128, 1), float(dmgb[0]), np.float32)
    return {
        "identb": identb, "maskneg": maskneg.astype(ml_dtypes.bfloat16), "logitw32": logitw,
        "dfw32": dfw32, "csfb": csfb, "dbrep": dbrep,
    }


def kernel(**inputs):
    T_full = int(np.asarray(inputs["L0_attn_x"]).shape[1])
    nc = build(T=T_full, dbg=False)

    x = np.ascontiguousarray(np.asarray(inputs["L0_attn_x"], np.float32))
    consts = _host_consts(inputs)
    shared = {
        "wdisp": np.ascontiguousarray(np.asarray(inputs["L0_attn_W_disp"], np.float32)),
        "wval": np.ascontiguousarray(np.asarray(inputs["L0_attn_W_val"], np.float32)),
        "relpos": np.ascontiguousarray(np.asarray(inputs["L0_attn_rel_pos_emb"], np.float32)[:WIN]),
        "pposw": np.ascontiguousarray(np.asarray(inputs["L0_attn_pos_proj_w"], np.float32)),
        "cprojw": np.ascontiguousarray(np.asarray(inputs["L0_attn_c_proj_w"], np.float32)),
        "wgate": np.ascontiguousarray(np.asarray(inputs["L0_ffn_w_gate"], np.float32)),
        "wup": np.ascontiguousarray(np.asarray(inputs["L0_ffn_w_up"], np.float32)),
        "wdown": np.ascontiguousarray(np.asarray(inputs["L0_ffn_w_down"], np.float32)),
        **consts,
    }
    ncores = x.shape[0]
    in_maps = [{"x": np.ascontiguousarray(x[c]), **shared} for c in range(ncores)]
    res = run_bass_kernel_spmd(nc, in_maps, core_ids=list(range(ncores)))
    outs = [np.asarray(r["out"], np.float32) for r in res.results]
    return np.stack(outs, axis=0)


# version 21
# speedup vs baseline: 1.0329x; 1.0042x over previous
import numpy as np
import ml_dtypes
import concourse.bass as bass
import concourse.mybir as mybir
from concourse.bass_utils import run_bass_kernel_spmd
from contextlib import ExitStack

F32 = mybir.dt.float32
BF16 = mybir.dt.bfloat16
I16 = mybir.dt.int16
FP8 = mybir.dt.float8e4
AF = mybir.ActivationFunctionType
ALU = mybir.AluOpType
AX = mybir.AxisListType

C = 1024
NH = 16
BD = 16
HS = 64
WIN = 64
INNER = 4096
PAD = 256
EPS = 1e-5


class Planner:
    def __init__(self):
        self.steps = []
        self.counts = {}
        self.engs = ("sp", "pe", "dv", "ac", "pl")
        self.chain = ("dv", "ac")
        self.last_chain = {}

    def reg(self, s):
        self.counts[id(s)] = 0
        return s

    def n(self, s):
        return self.counts[id(s)]

    def wait(self, eng, s, val=None):
        v = self.counts[id(s)] if val is None else val
        if v > 0:
            self.steps.append((eng, "w", (s, v)))

    def op(self, eng, fn, s=None, inc=1):
        if eng in self.chain and s is not None:
            cur = self.counts[id(s)]
            if cur > 0 and self.last_chain.get(eng, -1) < cur:
                self.steps.append((eng, "w", (s, cur)))
                self.last_chain[eng] = cur
        self.steps.append((eng, "o", (fn, s, inc)))
        if s is not None:
            self.counts[id(s)] += inc
            return self.counts[id(s)]
        return None

    def dma(self, eng, fn, s):
        return self.op(eng, fn, s, 16)

    def replay(self, key, eng):
        for k, kind, pl in self.steps:
            if k != key:
                continue
            if kind == "w":
                eng.wait_ge(pl[0], pl[1])
            else:
                fn, s, inc = pl
                r = fn(eng)
                if s is not None:
                    r.then_inc(s, inc)


def build(T=4096, dbg=False, stop_after=None, p1_lvl=8, p2_lvl=8, p3_lvl=4):
    NT = T // 128
    P2S = min(2048, T)
    NSUP2 = T // P2S
    NSUB = P2S // 128
    NQC = NSUB // 4
    P3S = min(1024, T)
    NSUP3 = T // P3S
    NTT = P3S // 128
    DF1 = T + 128

    nc = bass.Bass("TRN2", target_bir_lowering=False)

    x_in = nc.dram_tensor("x", [T, C], F32, kind="ExternalInput").ap()
    wdisp = nc.dram_tensor("wdisp", [NH * BD, C], F32, kind="ExternalInput").ap()
    wval = nc.dram_tensor("wval", [C, C], F32, kind="ExternalInput").ap()
    relpos = nc.dram_tensor("relpos", [WIN, BD], F32, kind="ExternalInput").ap()
    pposw = nc.dram_tensor("pposw", [BD, BD], F32, kind="ExternalInput").ap()
    cprojw = nc.dram_tensor("cprojw", [C, C], F32, kind="ExternalInput").ap()
    wgate = nc.dram_tensor("wgate", [INNER, C], F32, kind="ExternalInput").ap()
    wup = nc.dram_tensor("wup", [INNER, C], F32, kind="ExternalInput").ap()
    wdown = nc.dram_tensor("wdown", [C, INNER], F32, kind="ExternalInput").ap()
    identb_in = nc.dram_tensor("identb", [128, 128], BF16, kind="ExternalInput").ap()
    maskneg_in = nc.dram_tensor("maskneg", [128, WIN], BF16, kind="ExternalInput").ap()
    logitw_in = nc.dram_tensor("logitw32", [128, 8], F32, kind="ExternalInput").ap()
    dfw_in = nc.dram_tensor("dfw32", [128, 256], F32, kind="ExternalInput").ap()
    csfb_in = nc.dram_tensor("csfb", [128, WIN], F32, kind="ExternalInput").ap()
    dbrep_in = nc.dram_tensor("dbrep", [128, 1], F32, kind="ExternalInput").ap()

    wp = nc.dram_tensor("wp", [NH * T * PAD], BF16, kind="Internal")
    x2s = nc.dram_tensor("x2s", [T * C], BF16, kind="Internal")
    vals = nc.dram_tensor("vals", [T * C], BF16, kind="Internal")
    wg8 = nc.dram_tensor("wg8", [8 * 128 * INNER], FP8, kind="Internal")
    wu8 = nc.dram_tensor("wu8", [8 * 128 * INNER], FP8, kind="Internal")
    wd8 = nc.dram_tensor("wd8", [32 * 128 * C], FP8, kind="Internal")

    out_t = nc.dram_tensor("out", [T, C], F32, kind="ExternalOutput")
    out = out_t.ap()
    dbg_t = {}
    if dbg:
        for nm, shp in [("d_df", [128, 4 * DF1]), ("d_val", [T, C]),
                        ("d_ltm", [128, 16 * 512]), ("d_x2", [T, C]),
                        ("d_g", [128, min(8, NTT) * INNER]), ("d_ptm", [128, 16 * 256])]:
            dbg_t[nm] = nc.dram_tensor(nm, shp, F32, kind="ExternalOutput").ap()

    P = Planner()

    with ExitStack() as es:
        def sb(name, shape, dtype=F32):
            return es.enter_context(nc.sbuf_tensor(name, shape, dtype))

        def psum(name, shape, dtype=F32):
            return es.enter_context(nc.psum_tensor(name, shape, dtype))

        def sem(name):
            return P.reg(es.enter_context(nc.semaphore(name)))

        df = sb("df", [128, max(4 * DF1, 8 * P3S)], BF16)
        wvT = sb("wvT", [128, 8 * C], BF16)
        cpT = sb("cpT", [128, 8 * C], BF16)
        wdT = sb("wdT", [128, 2048], BF16)
        big = sb("big", [128, 32768], BF16)
        f32a = sb("f32a", [128, 3088], F32)
        i16a = sb("i16a", [128, 3072], I16)
        fp8a = sb("fp8a", [128, 8 * INNER + 16384], FP8)
        junk = sb("junk", [128, 1024], BF16)
        identb = sb("identb_t", [128, 128], BF16)
        dfwb = sb("dfwb", [128, 256], BF16)
        logit_w = sb("logit_w", [128, 8], BF16)
        c_all = sb("c_all", [128, WIN])
        maskneg = sb("maskneg_t", [128, WIN], BF16)
        dbcol = sb("dbcol", [128, 1])
        stats = sb("stats", [128, 128])
        scal = sb("scal", [128, 16])
        onesb = sb("onesb", [1, 128], BF16)
        ones32 = sb("ones32", [1, 128], F32)
        onescol = sb("onescol", [128, 1], F32)
        epscol = sb("epscol", [128, 1], F32)
        absac = sb("absac", [128, 32])
        ternb = sb("ternb", [128, 2048], BF16)

        w32 = f32a

        psA = [psum(f"psA{i}", [128, 512], F32) for i in range(4)]
        psB = [psum(f"psB{i}", [128, 1024], BF16) for i in range(2)]
        psC = [psum(f"psC{i}", [128, 512], F32) for i in range(2)]

        ev = sem("ev")
        ea = sem("ea")
        ep = sem("ep")
        ALLS = [ev, ea, ep]
        _streams = {}

        def dsem(name):
            if name not in _streams:
                s = sem("q_" + name)
                _streams[name] = s
                ALLS.append(s)
            return _streams[name]

        def fence():
            for e in P.engs:
                for s in ALLS:
                    P.wait(e, s)

        def ap_rows(a, r0, r1):
            return bass.AP(tensor=a.tensor, offset=a.offset + a.ap[0][0] * r0,
                           ap=[[a.ap[0][0], r1 - r0]] + [list(d) for d in a.ap[1:]])

        def splits(n):
            return [(128 * i // n, 128 * (i + 1) // n) for i in range(n)]

        B_h1buf, B_h1fm, B_disp, B_vstg = 0, 1024, 2048, 3072
        B_args, B_gargs, B_ptm = 0, 8192, 14336
        B_ebuf, B_btb, B_btb2 = 14848, 15360, 19456
        B_bb, B_attn, B_x2b, B_vring = 23552, 25600, 27648, 29696
        B_blob = B_x2b
        F_xb, F_sc = 0, 2048

        P.dma("sp", lambda e: e.dma_start(out=identb[:], in_=identb_in), dsem("c"))
        P.dma("sp", lambda e: e.dma_start(out=maskneg[:], in_=maskneg_in), dsem("c"))
        P.dma("sp", lambda e: e.dma_start(out=c_all[:], in_=csfb_in), dsem("c"))
        P.dma("sp", lambda e: e.dma_start(out=stats[:, 0:1], in_=dbrep_in), dsem("c"))
        P.dma("sp", lambda e: e.dma_start(out=f32a[:, 0:8], in_=logitw_in), dsem("c"))
        P.dma("sp", lambda e: e.dma_start(out=f32a[:, 8:264], in_=dfw_in), dsem("c"))
        P.dma("sp", lambda e: e.dma_start(out=f32a[0:WIN, 264:280], in_=relpos), dsem("c"))
        P.dma("sp", lambda e: e.dma_start(out=f32a[0:BD, 280:296], in_=pposw), dsem("c"))
        t_cl = P.n(dsem("c"))

        P.wait("dv", dsem("c"), t_cl)
        P.op("dv", lambda e: e.tensor_copy(out=logit_w[:], in_=f32a[:, 0:8]), ev)
        P.op("dv", lambda e: e.tensor_copy(out=dfwb[:], in_=f32a[:, 8:264]), ev)
        P.op("dv", lambda e: e.tensor_copy(out=ternb[0:WIN, 0:BD], in_=f32a[0:WIN, 264:280]), ev)
        P.op("dv", lambda e: e.tensor_copy(out=ternb[0:BD, 16:32], in_=f32a[0:BD, 280:296]), ev)
        P.op("dv", lambda e: e.memset(onesb[:], 1.0), ev)
        P.op("dv", lambda e: e.memset(ones32[:], 1.0), ev)
        P.op("dv", lambda e: e.memset(onescol[:], 1.0), ev)
        P.op("dv", lambda e: e.memset(epscol[:], float(EPS)), ev)
        P.op("dv", lambda e: e.memset(df[:], 0.0), ev)
        P.op("dv", lambda e: e.tensor_scalar(out=dbcol[:], in0=stats[:, 0:1],
                                             scalar1=0.5, scalar2=None, op0=ALU.mult), ev)
        t_cv = P.n(ev)

        for _pi in range(4):
            P.op("dv", lambda e, _pi=_pi: e.memset(psA[_pi][:], 0.0), ev)
        P.op("dv", lambda e: e.memset(psC[0][:], 0.0), ev)
        P.op("dv", lambda e: e.memset(psC[1][:], 0.0), ev)
        P.wait("pe", ev, t_cv)
        P.op("pe", lambda e: e.transpose(out=psB[0][0:BD, 0:WIN], in_=ternb[0:WIN, 0:BD],
                                         identity=identb[0:WIN, 0:WIN]), ep)
        P.op("pe", lambda e: e.transpose(out=psB[0][0:BD, 64:80], in_=ternb[0:BD, 16:32],
                                         identity=identb[0:BD, 0:BD]), ep)
        t_ct = P.n(ep)
        P.wait("dv", ep, t_ct)
        P.op("dv", lambda e: e.tensor_copy(out=ternb[0:BD, 32:96], in_=psB[0][0:BD, 0:WIN]), ev)
        P.op("dv", lambda e: e.tensor_copy(out=ternb[0:BD, 96:112], in_=psB[0][0:BD, 64:80]), ev)
        t_ce = P.n(ev)
        P.wait("pe", ev, t_ce)
        P.op("pe", lambda e: e.matmul(out=psA[0][0:BD, 0:WIN], lhsT=ternb[0:BD, 96:112],
                                      rhs=ternb[0:BD, 32:96], start=True, stop=True), ep)
        t_cp = P.n(ep)
        P.wait("dv", ep, t_cp)
        for hl in range(4):
            P.op("dv", lambda e, hl=hl: e.tensor_tensor(
                out=c_all[32 * hl : 32 * hl + BD, :], in0=c_all[32 * hl : 32 * hl + BD, :],
                in1=psA[0][0:BD, 0:WIN], op=ALU.add), ev)
        P.op("dv", lambda e: e.memset(big[:, B_x2b : B_x2b + 2048], 0.0), ev)
        P.op("dv", lambda e: e.memset(ternb[:], 0.0), ev)
        fence()

        tok_w32 = [0, 0]
        tok_tern = [0, 0]
        tok_psB = [0, 0]

        def prep_attn_w(src_fn, n_tiles, dest_tensor, dest_stride):
            for r in range(n_tiles):
                sl = r % 2
                P.wait("sp", ev, tok_w32[sl])
                for r0, r1 in splits(4):
                    t_ld = P.dma("sp", lambda e, r=r, sl=sl, src_fn=src_fn, r0=r0, r1=r1: e.dma_start(
                        out=w32[r0:r1, 1024 * sl : 1024 * sl + 1024],
                        in_=ap_rows(src_fn(r), r0, r1)), dsem(f"w{sl}"))
                P.wait("dv", dsem(f"w{sl}"), t_ld)
                P.wait("dv", ep, tok_tern[sl])
                t_cast = P.op("dv", lambda e, sl=sl: e.tensor_copy(
                    out=ternb[:, 1024 * sl : 1024 * sl + 1024],
                    in_=w32[:, 1024 * sl : 1024 * sl + 1024]), ev)
                tok_w32[sl] = t_cast
                P.wait("pe", ev, t_cast)
                P.wait("pe", ev, tok_psB[sl])
                t_tr = None
                for fc in range(8):
                    t_tr = P.op("pe", lambda e, fc=fc, sl=sl: e.transpose(
                        out=psB[sl][:, 128 * fc : 128 * fc + 128],
                        in_=ternb[:, 1024 * sl + 128 * fc : 1024 * sl + 128 * fc + 128],
                        identity=identb[:]), ep)
                tok_tern[sl] = t_tr
                P.wait("dv", ep, t_tr)
                t_ev = P.op("dv", lambda e, r=r, sl=sl, d=dest_tensor, ds=dest_stride: e.tensor_copy(
                    out=bass.AP(tensor=d[:].tensor, offset=128 * r,
                                ap=[list(d[:].ap[0]), [ds, 8], [1, 128]]),
                    in_=psB[sl][:].rearrange("p (a b) -> p a b", a=8)), ev)
                tok_psB[sl] = t_ev

        if stop_after != "consts":
            prep_attn_w(lambda r: wval[128 * r : 128 * r + 128, :], 8, wvT, C)
            prep_attn_w(lambda r: cprojw[128 * r : 128 * r + 128, :], 8, cpT, C)
            prep_attn_w(lambda r: wdisp[128 * r : 128 * r + 128, :], 2, wdT, 256)
        fence()
        tok_w32 = [0, 0]
        tok_psB = [0, 0]

        nzc = (NH * T * PAD) // (128 * 2048)
        t_zwp = 0
        for z in range(nzc):
            t_zwp = P.dma("sp", lambda e, z=z: e.dma_start(
                out=bass.AP(tensor=wp, offset=z * 128 * 2048, ap=[[2048, 128], [1, 2048]]),
                in_=df[:, 0:2048]), dsem("z"))

        def w_tile_ap(mat, ti):
            if mat == 0:
                return wgate[128 * ti : 128 * ti + 128, :]
            if mat == 1:
                return wup[128 * ti : 128 * ti + 128, :]
            r, q = divmod(ti, 4)
            return wdown[128 * r : 128 * r + 128, 1024 * q : 1024 * q + 1024]

        def w8_dst(mat, ti):
            if mat < 2:
                t8 = wg8 if mat == 0 else wu8
                return bass.AP(tensor=t8, offset=128 * ti,
                               ap=[[INNER, 128], [128 * INNER, 8], [1, 128]])
            r, q = divmod(ti, 4)
            return bass.AP(tensor=wd8, offset=(8 * q) * (128 * C) + 128 * r,
                           ap=[[C, 128], [128 * C, 8], [1, 128]])

        wb_on = stop_after != "consts" and stop_after != "wa"
        n_wbc = 192 if wb_on else 0
        t_red = 0
        t_scal = 0

        def wbc_chunk(k):
            nonlocal t_red, t_scal
            mat, ti, half = wcc_idx(k)
            csl = k % 2
            t_my = tld_wcc[csl]
            P.wait("ac", dsem(f"q{csl}"), t_my)
            if 2 * ti + half < 4:
                P.wait("ac", ev, t_red)
            t_abs = P.op("ac", lambda e, ti=ti, half=half, csl=csl: e.activation(
                out=junk[:, 0:512],
                in_=f32a[:, F_sc + 512 * csl : F_sc + 512 * csl + 512],
                func=AF.Abs, accum_out=stats[:, 64 + 2 * ti + half : 65 + 2 * ti + half]), ea)
            tok_w32c[csl] = t_abs
            if k + 2 < n_wbc:
                wcc_load(k + 2)
            elif k + 2 < n_wbc + 2 and n_wcc:
                wcc_load(k + 2 - n_wbc)
            if ti == 31 and half == 1:
                P.wait("dv", ea, t_abs)
                t_red = P.op("dv", lambda e: e.tensor_reduce(
                    out=stats[:, 30:31], in_=stats[:, 64:128], axis=AX.X, op=ALU.add), ev)
                P.wait("pe", ev, t_red)
                t_mm = P.op("pe", lambda e, mat=mat: e.matmul(
                    out=psC[0][0:1, mat : mat + 1], lhsT=onescol[:],
                    rhs=stats[:, 30:31], start=True, stop=True), ep)
                P.wait("dv", ep, t_mm)
                P.op("dv", lambda e, mat=mat: e.tensor_scalar(
                    out=stats[0:1, 60 + mat : 61 + mat], in0=psC[0][0:1, mat : mat + 1],
                    scalar1=1.0 / (INNER * C), scalar2=1e-5, op0=ALU.mult, op1=ALU.max), ev)
                if mat == 2:
                    t_mus = P.n(ev)
                    P.wait("pe", ev, t_mus)
                    t_mm2 = P.op("pe", lambda e: e.matmul(out=psC[1][0:128, 0:3], lhsT=ones32[0:1, :],
                                                          rhs=stats[0:1, 60:63],
                                                          start=True, stop=True), ep)
                    P.wait("dv", ep, t_mm2)
                    P.op("dv", lambda e: e.tensor_copy(out=scal[:, 6:9], in_=psC[1][0:128, 0:3]), ev)
                    P.op("dv", lambda e: e.tensor_scalar(out=scal[:, 0:3], in0=psC[1][0:128, 0:3],
                                                         scalar1=0.5, scalar2=None, op0=ALU.mult), ev)
                    P.op("dv", lambda e: e.tensor_scalar(out=scal[:, 3:6], in0=psC[1][0:128, 0:3],
                                                         scalar1=-0.5, scalar2=None, op0=ALU.mult), ev)
                    t_scal = P.n(ev)

        wc_on = stop_after not in ("consts", "wa", "wb")
        n_wcc = 192 if wc_on else 0
        tok_w32c = [0, 0]
        tok_terc = [0, 0]
        tok_psB1 = [0, 0]
        tok_stgc = [0, 0]
        tld_wcc = [0, 0]

        def ap_mid(a, i0, n):
            return bass.AP(tensor=a.tensor, offset=a.offset + a.ap[1][0] * i0,
                           ap=[list(a.ap[0]), [a.ap[1][0], n]] + [list(d) for d in a.ap[2:]])

        def wcc_idx(k):
            mat = k // 64
            ti = (k % 64) // 2
            half = k % 2
            return mat, ti, half

        def wcc_load(k):
            mat, ti, half = wcc_idx(k)
            csl = k % 2
            P.wait("pl", ea, tok_w32c[csl])
            for r0, r1 in splits(2):
                tld_wcc[csl] = P.dma("pl", lambda e, mat=mat, ti=ti, half=half, csl=csl, r0=r0, r1=r1: e.dma_start(
                    out=f32a[r0:r1, F_sc + 512 * csl : F_sc + 512 * csl + 512],
                    in_=ap_cols(ap_rows(w_tile_ap(mat, ti), r0, r1), half)), dsem(f"q{csl}"))

        def ap_cols(a, half):
            return bass.AP(tensor=a.tensor, offset=a.offset + 512 * half,
                           ap=[list(a.ap[0]), [1, 512]])

        def wcc_chunk(k):
            mat, ti, half = wcc_idx(k)
            csl = k % 2
            t_my_ld = tld_wcc[csl]
            P.wait("ac", dsem(f"q{csl}"), t_my_ld)
            if k < 2 and n_wbc:
                P.wait("ac", ev, t_scal)
            P.wait("ac", ep, tok_terc[csl])
            P.op("ac", lambda e, csl=csl, mat=mat: e.activation(
                out=ternb[:, 512 * csl : 512 * csl + 512],
                in_=f32a[:, F_sc + 512 * csl : F_sc + 512 * csl + 512],
                func=AF.Sign, bias=scal[:, 3 + mat : 4 + mat]), ea)
            t_b = P.op("ac", lambda e, csl=csl, mat=mat: e.activation(
                out=big[:, 5120 + 512 * csl : 5120 + 512 * csl + 512],
                in_=f32a[:, F_sc + 512 * csl : F_sc + 512 * csl + 512],
                func=AF.Sign, bias=scal[:, mat : mat + 1]), ea)
            tok_w32c[csl] = t_b
            if k + 2 < n_wcc:
                wcc_load(k + 2)
            P.wait("dv", ea, t_b)
            t_t = P.op("dv", lambda e, csl=csl: e.tensor_tensor(
                out=ternb[:, 512 * csl : 512 * csl + 512],
                in0=ternb[:, 512 * csl : 512 * csl + 512],
                in1=big[:, 5120 + 512 * csl : 5120 + 512 * csl + 512], op=ALU.add), ev)
            P.wait("pe", ev, t_t)
            P.wait("pe", ev, tok_psB1[csl])
            t_tr = None
            for fc2 in range(4):
                t_tr = P.op("pe", lambda e, csl=csl, fc2=fc2: e.transpose(
                    out=psB[1][:, 512 * csl + 128 * fc2 : 512 * csl + 128 * fc2 + 128],
                    in_=ternb[:, 512 * csl + 128 * fc2 : 512 * csl + 128 * fc2 + 128],
                    identity=identb[:]), ep)
            tok_terc[csl] = t_tr
            P.wait("dv", ep, t_tr)
            P.wait("dv", dsem(f"st{csl}"), tok_stgc[csl])
            t_ev = P.op("dv", lambda e, csl=csl: e.tensor_copy(
                out=fp8a[:, 32768 + 512 * csl : 32768 + 512 * csl + 512],
                in_=psB[1][:, 512 * csl : 512 * csl + 512]), ev)
            tok_psB1[csl] = t_ev
            P.wait("sp", ev, t_ev)
            for r0, r1 in splits(2):
                tok_stgc[csl] = P.dma("sp", lambda e, mat=mat, ti=ti, half=half, csl=csl, r0=r0, r1=r1: e.dma_start(
                    out=ap_rows(ap_mid(w8_dst(mat, ti), 4 * half, 4), r0, r1),
                    in_=fp8a[r0:r1, 32768 + 512 * csl : 32768 + 512 * csl + 512]), dsem(f"st{csl}"))

        fence()
        tok_w32 = [0, 0]
        tok_psB = [0, 0]

        tok_xb = [0, 0]
        tok_h1b = 0
        tok_h1fm = 0
        tok_psA01 = 0
        tok_psA2 = 0
        tok_psA3 = [0, 0]
        tok_dispb = 0
        tok_vstg = [0, 0]
        n_p1 = NT if stop_after not in ("consts", "wa", "wb", "wc") else 0
        tld_x = [0, 0]

        def p1_load(g):
            sl = g % 2
            P.wait("sp", ev, tok_xb[sl])
            for r0, r1 in splits(4):
                tld_x[sl] = P.dma("sp", lambda e, g=g, sl=sl, r0=r0, r1=r1: e.dma_start(
                    out=f32a[r0:r1, F_xb + 1024 * sl : F_xb + 1024 * sl + 1024],
                    in_=x_in[128 * g + r0 : 128 * g + r1, :]), dsem(f"xb{sl}"))

        t_rs_g = [0, 0]

        def p1_stats(g):
            sl = g % 2
            sp2 = 2 + 8 * (g % 2)
            t_x = tld_x[sl]

            def xt_ap(sl=sl):
                return f32a[:, F_xb + 1024 * sl : F_xb + 1024 * sl + 1024]

            P.wait("dv", dsem(f"xb{sl}"), t_x)
            P.op("dv", lambda e, sl=sl, sp2=sp2: e.tensor_reduce(
                out=stats[:, sp2 : sp2 + 1], in_=xt_ap(sl), axis=AX.X, op=ALU.add), ev)
            P.wait("ac", dsem(f"xb{sl}"), t_x)
            t_sq = P.op("ac", lambda e, sl=sl, sp2=sp2: e.activation(
                out=junk[:], in_=xt_ap(sl), func=AF.Square, accum_out=stats[:, sp2 + 1 : sp2 + 2]), ea)
            P.op("dv", lambda e, sp2=sp2: e.tensor_scalar(out=stats[:, sp2 + 2 : sp2 + 3], in0=stats[:, sp2 : sp2 + 1],
                                                 scalar1=-1.0 / C, scalar2=None, op0=ALU.mult), ev)
            P.wait("dv", ea, t_sq)
            P.op("dv", lambda e, sp2=sp2: e.tensor_scalar(out=stats[:, sp2 + 3 : sp2 + 4], in0=stats[:, sp2 + 1 : sp2 + 2],
                                                 scalar1=1.0 / C, scalar2=None, op0=ALU.mult), ev)
            P.op("dv", lambda e, sp2=sp2: e.tensor_tensor(out=stats[:, sp2 + 4 : sp2 + 5], in0=stats[:, sp2 + 2 : sp2 + 3],
                                                 in1=stats[:, sp2 + 2 : sp2 + 3], op=ALU.mult), ev)
            t_var = P.op("dv", lambda e, sp2=sp2: e.tensor_tensor(out=stats[:, sp2 + 5 : sp2 + 6], in0=stats[:, sp2 + 3 : sp2 + 4],
                                                         in1=stats[:, sp2 + 4 : sp2 + 5], op=ALU.subtract), ev)
            P.wait("ac", ev, t_var)
            P.op("ac", lambda e, sp2=sp2: e.activation(out=stats[:, sp2 + 6 : sp2 + 7], in_=stats[:, sp2 + 5 : sp2 + 6],
                                              func=AF.Ln, bias=epscol[:, 0:1]), ea)
            t_rs_g[g % 2] = P.op("ac", lambda e, sp2=sp2: e.activation(
                out=stats[:, sp2 + 7 : sp2 + 8], in_=stats[:, sp2 + 6 : sp2 + 7],
                func=AF.Exp, scale=-0.5), ea)

        n_units = n_wbc + n_wcc

        def emit_unit(u):
            if u < n_wbc:
                wbc_chunk(u)
            else:
                wcc_chunk(u - n_wbc)

        if n_p1:
            p1_load(0)
        if n_wbc:
            wcc_load(0)
            wcc_load(1)
        elif n_wcc:
            wcc_load(0)
            wcc_load(1)
        if n_p1:
            p1_stats(0)
        for g in range(n_p1):
            sl = g % 2
            sp2 = 2 + 8 * (g % 2)
            if g + 1 < n_p1:
                p1_load(g + 1)
                p1_stats(g + 1)
            for _j in range(12):
                if 12 * g + _j < n_units:
                    emit_unit(12 * g + _j)

            def xt_ap(sl=sl):
                return f32a[:, F_xb + 1024 * sl : F_xb + 1024 * sl + 1024]

            t_rs = t_rs_g[g % 2]
            if p1_lvl < 2:
                continue
            P.wait("dv", ea, t_rs)
            P.wait("dv", ep, tok_h1b)
            t_h1 = P.op("dv", lambda e, sl=sl, sp2=sp2: e.tensor_scalar(
                out=big[:, B_h1buf : B_h1buf + 1024], in0=xt_ap(sl),
                scalar1=stats[:, sp2 + 2 : sp2 + 3], scalar2=stats[:, sp2 + 7 : sp2 + 8],
                op0=ALU.add, op1=ALU.mult), ev)
            tok_xb[sl] = t_h1
            P.wait("pe", ev, t_h1)
            P.wait("pe", ev, tok_psB[0])
            t_tr = None
            for fc in range(8):
                t_tr = P.op("pe", lambda e, fc=fc: e.transpose(
                    out=psB[0][:, 128 * fc : 128 * fc + 128],
                    in_=big[:, B_h1buf + 128 * fc : B_h1buf + 128 * fc + 128],
                    identity=identb[:]), ep)
            tok_h1b = t_tr
            P.wait("dv", ep, t_tr)
            P.wait("dv", ep, tok_h1fm)
            t_fm = P.op("dv", lambda e: e.tensor_copy(
                out=big[:, B_h1fm : B_h1fm + 1024], in_=psB[0][:]), ev)
            tok_psB[0] = t_fm
            if p1_lvl < 3:
                continue
            P.wait("pe", ev, t_fm)
            P.wait("pe", ev, tok_psA01)
            for nck in range(2):
                for fc in range(8):
                    P.op("pe", lambda e, fc=fc, nck=nck: e.matmul(
                        out=psA[nck][:, :],
                        lhsT=big[:, B_h1fm + 128 * fc : B_h1fm + 128 * fc + 128],
                        rhs=wvT[:, C * fc + 512 * nck : C * fc + 512 * nck + 512],
                        start=(fc == 0), stop=(fc == 7)), ep)
            t_vmm = P.n(ep)
            if p1_lvl < 4:
                continue
            P.wait("pe", ev, tok_psA2)
            for oc in range(2):
                for fc in range(8):
                    P.op("pe", lambda e, oc=oc, fc=fc: e.matmul(
                        out=psA[2][:, 128 * oc : 128 * oc + 128],
                        lhsT=wdT[:, 256 * fc + 128 * oc : 256 * fc + 128 * oc + 128],
                        rhs=big[:, B_h1fm + 128 * fc : B_h1fm + 128 * fc + 128],
                        start=(fc == 0), stop=(fc == 7)), ep)
            t_dmm = P.n(ep)
            tok_h1fm = t_dmm
            P.wait("dv", ep, t_vmm)
            P.wait("dv", dsem(f"vs{sl}"), tok_vstg[sl])
            P.op("dv", lambda e, sl=sl: e.tensor_copy(
                out=big[:, B_vstg + 1024 * sl : B_vstg + 1024 * sl + 512], in_=psA[0][:]), ev)
            t_vst = P.op("dv", lambda e, sl=sl: e.tensor_copy(
                out=big[:, B_vstg + 1024 * sl + 512 : B_vstg + 1024 * sl + 1024],
                in_=psA[1][:]), ev)
            tok_psA01 = t_vst
            P.wait("sp", ev, t_vst)
            t_vd = P.dma("sp", lambda e, g=g, sl=sl: e.dma_start(
                out=bass.AP(tensor=vals, offset=g * 128 * C, ap=[[C, 128], [1, C]]),
                in_=big[:, B_vstg + 1024 * sl : B_vstg + 1024 * sl + 1024]), dsem(f"vs{sl}"))
            if dbg:
                P.wait("pl", ev, t_vst)
                t_dbg = P.dma("pl", lambda e, g=g, sl=sl: e.dma_start(
                    out=dbg_t["d_val"][128 * g : 128 * g + 128, :],
                    in_=big[:, B_vstg + 1024 * sl : B_vstg + 1024 * sl + 1024]), dsem("dbg"))
                P.wait("dv", dsem("dbg"), t_dbg)
            tok_vstg[sl] = t_vd
            if p1_lvl < 5:
                continue
            P.wait("dv", ep, t_dmm)
            P.wait("dv", ep, tok_dispb)
            t_de = P.op("dv", lambda e: e.tensor_copy(
                out=big[:, B_disp : B_disp + 256], in_=psA[2][:, 0:256]), ev)
            tok_psA2 = t_de
            t_mmdf_last = 0
            P.wait("pe", ev, t_de)
            P.wait("pe", ev, tok_psA3[0])
            for hg in range(4 if p1_lvl >= 6 else 0):
                oc, hh = hg // 2, hg % 2
                t_mmdf_last = P.op("pe", lambda e, oc=oc, hh=hh, hg=hg: e.matmul(
                    out=psA[3][:, 128 * hg : 128 * hg + 128],
                    lhsT=dfwb[:, 128 * hh : 128 * hh + 128],
                    rhs=big[:, B_disp + 128 * oc : B_disp + 128 * oc + 128],
                    start=True, stop=True), ep)
            if p1_lvl >= 7:
                P.wait("dv", ep, t_mmdf_last)
                if g == 0:
                    P.wait("dv", dsem("z"), t_zwp)
                t_dfe = P.op("dv", lambda e, g=g: e.tensor_copy(
                    out=bass.AP(tensor=df[:].tensor, offset=64 + 128 * g,
                                ap=[list(df[:].ap[0]), [DF1, 4], [1, 128]]),
                    in_=psA[3][:]), ev)
                tok_psA3[0] = t_dfe
            if p1_lvl >= 6:
                tok_dispb = t_mmdf_last
        for _k in range(12 * n_p1, n_units):
            emit_unit(_k)
        if dbg:
            P.wait("pl", ev)
            P.dma("pl", lambda e: e.dma_start(out=dbg_t["d_df"], in_=df[:, 0 : 4 * DF1]), dsem("dbg"))
        fence()

        tok_args = [0, 0, 0, 0]
        tok_gargs = [0, 0, 0]
        tok_psA_q = [0, 0, 0, 0]
        tok_psA_j0 = [0, 0, 0, 0]
        tok_blob = [0, 0]
        tok_ltm_w = 0
        tok_ltm_r = 0
        tok_ebuf = [0, 0]
        tok_ptm = [0, 0]
        tok_btb = [0, 0]
        tok_bb = [0, 0]
        tok_attn = [0, 0]
        tok_psC_ev = 0
        tok_psC_ac = 0
        tok_xres = [0, 0]
        tok_x2b = [0, 0]
        tok_vring = [0, 0, 0]

        if stop_after == "p2w":
            NSUBX = 0
        _stopped2 = stop_after in ("consts", "wa", "wb", "wc", "p1")
        for S in range(NSUP2 if not _stopped2 else 0):
            T0 = S * P2S
            for hg in range(4):
                dfo = DF1 * hg
                t_diff = [0, 0, 0, 0]
                t_eb_prev = 0

                def blob_ap(q2, r0, r1, c0, c1):
                    if q2 % 2 == 0:
                        return big[r0:r1, B_blob + c0 : B_blob + c1]
                    return ternb[r0:r1, c0:c1]

                def wl_tail(q2, t_eb2):
                    nonlocal tok_ltm_r
                    P.wait("pe", ev, t_eb2)
                    P.wait("pe", ev, tok_psB[0])
                    P.wait("pe", ev, tok_psB[1])
                    t_tr = None
                    for tc in range(NQC):
                        for i in range(4):
                            t_tr = P.op("pe", lambda e, tc=tc, i=i, q2=q2: e.transpose(
                                out=psB[tc % 2][:, 512 * (tc // 2) + 128 * i : 512 * (tc // 2) + 128 * i + 128],
                                in_=blob_ap(q2, 0, 128, 512 * tc + 128 * i, 512 * tc + 128 * i + 128),
                                identity=identb[:]), ep)
                    tok_blob[q2 % 2] = t_tr
                    P.wait("dv", ep, t_tr)
                    if q2 == 0:
                        P.wait("dv", ea, tok_ltm_r)
                    t_cp2 = None
                    for tc in range(NQC):
                        t_cp2 = P.op("dv", lambda e, tc=tc, q2=q2: e.tensor_copy(
                            out=bass.AP(tensor=wvT[:].tensor,
                                        offset=512 * (tc * 4) + 32 * q2,
                                        ap=[list(wvT[:].ap[0]), [512, 4], [8, 4], [1, 8]]),
                            in_=bass.AP(tensor=psB[tc % 2][:].tensor,
                                        offset=512 * (tc // 2),
                                        ap=[list(psB[tc % 2][:].ap[0]), [128, 4], [32, 4], [1, 8]])), ev)
                    tok_psB[0] = t_cp2
                    tok_psB[1] = t_cp2

                def w_diff2(q2, h2, dfo=dfo, T0=T0):
                    for j2 in (0, 1) if h2 == 0 else (2, 3):
                        w2 = 4 * q2 + j2
                        P.wait("dv", ea, tok_args[j2])
                        t_diff[j2] = P.op("dv", lambda e, w2=w2, j2=j2, dfo=dfo, T0=T0: e.tensor_tensor(
                            out=big[:, B_args + 2048 * j2 : B_args + 2048 * j2 + P2S],
                            in0=df[:, dfo + 1 + T0 + w2 : dfo + 1 + T0 + w2 + P2S],
                            in1=df[:, dfo + 64 + T0 : dfo + 64 + T0 + P2S],
                            op=ALU.subtract), ev)

                def w_diff(q2, dfo=dfo, T0=T0):
                    for j2 in range(4):
                        w2 = 4 * q2 + j2
                        P.wait("dv", ea, tok_args[j2])
                        t_diff[j2] = P.op("dv", lambda e, w2=w2, j2=j2, dfo=dfo, T0=T0: e.tensor_tensor(
                            out=big[:, B_args + 2048 * j2 : B_args + 2048 * j2 + P2S],
                            in0=df[:, dfo + 1 + T0 + w2 : dfo + 1 + T0 + w2 + P2S],
                            in1=df[:, dfo + 64 + T0 : dfo + 64 + T0 + P2S],
                            op=ALU.subtract), ev)

                if p2_lvl >= 0:
                    w_diff(0)
                for q in range(16 if p2_lvl >= 1 else 0):
                    t_lm = None
                    for j in range(4):
                        w = 4 * q + j
                        gsl = w % 3
                        P.wait("ac", ev, t_diff[j])
                        P.wait("ac", ep, tok_gargs[gsl])
                        t_gel = P.op("ac", lambda e, w=w, j=j, gsl=gsl: e.activation(
                            out=big[:, B_gargs + 2048 * gsl : B_gargs + 2048 * gsl + P2S],
                            in_=big[:, B_args + 2048 * j : B_args + 2048 * j + P2S],
                            func=AF.Gelu, bias=c_all[:, w : w + 1]), ea)
                        tok_args[j] = t_gel
                        if p2_lvl < 2:
                            tok_gargs[gsl] = 0
                            continue
                        P.wait("pe", ea, t_gel)
                        for tc in range(NQC):
                            if j == 0:
                                P.wait("pe", ev, tok_psA_j0[tc])
                            elif j == 1:
                                P.wait("pe", ev, tok_psA_q[tc])
                            t_lm = P.op("pe", lambda e, gsl=gsl, tc=tc, j=j: e.matmul(
                                out=psA[tc][32 * j : 32 * j + 8, :],
                                lhsT=logit_w[:, :],
                                rhs=big[:, B_gargs + 2048 * gsl + 512 * tc : B_gargs + 2048 * gsl + 512 * tc + 512],
                                start=True, stop=True, tile_position=(0, 32 * j)), ep)
                        tok_gargs[gsl] = t_lm
                    if q + 1 < 16:
                        w_diff2(q + 1, 0)
                    if p2_lvl >= 3:
                        P.wait("dv", ep, t_lm)
                        P.wait("dv", ep, tok_blob[q % 2])
                        if hg == 0 and q == 0:
                            P.wait("dv", dsem("x20"), tok_x2b[0])
                            P.wait("dv", dsem("x21"), tok_x2b[1])
                        t_eb = None
                        for tc in range(NQC):
                            t_eb = P.op("dv", lambda e, tc=tc, q=q: e.tensor_copy(
                                out=blob_ap(q, 0, 128, 512 * tc, 512 * tc + 512),
                                in_=psA[tc][:]), ev)
                        for tc in range(NQC):
                            tok_psA_j0[tc] = t_eb
                            tok_psA_q[tc] = t_eb
                        if q + 1 < 16:
                            w_diff2(q + 1, 1)
                        if q > 0:
                            wl_tail(q - 1, t_eb_prev)
                        t_eb_prev = t_eb
                    elif q + 1 < 16:
                        w_diff2(q + 1, 1)
                tok_ltm_w_pending = True
                if p2_lvl >= 3:
                    wl_tail(15, t_eb_prev)
                tok_ltm_w = P.n(ev)
                if dbg and S == 0 and hg == 0:
                    P.wait("pl", ev, tok_ltm_w)
                    t_dl = P.dma("pl", lambda e: e.dma_start(out=dbg_t["d_ltm"], in_=wvT[:, 0 : 16 * 512]), dsem("dbg"))
                    P.wait("dv", dsem("dbg"), t_dl)
                for s_sub in range(NSUB if stop_after != "p2w" else 0):
                    sl = s_sub % 2
                    lt_off = 512 * s_sub
                    P.wait("ac", ev, tok_ltm_w)
                    t_sg = P.op("ac", lambda e, lt_off=lt_off: e.activation(
                        out=f32a[:, F_sc + 4 : F_sc + 260],
                        in_=bass.AP(tensor=wvT[:].tensor, offset=lt_off + 4,
                                    ap=[list(wvT[:].ap[0]), [8, WIN], [1, 4]]),
                        func=AF.Tanh, scale=0.5, bias=dbcol[:]), ea)
                    tok_ltm_r = t_sg
                    P.wait("dv", ea, t_sg)
                    t_lg = P.op("dv", lambda e, lt_off=lt_off: e.scalar_tensor_tensor(
                        out=f32a[:, F_sc + 260 : F_sc + 516],
                        in0=f32a[:, F_sc + 4 : F_sc + 260],
                        scalar=-5.0, op0=ALU.mult,
                        in1=bass.AP(tensor=wvT[:].tensor, offset=lt_off,
                                    ap=[list(wvT[:].ap[0]), [8, WIN], [1, 4]]),
                        op1=ALU.add), ev)
                    if S == 0 and s_sub == 0:
                        t_lg = P.op("dv", lambda e: e.tensor_tensor(
                            out=f32a[:, F_sc + 260 : F_sc + 516],
                            in0=f32a[:, F_sc + 260 : F_sc + 516],
                            in1=bass.AP(tensor=maskneg[:].tensor, offset=0,
                                        ap=[list(maskneg[:].ap[0]), [1, WIN], [0, 4]]),
                            op=ALU.add), ev)
                    P.op("dv", lambda e: e.tensor_reduce(
                        out=stats[:, 16:20],
                        in_=bass.AP(tensor=f32a[:].tensor, offset=F_sc + 260,
                                    ap=[list(f32a[:].ap[0]), [1, 4], [4, WIN]]),
                        axis=AX.X, op=ALU.max, negate=True), ev)
                    t_ei = P.op("dv", lambda e: e.tensor_tensor(
                        out=f32a[:, F_sc + 260 : F_sc + 516],
                        in0=f32a[:, F_sc + 260 : F_sc + 516],
                        in1=bass.AP(tensor=stats[:].tensor, offset=16,
                                    ap=[list(stats[:].ap[0]), [0, WIN], [1, 4]]),
                        op=ALU.add), ev)
                    P.wait("ac", ev, t_ei)
                    P.wait("ac", ev, tok_ebuf[sl])
                    t_ex = P.op("ac", lambda e, sl=sl: e.activation(
                        out=big[:, B_ebuf + 256 * sl : B_ebuf + 256 * sl + 256],
                        in_=f32a[:, F_sc + 260 : F_sc + 516], func=AF.Exp), ea)
                    P.wait("dv", ea, t_ex)
                    P.op("dv", lambda e, sl=sl: e.tensor_reduce(
                        out=stats[:, 20:24],
                        in_=bass.AP(tensor=big[:].tensor, offset=B_ebuf + 256 * sl,
                                    ap=[list(big[:].ap[0]), [1, 4], [4, WIN]]),
                        axis=AX.X, op=ALU.add), ev)
                    P.op("dv", lambda e: e.reciprocal(out=stats[:, 24:28], in_=stats[:, 20:24]), ev)
                    P.wait("dv", dsem(f"sk{sl}"), tok_ptm[sl])
                    t_p = P.op("dv", lambda e, sl=sl: e.tensor_tensor(
                        out=bass.AP(tensor=big[:].tensor, offset=B_ptm + 256 * sl,
                                    ap=[list(big[:].ap[0]), [1, WIN], [WIN, 4]]),
                        in0=big[:, B_ebuf + 256 * sl : B_ebuf + 256 * sl + 256],
                        in1=bass.AP(tensor=stats[:].tensor, offset=24,
                                    ap=[list(stats[:].ap[0]), [0, WIN], [1, 4]]),
                        op=ALU.mult), ev)
                    tok_ebuf[sl] = t_p
                    if dbg and S == 0 and hg == 0:
                        P.wait("pl", ev, t_p)
                        t_dbg = P.dma("pl", lambda e, s_sub=s_sub, sl=sl: e.dma_start(
                            out=dbg_t["d_ptm"][:, 256 * s_sub : 256 * s_sub + 256],
                            in_=big[:, B_ptm + 256 * sl : B_ptm + 256 * sl + 256]), dsem("dbg"))
                        P.wait("dv", dsem("dbg"), t_dbg)
                    P.wait("sp", ev, t_p)
                    if S == 0 and hg == 0 and s_sub == sl:
                        P.wait("sp", dsem("z"), t_zwp)
                    for r0, r1 in splits(4):
                        tok_ptm[sl] = P.dma("sp", lambda e, hg=hg, s_sub=s_sub, sl=sl, T0=T0, r0=r0, r1=r1: e.dma_start(
                            out=bass.AP(tensor=wp,
                                        offset=(T0 + 128 * s_sub + r0) * NH * PAD + (hg * 4) * PAD + 65 + r0,
                                        ap=[[NH * PAD + 1, r1 - r0], [PAD, 4], [1, WIN]]),
                            in_=big[r0:r1, B_ptm + 256 * sl : B_ptm + 256 * sl + 256].rearrange(
                                "p (a b) -> p a b", a=4)), dsem(f"sk{sl}"))
            t_skew_all = (P.n(dsem("sk0")), P.n(dsem("sk1")))
            nsub_b = NSUB if stop_after not in ("p2a", "p2w") else 0
            tbt = [0, 0]
            tvl = [0, 0, 0]
            txr = [0, 0]

            def bl_load(bs2, T0=T0):
                gb = (T0 // 128) + bs2
                t0 = 128 * gb
                bb_ = gb % 2
                btbase = B_btb if bb_ == 0 else B_btb2
                P.wait("sp", dsem("sk0"), t_skew_all[0])
                P.wait("sp", dsem("sk1"), t_skew_all[1])
                P.wait("sp", ep, tok_btb[bb_])
                for r0, r1 in splits(2):
                    tbt[bb_] = P.dma("sp", lambda e, t0=t0, r0=r0, r1=r1, btbase=btbase: e.dma_start(
                        out=big[r0:r1, btbase : btbase + 4096],
                        in_=bass.AP(tensor=wp, offset=(t0 + r0) * NH * PAD,
                                    ap=[[NH * PAD, r1 - r0], [1, NH * PAD]])), dsem(f"bt{bb_}"))
                vsl = gb % 3
                P.wait("sp", ep, tok_vring[vsl])
                for r0, r1 in splits(2):
                    tvl[vsl] = P.dma("sp", lambda e, gb=gb, vsl=vsl, r0=r0, r1=r1: e.dma_start(
                        out=big[r0:r1, B_vring + 1024 * vsl : B_vring + 1024 * vsl + 1024],
                        in_=bass.AP(tensor=vals, offset=(gb * 128 + r0) * C,
                                    ap=[[C, r1 - r0], [1, C]])), dsem(f"vr{vsl}"))
                xsl = gb % 2
                P.wait("sp", ev, tok_xres[xsl])
                for r0, r1 in splits(4):
                    txr[xsl] = P.dma("sp", lambda e, t0=t0, xsl=xsl, r0=r0, r1=r1: e.dma_start(
                        out=f32a[r0:r1, F_xb + 1024 * xsl : F_xb + 1024 * xsl + 1024],
                        in_=x_in[t0 + r0 : t0 + r1, :]), dsem(f"xr{xsl}"))

            if nsub_b:
                bl_load(0)
            for bs_ in range(nsub_b):
                gb = (T0 // 128) + bs_
                t0 = 128 * gb
                bb_ = gb % 2
                btbase = B_btb if bb_ == 0 else B_btb2
                if bs_ + 1 < nsub_b:
                    bl_load(bs_ + 1)
                vsl = gb % 3
                xsl = gb % 2
                t_bt = tbt[bb_]
                t_vl = tvl[vsl]
                t_xr = txr[xsl]
                t_band = None
                for g2 in range(2):
                    P.wait("pe", dsem(f"bt{bb_}"), t_bt)
                    P.wait("pe", ev, tok_psB[0])
                    P.wait("pe", ev, tok_psB[1])
                    t_tr = None
                    for k in range(8):
                        h = 8 * g2 + k
                        for half in range(2):
                            t_tr = P.op("pe", lambda e, h=h, k=k, half=half, btbase=btbase: e.transpose(
                                out=psB[k % 2][:, 256 * (k // 2) + 128 * half : 256 * (k // 2) + 128 * half + 128],
                                in_=big[:, btbase + 256 * h + 128 * half : btbase + 256 * h + 128 * half + 128],
                                identity=identb[:]), ep)
                    if g2 == 1:
                        tok_btb[bb_] = t_tr
                    P.wait("ac", ep, t_tr)
                    P.wait("ac", ep, tok_bb[0])
                    t_b1 = P.op("ac", lambda e: e.activation(
                        out=big[:, B_bb : B_bb + 1024], in_=psB[0][:], func=AF.Identity), ea)
                    P.wait("dv", ep, t_tr)
                    P.wait("dv", ep, tok_bb[0])
                    P.wait("dv", ea, t_b1)
                    t_bbe = P.op("dv", lambda e: e.tensor_copy(
                        out=big[:, B_bb + 1024 : B_bb + 2048], in_=psB[1][:]), ev)
                    tok_psB[0] = t_bbe
                    tok_psB[1] = t_bbe
                    if p2_lvl < 5:
                        continue
                    P.wait("pe", ev, t_bbe)
                    P.wait("pe", dsem(f"vr{vsl}"), t_vl)
                    if g2 == 0:
                        P.wait("pe", ev, tok_psC_ev)
                        P.wait("pe", ea, tok_psC_ac)
                    for k in range(8):
                        h = 8 * g2 + k
                        bank = psC[g2]
                        r0 = 64 * (h % 2)
                        c0 = 128 * ((h // 2) % 4)
                        boff = B_bb + 1024 * (k % 2) + 256 * (k // 2)
                        if gb == 0:
                            t_band = P.op("pe", lambda e, h=h, bank=bank, r0=r0, c0=c0, vsl=vsl, boff=boff: e.matmul(
                                out=bank[r0 : r0 + 64, c0 : c0 + 128],
                                lhsT=big[:, B_vring + 1024 * vsl + 64 * h : B_vring + 1024 * vsl + 64 * h + 64],
                                rhs=big[:, boff + 128 : boff + 256],
                                start=True, stop=True), ep)
                        else:
                            pv = (gb + 2) % 3
                            P.op("pe", lambda e, h=h, bank=bank, r0=r0, c0=c0, pv=pv, boff=boff: e.matmul(
                                out=bank[r0 : r0 + 64, c0 : c0 + 128],
                                lhsT=big[:, B_vring + 1024 * pv + 64 * h : B_vring + 1024 * pv + 64 * h + 64],
                                rhs=big[:, boff : boff + 128],
                                start=True, stop=False), ep)
                            t_band = P.op("pe", lambda e, h=h, bank=bank, r0=r0, c0=c0, vsl=vsl, boff=boff: e.matmul(
                                out=bank[r0 : r0 + 64, c0 : c0 + 128],
                                lhsT=big[:, B_vring + 1024 * vsl + 64 * h : B_vring + 1024 * vsl + 64 * h + 64],
                                rhs=big[:, boff + 128 : boff + 256],
                                start=False, stop=True), ep)
                    tok_bb[0] = t_band
                tok_vring[vsl] = t_band
                if gb > 0:
                    tok_vring[(gb + 2) % 3] = t_band
                if p2_lvl < 6:
                    continue
                asl = gb % 2
                P.wait("ac", ep, t_band)
                P.wait("ac", ep, tok_attn[asl])
                P.op("ac", lambda e, asl=asl: e.activation(
                    out=big[:, B_attn + 1024 * asl : B_attn + 1024 * asl + 512], in_=psC[0][:],
                    func=AF.Identity), ea)
                t_ae2 = P.op("ac", lambda e, asl=asl: e.activation(
                    out=big[:, B_attn + 1024 * asl + 512 : B_attn + 1024 * asl + 1024], in_=psC[1][:],
                    func=AF.Identity), ea)
                P.wait("pe", ea, t_ae2)
                for ncx in range(2):
                    for ch in range(8):
                        P.op("pe", lambda e, ncx=ncx, ch=ch, asl=asl: e.matmul(
                            out=psC[ncx][:, :],
                            lhsT=big[:, B_attn + 1024 * asl + 128 * ch : B_attn + 1024 * asl + 128 * ch + 128],
                            rhs=cpT[:, C * ch + 512 * ncx : C * ch + 512 * ncx + 512],
                            start=(ch == 0), stop=(ch == 7)), ep)
                t_cpj = P.n(ep)
                tok_attn[asl] = t_cpj
                if p2_lvl < 7:
                    continue
                P.wait("dv", ep, t_cpj)
                P.op("dv", lambda e: e.tensor_copy(out=f32a[:, F_sc : F_sc + 512], in_=psC[0][:]), ev)
                t_yev = P.op("dv", lambda e: e.tensor_copy(out=f32a[:, F_sc + 512 : F_sc + 1024], in_=psC[1][:]), ev)
                tok_psC_ev = t_yev
                tok_psC_ac = 0
                P.op("dv", lambda e: e.tensor_reduce(out=stats[:, 36:37],
                                                     in_=f32a[:, F_sc : F_sc + 1024],
                                                     axis=AX.X, op=ALU.add), ev)
                P.wait("ac", ev, t_yev)
                t_sq2 = P.op("ac", lambda e: e.activation(out=junk[:], in_=f32a[:, F_sc : F_sc + 1024],
                                                          func=AF.Square, accum_out=stats[:, 37:38]), ea)
                P.wait("dv", ea, t_sq2)
                P.op("dv", lambda e: e.tensor_scalar(out=stats[:, 38:39], in0=stats[:, 36:37],
                                                     scalar1=-1.0 / C, scalar2=None, op0=ALU.mult), ev)
                P.op("dv", lambda e: e.tensor_scalar(out=stats[:, 39:40], in0=stats[:, 37:38],
                                                     scalar1=1.0 / C, scalar2=None, op0=ALU.mult), ev)
                P.op("dv", lambda e: e.tensor_tensor(out=stats[:, 40:41], in0=stats[:, 38:39],
                                                     in1=stats[:, 38:39], op=ALU.mult), ev)
                t_v2 = P.op("dv", lambda e: e.tensor_tensor(out=stats[:, 41:42], in0=stats[:, 39:40],
                                                            in1=stats[:, 40:41], op=ALU.subtract), ev)
                P.wait("ac", ev, t_v2)
                P.op("ac", lambda e: e.activation(out=stats[:, 42:43], in_=stats[:, 41:42],
                                                  func=AF.Ln, bias=epscol[:, 0:1]), ea)
                t_r2 = P.op("ac", lambda e: e.activation(out=stats[:, 43:44], in_=stats[:, 42:43],
                                                         func=AF.Exp, scale=-0.5), ea)
                if p2_lvl < 8:
                    continue
                P.wait("dv", ea, t_r2)
                P.wait("dv", dsem(f"xr{xsl}"), t_xr)
                P.wait("dv", dsem(f"x2{asl}"), tok_x2b[asl])
                t_y1 = P.op("dv", lambda e: e.tensor_scalar(
                    out=f32a[:, F_sc : F_sc + 1024], in0=f32a[:, F_sc : F_sc + 1024],
                    scalar1=stats[:, 38:39], scalar2=stats[:, 43:44],
                    op0=ALU.add, op1=ALU.mult), ev)
                P.wait("dv", ep, tok_blob[0])
                t_x2 = P.op("dv", lambda e, xsl=xsl, asl=asl: e.tensor_tensor(
                    out=big[:, B_x2b + 1024 * asl : B_x2b + 1024 * asl + 1024],
                    in0=f32a[:, F_sc : F_sc + 1024],
                    in1=f32a[:, F_xb + 1024 * xsl : F_xb + 1024 * xsl + 1024],
                    op=ALU.add), ev)
                tok_xres[xsl] = t_x2
                P.wait("sp", ev, t_x2)
                t_x2d = P.dma("sp", lambda e, t0=t0, asl=asl: e.dma_start(
                    out=bass.AP(tensor=x2s, offset=t0 * C, ap=[[C, 128], [1, C]]),
                    in_=big[:, B_x2b + 1024 * asl : B_x2b + 1024 * asl + 1024]), dsem(f"x2{asl}"))
                if dbg:
                    P.wait("pl", ev, t_x2)
                    t_dbg = P.dma("pl", lambda e, t0=t0, asl=asl: e.dma_start(
                        out=dbg_t["d_x2"][t0 : t0 + 128, :],
                        in_=big[:, B_x2b + 1024 * asl : B_x2b + 1024 * asl + 1024]), dsem("dbg"))
                    P.wait("dv", dsem("dbg"), t_dbg)
                tok_x2b[asl] = t_x2d
        fence()

        for r0, r1 in splits(4):
            t_wd = P.dma("sp", lambda e, r0=r0, r1=r1: e.dma_start(
                out=fp8a[r0:r1, 0:32768],
                in_=bass.AP(tensor=wd8, offset=C * r0,
                            ap=[[C, r1 - r0], [128 * C, 32], [1, C]])), dsem("wd"))

        tok_x2t = [0, 0]
        tok_a1i = 0
        tok_a1b = 0
        tok_a1fm = 0
        tok_gw = [0, 0]
        tok_uw = [0, 0]
        tok_gA = [0, 0]
        tok_uA = [0, 0]
        tok_sil = 0
        tok_a2tm = 0
        tok_a2i = 0
        tok_a2fm = 0
        tok_ybuf = 0
        tok_scr_ac = 0
        tok_ob = [0, 0]
        tok_obx = 0
        tok_xtb = 0
        tok_g = 0

        _stopped3 = stop_after in ("consts", "wa", "wb", "wc", "p1", "p2", "p2w", "p2a")
        _NO_ILV = False
        t_ic_done_prev = 0
        t_sg3 = [0] * 8
        t_fm3 = [0] * 8
        t_dn3 = [0] * 8

        def ln4_tt(tt, sb):
            nonlocal tok_psC_ac, tok_ybuf
            P.wait("dv", ep, t_dn3[tt])
            P.op("dv", lambda e, tt=tt: e.tensor_copy(
                out=cpT[:, 1024 * tt : 1024 * tt + 512], in_=psC[0][:]), ev)
            t_ye2 = P.op("dv", lambda e, tt=tt: e.tensor_copy(
                out=cpT[:, 1024 * tt + 512 : 1024 * tt + 1024], in_=psC[1][:]), ev)
            P.op("dv", lambda e, tt=tt, sb=sb: e.tensor_reduce(
                out=stats[:, sb + 56 : sb + 57], in_=cpT[:, 1024 * tt : 1024 * tt + 1024],
                axis=AX.X, op=ALU.add), ev)
            P.wait("ac", ev, t_ye2)
            t_sq4 = P.op("ac", lambda e, tt=tt, sb=sb: e.activation(
                out=junk[:], in_=cpT[:, 1024 * tt : 1024 * tt + 1024],
                func=AF.Square, accum_out=stats[:, sb + 58 : sb + 59]), ea)
            tok_psC_ac = t_sq4
            P.op("dv", lambda e, tt=tt, sb=sb: e.tensor_scalar(out=stats[:, sb + 40 + tt : sb + 41 + tt],
                                                        in0=stats[:, sb + 56 : sb + 57],
                                                        scalar1=-1.0 / C, scalar2=None, op0=ALU.mult), ev)
            P.wait("dv", ea, t_sq4)
            P.op("dv", lambda e, tt=tt, sb=sb: e.tensor_scalar(out=stats[:, sb + 48 + tt : sb + 49 + tt],
                                                        in0=stats[:, sb + 58 : sb + 59],
                                                        scalar1=1.0 / C, scalar2=None, op0=ALU.mult), ev)
            P.op("dv", lambda e, tt=tt, sb=sb: e.tensor_tensor(out=stats[:, sb + 56 : sb + 57],
                                                        in0=stats[:, sb + 40 + tt : sb + 41 + tt],
                                                        in1=stats[:, sb + 40 + tt : sb + 41 + tt], op=ALU.mult), ev)
            P.op("dv", lambda e, tt=tt, sb=sb: e.tensor_tensor(out=stats[:, sb + 48 + tt : sb + 49 + tt],
                                                        in0=stats[:, sb + 48 + tt : sb + 49 + tt],
                                                        in1=stats[:, sb + 56 : sb + 57], op=ALU.subtract), ev)
            tok_ybuf = t_ye2

        def ln3_tt(F2, tt, t_icd):
            nonlocal tok_scr_ac, tok_a1i, tok_a1b
            sb = 64 * (F2 % 2)
            TF2 = F2 * P3S
            sl = tt % 2
            P.wait("sp", ea, tok_x2t[sl])
            for r0, r1 in splits(2):
                t_xl = P.dma("sp", lambda e, TF2=TF2, tt=tt, sl=sl, r0=r0, r1=r1: e.dma_start(
                    out=ternb[r0:r1, 1024 * sl : 1024 * sl + 1024],
                    in_=bass.AP(tensor=x2s, offset=(TF2 + 128 * tt + r0) * C,
                                ap=[[C, r1 - r0], [1, C]])), dsem(f"xt{sl}"))

            def x2t_ap(sl=sl):
                return ternb[:, 1024 * sl : 1024 * sl + 1024]

            P.wait("dv", dsem(f"xt{sl}"), t_xl)
            P.wait("dv", ea, tok_scr_ac)
            P.op("dv", lambda e, sl=sl: e.tensor_reduce(
                out=stats[:, sb + 56 : sb + 57], in_=x2t_ap(sl), axis=AX.X, op=ALU.add), ev)
            P.op("dv", lambda e, sl=sl: e.tensor_reduce(
                out=stats[:, sb + 58 : sb + 59], in_=x2t_ap(sl), axis=AX.X, op=ALU.max), ev)
            P.op("dv", lambda e, sl=sl: e.tensor_reduce(
                out=stats[:, sb + 59 : sb + 60], in_=x2t_ap(sl), axis=AX.X, op=ALU.min, negate=True), ev)
            P.wait("ac", dsem(f"xt{sl}"), t_xl)
            t_sq = P.op("ac", lambda e, sl=sl: e.activation(
                out=junk[:], in_=x2t_ap(sl), func=AF.Square, accum_out=stats[:, sb + 57 : sb + 58]), ea)
            P.op("dv", lambda e, tt=tt: e.tensor_scalar(
                out=stats[:, sb + 0 + tt : sb + 1 + tt], in0=stats[:, sb + 56 : sb + 57],
                scalar1=-1.0 / C, scalar2=None, op0=ALU.mult), ev)
            P.wait("dv", ea, t_sq)
            P.op("dv", lambda e: e.tensor_scalar(out=stats[:, sb + 60 : sb + 61], in0=stats[:, sb + 57 : sb + 58],
                                                 scalar1=1.0 / C, scalar2=None, op0=ALU.mult), ev)
            P.op("dv", lambda e, tt=tt: e.tensor_tensor(out=stats[:, sb + 61 : sb + 62],
                                                        in0=stats[:, sb + 0 + tt : sb + 1 + tt],
                                                        in1=stats[:, sb + 0 + tt : sb + 1 + tt], op=ALU.mult), ev)
            t_v3 = P.op("dv", lambda e: e.tensor_tensor(out=stats[:, sb + 62 : sb + 63], in0=stats[:, sb + 60 : sb + 61],
                                                        in1=stats[:, sb + 61 : sb + 62], op=ALU.subtract), ev)
            P.wait("ac", ev, t_v3)
            P.op("ac", lambda e: e.activation(out=stats[:, sb + 63 : sb + 64], in_=stats[:, sb + 62 : sb + 63],
                                              func=AF.Ln, bias=epscol[:, 0:1]), ea)
            t_r3 = P.op("ac", lambda e, tt=tt: e.activation(
                out=stats[:, sb + 8 + tt : sb + 9 + tt], in_=stats[:, sb + 63 : sb + 64],
                func=AF.Exp, scale=-0.5), ea)
            P.wait("dv", ea, t_r3)
            P.op("dv", lambda e, tt=tt: e.tensor_scalar(out=stats[:, sb + 58 : sb + 59], in0=stats[:, sb + 58 : sb + 59],
                                                        scalar1=stats[:, sb + 0 + tt : sb + 1 + tt],
                                                        scalar2=None, op0=ALU.add), ev)
            P.op("dv", lambda e, tt=tt: e.tensor_scalar(out=stats[:, sb + 59 : sb + 60], in0=stats[:, sb + 59 : sb + 60],
                                                        scalar1=stats[:, sb + 0 + tt : sb + 1 + tt],
                                                        scalar2=None, op0=ALU.add), ev)
            P.op("dv", lambda e: e.tensor_tensor(out=stats[:, sb + 58 : sb + 59], in0=stats[:, sb + 58 : sb + 59],
                                                 in1=stats[:, sb + 59 : sb + 60], op=ALU.max), ev)
            P.op("dv", lambda e, tt=tt: e.tensor_tensor(out=stats[:, sb + 16 + tt : sb + 17 + tt],
                                                        in0=stats[:, sb + 58 : sb + 59],
                                                        in1=stats[:, sb + 8 + tt : sb + 9 + tt], op=ALU.mult), ev)
            P.op("dv", lambda e, tt=tt: e.tensor_scalar(out=stats[:, sb + 60 : sb + 61],
                                                        in0=stats[:, sb + 16 + tt : sb + 17 + tt],
                                                        scalar1=1e-5, scalar2=None, op0=ALU.max), ev)
            P.op("dv", lambda e: e.reciprocal(out=stats[:, sb + 61 : sb + 62], in_=stats[:, sb + 60 : sb + 61]), ev)
            P.op("dv", lambda e: e.tensor_scalar(out=stats[:, sb + 62 : sb + 63], in0=stats[:, sb + 61 : sb + 62],
                                                 scalar1=127.0, scalar2=None, op0=ALU.mult), ev)
            P.op("dv", lambda e, tt=tt: e.tensor_tensor(out=stats[:, sb + 61 : sb + 62], in0=stats[:, sb + 62 : sb + 63],
                                                        in1=stats[:, sb + 8 + tt : sb + 9 + tt], op=ALU.mult), ev)
            t_ab = P.op("dv", lambda e, tt=tt: e.tensor_tensor(out=stats[:, sb + 63 : sb + 64],
                                                               in0=stats[:, sb + 61 : sb + 62],
                                                               in1=stats[:, sb + 0 + tt : sb + 1 + tt],
                                                               op=ALU.mult), ev)
            P.op("dv", lambda e, tt=tt: e.tensor_tensor(out=stats[:, sb + 24 + tt : sb + 25 + tt],
                                                        in0=stats[:, sb + 16 + tt : sb + 17 + tt],
                                                        in1=scal[:, 6:7], op=ALU.mult), ev)
            P.op("dv", lambda e, tt=tt: e.tensor_scalar(out=stats[:, sb + 24 + tt : sb + 25 + tt],
                                                        in0=stats[:, sb + 24 + tt : sb + 25 + tt],
                                                        scalar1=1.0 / 254.0, scalar2=None, op0=ALU.mult), ev)
            t_sg3[tt] = P.n(ev)
            P.op("dv", lambda e, tt=tt: e.tensor_tensor(out=stats[:, sb + 32 + tt : sb + 33 + tt],
                                                        in0=stats[:, sb + 16 + tt : sb + 17 + tt],
                                                        in1=scal[:, 7:8], op=ALU.mult), ev)
            P.op("dv", lambda e, tt=tt: e.tensor_tensor(out=stats[:, sb + 32 + tt : sb + 33 + tt],
                                                        in0=stats[:, sb + 32 + tt : sb + 33 + tt],
                                                        in1=scal[:, 8:9], op=ALU.mult), ev)
            P.op("dv", lambda e, tt=tt: e.tensor_scalar(out=stats[:, sb + 32 + tt : sb + 33 + tt],
                                                        in0=stats[:, sb + 32 + tt : sb + 33 + tt],
                                                        scalar1=1.0 / 64516.0, scalar2=None, op0=ALU.mult), ev)
            P.wait("ac", ev, t_ab)
            P.wait("ac", ev, tok_a1i)
            t_ai = P.op("ac", lambda e, sl=sl: e.activation(
                out=i16a[:, 2048:3072], in_=x2t_ap(sl), func=AF.Identity,
                scale=stats[:, sb + 61 : sb + 62], bias=stats[:, sb + 63 : sb + 64]), ea)
            tok_x2t[sl] = t_ai
            tok_scr_ac = t_ai
            P.wait("dv", ea, t_ai)
            P.wait("dv", ep, tok_a1b)
            t_a1b = P.op("dv", lambda e: e.tensor_copy(out=wdT[:, 0:1024], in_=i16a[:, 2048:3072]), ev)
            tok_a1i = t_a1b
            P.wait("pe", ev, t_a1b)
            P.wait("pe", ev, tok_psB[0])
            t_tr = None
            for fc in range(8):
                t_tr = P.op("pe", lambda e, fc=fc: e.transpose(
                    out=psB[0][:, 128 * fc : 128 * fc + 128],
                    in_=wdT[:, 128 * fc : 128 * fc + 128], identity=identb[:]), ep)
            tok_a1b = t_tr
            P.wait("dv", ep, t_tr)
            if tt == 0:
                P.wait("dv", ep, t_icd)
            t_fm = P.op("dv", lambda e, tt=tt: e.tensor_copy(
                out=bass.AP(tensor=df[:].tensor, offset=128 * tt,
                            ap=[list(df[:].ap[0]), [P3S, 8], [1, 128]]),
                in_=psB[0][:].rearrange("p (a b) -> p a b", a=8)), ev)
            tok_psB[0] = t_fm
            t_fm3[tt] = t_fm


        for F in range(NSUP3 if not _stopped3 else 0):
            TF = F * P3S
            sb = 64 * (F % 2)
            if F == 0 or _NO_ILV:
                for tt in range(NTT):
                    ln3_tt(F, tt, t_ic_done_prev)
            t_a1_all = P.n(ev)
            for ic in range(8 if p3_lvl >= 2 else 0):
                wsl = ic % 2
                P.wait("sp", ep, tok_gw[wsl])
                for r0, r1 in splits(2):
                    t_gl = P.dma("sp", lambda e, ic=ic, wsl=wsl, r0=r0, r1=r1: e.dma_start(
                        out=fp8a[r0:r1, 32768 + 4096 * wsl : 32768 + 4096 * wsl + 4096],
                        in_=bass.AP(tensor=wg8, offset=512 * ic + INNER * r0,
                                    ap=[[INNER, r1 - r0], [128 * INNER, 8], [1, 512]])), dsem(f"wg{wsl}"))
                P.wait("sp", ep, tok_uw[wsl])
                for r0, r1 in splits(2):
                    t_ul = P.dma("sp", lambda e, ic=ic, wsl=wsl, r0=r0, r1=r1: e.dma_start(
                        out=fp8a[r0:r1, 40960 + 4096 * wsl : 40960 + 4096 * wsl + 4096],
                        in_=bass.AP(tensor=wu8, offset=512 * ic + INNER * r0,
                                    ap=[[INNER, r1 - r0], [128 * INNER, 8], [1, 512]])), dsem(f"wu{wsl}"))
                t_gmm = t_umm = None
                for tt in range(NTT):
                    ba = tt % 2
                    P.wait("pe", ev, t_fm3[tt] if ic == 0 else t_a1_all)
                    P.wait("pe", dsem(f"wg{wsl}"), t_gl)
                    P.wait("pe", ea, tok_gA[ba])
                    for fc in range(8):
                        P.op("pe", lambda e, tt=tt, fc=fc, ba=ba, wsl=wsl: e.matmul(
                            out=psA[ba][:, :],
                            lhsT=bass.AP(tensor=df[:].tensor, offset=128 * tt + P3S * fc,
                                         ap=[list(df[:].ap[0]), [1, 128]]),
                            rhs=fp8a[:, 32768 + 4096 * wsl + 512 * fc : 32768 + 4096 * wsl + 512 * fc + 512],
                            start=(fc == 0), stop=(fc == 7)), ep)
                    t_gmm = P.n(ep)
                    P.wait("pe", dsem(f"wu{wsl}"), t_ul)
                    P.wait("pe", ev, tok_uA[ba])
                    for fc in range(8):
                        P.op("pe", lambda e, tt=tt, fc=fc, ba=ba, wsl=wsl: e.matmul(
                            out=psA[2 + ba][:, :],
                            lhsT=bass.AP(tensor=df[:].tensor, offset=128 * tt + P3S * fc,
                                         ap=[list(df[:].ap[0]), [1, 128]]),
                            rhs=fp8a[:, 40960 + 4096 * wsl + 512 * fc : 40960 + 4096 * wsl + 512 * fc + 512],
                            start=(fc == 0), stop=(fc == 7)), ep)
                    t_umm = P.n(ep)
                    P.wait("ac", ep, t_gmm)
                    P.wait("ac", ev, tok_sil)
                    if ic == 0:
                        P.wait("ac", ev, t_sg3[tt])
                    t_si = P.op("ac", lambda e, tt=tt, sb=sb: e.activation(
                        out=wdT[:, 1024:1536], in_=psA[tt % 2][:],
                        func=AF.Silu, scale=stats[:, sb + 24 + tt : sb + 25 + tt]), ea)
                    tok_gA[tt % 2] = t_si
                    P.wait("dv", ep, t_umm)
                    P.wait("dv", ea, t_si)
                    if ic == 0 and tt in (2, 3):
                        P.wait("dv", dsem("obx"), tok_obx)
                    if ic == 0 and tt in (4, 5):
                        P.wait("dv", dsem("xtb"), tok_xtb)
                    t_g = P.op("dv", lambda e, tt=tt, ic=ic: e.tensor_tensor(
                        out=big[:, 4096 * tt + 512 * ic : 4096 * tt + 512 * ic + 512],
                        in0=wdT[:, 1024:1536], in1=psA[2 + tt % 2][:], op=ALU.mult), ev)
                    tok_sil = t_g
                    tok_uA[tt % 2] = t_g
                tok_gw[wsl] = t_gmm
                tok_uw[wsl] = t_umm
            t_ic_done = P.n(ep)
            t_ic_done_prev = t_ic_done
            t_g_all = P.n(ev)
            if dbg and F == 0:
                P.wait("pl", ev, t_g_all)
                t_dg = P.dma("pl", lambda e: e.dma_start(out=dbg_t["d_g"], in_=big[:, 0 : NTT * INNER]), dsem("dbg"))
                P.wait("dv", dsem("dbg"), t_dg)
            for tt in range(NTT if p3_lvl >= 3 else 0):
                P.wait("dv", ea, tok_scr_ac)
                t_mg = P.op("dv", lambda e, tt=tt, sb=sb: e.tensor_reduce(
                    out=stats[:, sb + 56 : sb + 57], in_=big[:, 4096 * tt : 4096 * tt + 4096],
                    axis=AX.X, op=ALU.max, apply_absolute_value=True), ev)
                P.op("dv", lambda e, sb=sb: e.tensor_scalar(out=stats[:, sb + 57 : sb + 58], in0=stats[:, sb + 56 : sb + 57],
                                                     scalar1=1e-30, scalar2=None, op0=ALU.max), ev)
                P.op("dv", lambda e, sb=sb: e.reciprocal(out=stats[:, sb + 58 : sb + 59], in_=stats[:, sb + 57 : sb + 58]), ev)
                t_s2 = P.op("dv", lambda e, sb=sb: e.tensor_scalar(out=stats[:, sb + 59 : sb + 60], in0=stats[:, sb + 58 : sb + 59],
                                                            scalar1=127.0, scalar2=None, op0=ALU.mult), ev)
                P.op("dv", lambda e, tt=tt, sb=sb: e.tensor_tensor(out=stats[:, sb + 32 + tt : sb + 33 + tt],
                                                            in0=stats[:, sb + 32 + tt : sb + 33 + tt],
                                                            in1=stats[:, sb + 56 : sb + 57], op=ALU.mult), ev)
                t_c = None
                for pair in range(2):
                    for hf2 in range(2):
                        hf = 2 * pair + hf2
                        P.wait("ac", ev, t_s2)
                        P.wait("ac", ev, tok_a2i)
                        t_c = P.op("ac", lambda e, tt=tt, hf=hf, hf2=hf2, sb=sb: e.activation(
                            out=i16a[:, 1024 * hf2 : 1024 * hf2 + 1024],
                            in_=big[:, 4096 * tt + 1024 * hf : 4096 * tt + 1024 * hf + 1024],
                            func=AF.Identity, scale=stats[:, sb + 59 : sb + 60]), ea)
                    P.wait("dv", ea, t_c)
                    P.wait("dv", ep, tok_a2tm)
                    t_cp3 = None
                    for hf2 in range(2):
                        hf = 2 * pair + hf2
                        t_cp3 = P.op("dv", lambda e, hf=hf, hf2=hf2: e.tensor_copy(
                            out=wvT[:, 1024 * hf : 1024 * hf + 1024],
                            in_=i16a[:, 1024 * hf2 : 1024 * hf2 + 1024]), ev)
                    tok_a2i = t_cp3
                tok_scr_ac = t_c
                t_a2tm = tok_a2i
                P.wait("pe", ev, t_a2tm)
                P.wait("pe", ep, tok_a2fm)
                t_fm2 = None
                for w4 in range(2):
                    P.wait("pe", ev, tok_psB[0])
                    P.wait("pe", ev, tok_psB[1])
                    t_tr = None
                    for kk in range(16):
                        icc = 16 * w4 + kk
                        t_tr = P.op("pe", lambda e, icc=icc, kk=kk: e.transpose(
                            out=psB[kk % 2][:, 128 * (kk // 2) : 128 * (kk // 2) + 128],
                            in_=wvT[:, 128 * icc : 128 * icc + 128], identity=identb[:]), ep)
                    if w4 == 1:
                        tok_a2tm = t_tr
                    P.wait("dv", ep, t_tr)
                    for b in range(2):
                        t_fm2 = P.op("dv", lambda e, w4=w4, b=b: e.tensor_copy(
                            out=bass.AP(tensor=wvT[:].tensor,
                                        offset=4096 + 2048 * w4 + 128 * b,
                                        ap=[list(wvT[:].ap[0]), [256, 8], [1, 128]]),
                            in_=psB[b][:]), ev)
                    tok_psB[0] = t_fm2
                    tok_psB[1] = t_fm2
                t_a2f = t_fm2
                if F + 1 < NSUP3 and p3_lvl >= 4 and not _NO_ILV:
                    ln3_tt(F + 1, tt, t_ic_done)
                if tt > 0:
                    ln4_tt(tt - 1, sb)
                P.wait("pe", ev, t_a2f)
                P.wait("pe", dsem("wd"), t_wd)
                P.wait("pe", ev, tok_ybuf)
                P.wait("pe", ea, tok_psC_ac)
                for ncx in range(2):
                    for icc in range(32):
                        P.op("pe", lambda e, icc=icc, ncx=ncx: e.matmul(
                            out=psC[ncx][:, :],
                            lhsT=wvT[:, 4096 + 128 * icc : 4096 + 128 * icc + 128],
                            rhs=fp8a[:, 1024 * icc + 512 * ncx : 1024 * icc + 512 * ncx + 512],
                            start=(icc == 0), stop=(icc == 31)), ep)
                t_dn = P.n(ep)
                tok_a2fm = t_dn
                t_dn3[tt] = t_dn
            if NTT and p3_lvl >= 3:
                ln4_tt(NTT - 1, sb)
            t_q_ev = P.n(ev)
            t_q_ea = P.n(ea)
            if p3_lvl < 4:
                continue
            P.op("dv", lambda e, sb=sb: e.tensor_tensor(out=stats[:, sb + 56 : sb + 56 + NTT],
                                                 in0=stats[:, sb + 32 : sb + 32 + NTT],
                                                 in1=stats[:, sb + 32 : sb + 32 + NTT], op=ALU.mult), ev)
            t_vt = P.op("dv", lambda e, sb=sb: e.tensor_tensor(out=stats[:, sb + 48 : sb + 48 + NTT],
                                                        in0=stats[:, sb + 48 : sb + 48 + NTT],
                                                        in1=stats[:, sb + 56 : sb + 56 + NTT], op=ALU.mult), ev)
            P.wait("ac", ev, t_vt)
            P.op("ac", lambda e, sb=sb: e.activation(out=stats[:, sb + 56 : sb + 56 + NTT], in_=stats[:, sb + 48 : sb + 48 + NTT],
                                              func=AF.Ln, bias=epscol[:, 0:1]), ea)
            t_r4 = P.op("ac", lambda e, sb=sb: e.activation(out=stats[:, sb + 48 : sb + 48 + NTT], in_=stats[:, sb + 56 : sb + 56 + NTT],
                                                     func=AF.Exp, scale=-0.5), ea)
            P.wait("dv", ea, t_r4)
            t_cr = P.op("dv", lambda e, sb=sb: e.tensor_tensor(out=stats[:, sb + 48 : sb + 48 + NTT],
                                                        in0=stats[:, sb + 48 : sb + 48 + NTT],
                                                        in1=stats[:, sb + 32 : sb + 32 + NTT], op=ALU.mult), ev)
            P.wait("sp", ev, t_q_ev)
            P.wait("sp", ea, t_q_ea)
            P.wait("sp", dsem("obx"), tok_obx)
            for r0, r1 in splits(4):
                t_xb = P.dma("sp", lambda e, TF=TF, r0=r0, r1=r1: e.dma_start(
                    out=big[r0:r1, 16384 : 16384 + NTT * 1024],
                    in_=bass.AP(tensor=x2s, offset=(TF + r0) * C,
                                ap=[[C, r1 - r0], [128 * C, NTT], [1, C]])), dsem("xtb"))
            tok_xtb = t_xb
            P.wait("dv", dsem("xtb"), t_xb)
            t_ou = None
            for tt in range(NTT):
                P.op("dv", lambda e, tt=tt, sb=sb: e.tensor_scalar(
                    out=f32a[:, F_sc : F_sc + 1024], in0=cpT[:, 1024 * tt : 1024 * tt + 1024],
                    scalar1=stats[:, sb + 40 + tt : sb + 41 + tt], scalar2=stats[:, sb + 48 + tt : sb + 49 + tt],
                    op0=ALU.add, op1=ALU.mult), ev)
                t_ou = P.op("dv", lambda e, tt=tt: e.tensor_tensor(
                    out=big[:, 8192 + 1024 * tt : 8192 + 1024 * tt + 1024],
                    in0=f32a[:, F_sc : F_sc + 1024],
                    in1=big[:, 16384 + 1024 * tt : 16384 + 1024 * tt + 1024], op=ALU.add), ev)
            P.wait("pl", ev, t_ou)
            for r0, r1 in splits(4):
                tok_obx = P.dma("pl", lambda e, TF=TF, r0=r0, r1=r1: e.dma_start(
                    out=bass.AP(tensor=out_t, offset=(TF + r0) * C,
                                ap=[[C, r1 - r0], [128 * C, NTT], [1, C]]),
                    in_=big[r0:r1, 8192 : 8192 + NTT * 1024]), dsem("obx"))

            tok_ybuf = 0
            tok_a1fm = tok_gw[0] if tok_gw[0] else 0
        fence()

        with nc.Block() as block:
            @block.sync
            def _(e):
                P.replay("sp", e)

            @block.tensor
            def _(e):
                P.replay("pe", e)

            @block.vector
            def _(e):
                P.replay("dv", e)

            @block.scalar
            def _(e):
                P.replay("ac", e)

            @block.gpsimd
            def _(e):
                P.replay("pl", e)

    return nc



def _host_consts(inputs):
    sfw = np.asarray(inputs["L0_attn_strain_fused_w"], np.float32)
    sfb = np.asarray(inputs["L0_attn_strain_fused_b"], np.float32)
    bondw = np.asarray(inputs["L0_attn_bond_out_w"], np.float32)
    dmgw = np.asarray(inputs["L0_attn_damage_out_w"], np.float32)
    dmgb = np.asarray(inputs["L0_attn_damage_out_b"], np.float32)

    identb = np.eye(128, dtype=ml_dtypes.bfloat16)
    maskneg = np.zeros((128, WIN), np.float32)
    r = np.arange(128)[:, None]
    wv = np.arange(WIN)[None, :]
    maskneg[(r + wv) < (WIN - 1)] = -1e9

    logitw = np.zeros((128, 8), np.float32)
    for h in range(4):
        logitw[32 * h : 32 * h + BD, h] = bondw[0]
        logitw[32 * h + BD : 32 * h + 2 * BD, 4 + h] = dmgw[0]

    dfw32 = np.zeros((128, 256), np.float32)
    for h in range(8):
        c0 = (h % 4) * 32 + (h // 4) * 128
        dfw32[16 * h : 16 * h + BD, c0 : c0 + 2 * BD] = sfw.T
    csfb = np.repeat(np.tile(sfb, 4)[:, None], WIN, axis=1).astype(np.float32)
    dbrep = np.full((128, 1), float(dmgb[0]), np.float32)
    return {
        "identb": identb, "maskneg": maskneg.astype(ml_dtypes.bfloat16), "logitw32": logitw,
        "dfw32": dfw32, "csfb": csfb, "dbrep": dbrep,
    }


def kernel(**inputs):
    T_full = int(np.asarray(inputs["L0_attn_x"]).shape[1])
    nc = build(T=T_full, dbg=False)

    x = np.ascontiguousarray(np.asarray(inputs["L0_attn_x"], np.float32))
    consts = _host_consts(inputs)
    shared = {
        "wdisp": np.ascontiguousarray(np.asarray(inputs["L0_attn_W_disp"], np.float32)),
        "wval": np.ascontiguousarray(np.asarray(inputs["L0_attn_W_val"], np.float32)),
        "relpos": np.ascontiguousarray(np.asarray(inputs["L0_attn_rel_pos_emb"], np.float32)[:WIN]),
        "pposw": np.ascontiguousarray(np.asarray(inputs["L0_attn_pos_proj_w"], np.float32)),
        "cprojw": np.ascontiguousarray(np.asarray(inputs["L0_attn_c_proj_w"], np.float32)),
        "wgate": np.ascontiguousarray(np.asarray(inputs["L0_ffn_w_gate"], np.float32)),
        "wup": np.ascontiguousarray(np.asarray(inputs["L0_ffn_w_up"], np.float32)),
        "wdown": np.ascontiguousarray(np.asarray(inputs["L0_ffn_w_down"], np.float32)),
        **consts,
    }
    ncores = x.shape[0]
    in_maps = [{"x": np.ascontiguousarray(x[c]), **shared} for c in range(ncores)]
    res = run_bass_kernel_spmd(nc, in_maps, core_ids=list(range(ncores)))
    outs = [np.asarray(r["out"], np.float32) for r in res.results]
    return np.stack(outs, axis=0)
